# version 17 (final)
# speedup vs baseline: 1.0088x; 1.0088x over previous
from contextlib import ExitStack

import numpy as np
import ml_dtypes

import concourse.bass as bass
import concourse.mybir as mybir
from concourse.bass_utils import run_bass_kernel_spmd

F32 = mybir.dt.float32
BF16 = mybir.dt.bfloat16
F16 = mybir.dt.float16
AF = mybir.ActivationFunctionType
ALU = mybir.AluOpType
AX = mybir.AxisListType

BF = ml_dtypes.bfloat16

B, S, D = 2, 8192, 1024
NCHUNK = 8
CHT = 1024
TCORE = 2 * CHT
NCORES = 8
EPS = 1e-5
GROUPS = [[0, 1, 2, 3], [4, 5, 6, 7]]

def chunks_of_core(c):
    g = c % 4
    return [g, 7 - g]

def owner_of_chunk(cg):
    return (cg, 0) if cg < 4 else (7 - cg, 1)



class Sem:
    def __init__(self, stack, nc, name):
        self.h = stack.enter_context(nc.semaphore(name))
        self.n = 0


FUSE = {"active": False}


class Prog:

    _shared = None

    def __new__(cls):
        if FUSE["active"] and cls._shared is not None:
            return cls._shared
        return super().__new__(cls)

    def __init__(self):
        if getattr(self, "_inited", False):
            return
        self._inited = True
        self.nc = bass.Bass("TRN2", target_bir_lowering=False)
        if FUSE["active"]:
            Prog._shared = self
            self.nc.cache_partition_id()
            orig = self.nc.dram_tensor

            def dram_pref(name, *a, **k):
                return orig(FUSE.get("pref", "") + name, *a, **k)
            self.nc.dram_tensor = dram_pref
        self.stack = ExitStack()
        self.stack.__enter__()
        self._scopes = [(self.stack, set())]
        nc, stack = self.nc, self.stack
        self.dw = Sem(stack, nc, "s_dw")
        self.da = Sem(stack, nc, "s_da")
        self.ds = Sem(stack, nc, "s_ds")
        self.pe = Sem(stack, nc, "s_pe")
        self.ac = Sem(stack, nc, "s_ac")
        self.dv = Sem(stack, nc, "s_dv")
        self.po = Sem(stack, nc, "s_po")
        self.cc = Sem(stack, nc, "s_cc")
        self.w2 = [Sem(stack, nc, "s_w2a"), Sem(stack, nc, "s_w2b")]
        self.st = Sem(stack, nc, "s_st")
        self._names = {}

    def scope(self):
        from contextlib import contextmanager

        @contextmanager
        def _s():
            st = ExitStack()
            st.__enter__()
            self._scopes.append((st, set()))
            try:
                yield
            finally:
                _, keys = self._scopes.pop()
                for k in keys:
                    del self._names[k]
                st.__exit__(None, None, None)
        return _s()

    def _alloc(self, key, mk):
        if key in self._names:
            return self._names[key]
        st, keys = self._scopes[-1]
        t = st.enter_context(mk())
        self._names[key] = t
        keys.add(key)
        return t

    _uid = 0

    def _mkname(self, name):
        Prog._uid += 1
        return f"{name}__u{Prog._uid}"

    def sbuf(self, name, shape, dtype):
        key = ("sb", name)
        if key in self._names:
            return self._names[key]
        st, keys = self._scopes[-1]
        t = st.enter_context(self.nc.sbuf_tensor(self._mkname(name), shape, dtype))
        self._names[key] = t
        keys.add(key)
        return t

    def psum(self, name, shape, dtype=F32):
        key = ("ps", name)
        if key in self._names:
            return self._names[key]
        st, keys = self._scopes[-1]
        t = st.enter_context(self.nc.psum_tensor(self._mkname(name), shape, dtype))
        self._names[key] = t
        keys.add(key)
        return t

    def phase(self):
        return self.nc.Block()

    def finish(self):
        if FUSE["active"]:
            return self.nc
        self.stack.__exit__(None, None, None)
        return self.nc


def dma_wait_all(e, sem):
    e.wait_ge(sem.h, sem.n)


def dma(e, sem, dst, src):
    e.dma_start(out=dst, in_=src).then_inc(sem.h, 16)
    sem.n += 16



CONST_VALS = [EPS, 1.0 / D, 1e-12, -1e-12, 1e9, -30000.0]

def load_consts(p, blk, io, want_f16=False):
    if FUSE["active"] and "c" in FUSE:
        return FUSE["c"]
    c = {}
    c["ident"] = p.sbuf("ident", [128, 128], F32)
    c["ones1"] = p.sbuf("ones1", [1, 128], F32)
    c["ones1b"] = p.sbuf("ones1b", [1, 128], BF16)
    c["constv"] = p.sbuf("constv_sb", [128, len(CONST_VALS)], F32)
    if want_f16:
        c["ones1h"] = p.sbuf("ones1h", [1, 128], F16)

    def _(g):
        dma(g, p.da, c["ident"][:, :], io["ident_f32"][:, :])
        dma(g, p.da, c["ones1"][:, :], io["ones1_f32"][:, :])
        dma(g, p.da, c["constv"][:, :], io["constv"][:, :])
        dma_wait_all(g, p.da)
    blk.gpsimd(_)
    for i, v in enumerate(CONST_VALS):
        p.nc.const_aps.register(F32, float(v), c["constv"][:, i:i + 1])

    c["onescf"] = p.sbuf("onescf", [128, 1], F32)
    c["onescb"] = p.sbuf("onescb", [128, 1], BF16)
    if want_f16:
        c["onesch"] = p.sbuf("onesch", [128, 1], F16)

    def v(v_):
        v_.wait_ge(p.da.h, p.da.n)
        v_.tensor_copy(c["ones1b"][:, :], c["ones1"][:, :])
        v_.memset(c["onescf"][:, :], 1.0)
        v_.memset(c["onescb"][:, :], 1.0)
        if want_f16:
            v_.tensor_copy(c["ones1h"][:, :], c["ones1"][:, :])
            v_.memset(c["onesch"][:, :], 1.0)
    blk.vector(v)
    if FUSE["active"]:
        FUSE["c"] = c
    return c


def load_weight(p, blk, wdram, K, N, name, dtype=BF16, sem=None):
    KC = (K + 127) // 128
    w = p.sbuf(name, [128, KC * N], dtype)
    sem = sem or p.da

    def _(g):
        if K % 128 == 0:
            for kc in range(K // 128):
                dma(g, sem, w[:, kc * N:(kc + 1) * N],
                    wdram.ap()[kc * 128:(kc + 1) * 128, :])
        else:
            full = (K // 128) * 128
            for kc in range(K // 128):
                dma(g, sem, w[:, kc * N:(kc + 1) * N],
                    wdram.ap()[kc * 128:(kc + 1) * 128, :])
            rem = K - full
            dma(g, sem, w[0:rem, (K // 128) * N:(K // 128) * N + N],
                wdram.ap()[full:K, :])
        dma_wait_all(g, sem)
    blk.gpsimd(_)
    return w



def emit_proj(p, blk, w_sb, NOUT, xl, TD, epilogue, nk=8, ttile=512,
              x_col0=0, psums=None, extra_mm=None, fclist=None, kclist=None,
              tokens=None, x_stride=None):
    nc = p.nc
    FC = NOUT // 128
    fclist = fclist if fclist is not None else list(range(FC))
    kclist = kclist if kclist is not None else list(range(nk))
    ps = psums or [p.psum(f"prj{i}_{ttile}", [128, ttile], F32) for i in range(2)]
    TD = x_stride if x_stride is not None else TD
    nt = (tokens if tokens is not None else TD) // ttile
    ntile = len(fclist) * nt
    pe0, ac0 = p.pe.n, p.ac.n

    def tloop(t_):
        i = 0
        for fc in fclist:
            for ti in range(nt):
                if i >= 2:
                    t_.wait_ge(p.ac.h, ac0 + i - 1)
                t0 = ti * ttile
                pp = ps[i % 2]
                for j, kc in enumerate(kclist):
                    last_mm = nc.tensor.matmul(
                        pp[:, :],
                        w_sb[:, kc * NOUT + fc * 128: kc * NOUT + fc * 128 + 128],
                        xl[:, kc * TD + x_col0 + t0: kc * TD + x_col0 + t0 + ttile],
                        start=(j == 0),
                        stop=(j == len(kclist) - 1 and extra_mm is None))
                if extra_mm is not None:
                    extra_mm(pp, fc, t0)
                last_mm.then_inc(p.pe.h, 1)
                i += 1
    blk.tensor(tloop)
    p.pe.n = pe0 + ntile

    def aloop(a_):
        i = 0
        for fc in fclist:
            for ti in range(nt):
                a_.wait_ge(p.pe.h, pe0 + i + 1)
                epilogue(a_, ps[i % 2], fc, ti * ttile).then_inc(p.ac.h, 1)
                i += 1
    blk.scalar(aloop)
    p.ac.n = ac0 + ntile


def act_epi(out_sb, TD, func=AF.Copy, bias=None, scale=1.0, out_col0=0):
    def epi(a_, psum, fc, t0):
        tt = psum.shape[-1]
        b = 0.0 if bias is None else bias[:, fc:fc + 1]
        s = scale if not isinstance(scale, tuple) else scale[0][:, fc:fc + 1]
        f = AF.Identity if (func == AF.Copy and bias is not None) else func
        return a_.activation(
            out_sb[:, fc * TD + out_col0 + t0: fc * TD + out_col0 + t0 + tt],
            psum[:, :], f, bias=b, scale=s)
    return epi



def emit_sumsq_row(p, blk, xb, TD, ident, row, row_off, nk=8, rstd_cols=None):
    nc = p.nc
    NT = TD // 128
    ps = [p.psum(f"ssq{i}", [128, 128], F32) for i in range(2)]
    cols = p.sbuf("sscols", [128, 32], F32)
    colsT = p.sbuf("sscolsT", [128, 32], F32)
    scr = p.sbuf("ssscr", [128, 128], F32)
    pe0, dv0 = p.pe.n, p.dv.n
    assert NT <= 32

    def tens(t_):
        for ti in range(NT):
            if ti >= 2:
                t_.wait_ge(p.dv.h, dv0 + ti - 1)
            pp = ps[ti % 2]
            for j in range(nk):
                off = j * TD + ti * 128
                last_mm = nc.tensor.matmul(pp[:, :], xb[:, off:off + 128],
                                 xb[:, off:off + 128],
                                 start=(j == 0), stop=(j == nk - 1))
            last_mm.then_inc(p.pe.h, 1)
    blk.tensor(tens)
    p.pe.n = pe0 + NT

    def vec(v_):
        if NT < 32:
            v_.memset(cols[:, NT:32], 0.0)
        for ti in range(NT):
            v_.wait_ge(p.pe.h, pe0 + ti + 1)
            if ti >= 1:
                v_.wait_ge(p.dv.h, dv0 + ti)
            v_.scalar_tensor_tensor(
                scr[:, :], ps[ti % 2][:, :], 1.0, ident[:, :],
                op0=ALU.mult, op1=ALU.mult,
                accum_out=cols[:, ti:ti + 1]).then_inc(p.dv.h, 1)
        v_.wait_ge(p.dv.h, dv0 + NT)
        v_.transpose(colsT[:, :], cols[:, :]).then_inc(p.dv.h, 1)
        p.dv.n = dv0 + NT + 1
    blk.vector(vec)

    def sc(s_):
        s_.wait_ge(p.dv.h, p.dv.n)
        if rstd_cols is not None:
            s_.activation(rstd_cols[:, 0:NT], cols[:, 0:NT], AF.Ln,
                          bias=EPS, scale=1.0 / D).then_inc(p.ac.h, 1)
            p.ac.n += 1
            s_.wait_ge(p.ac.h, p.ac.n)
            s_.activation(rstd_cols[:, 0:NT], rstd_cols[:, 0:NT], AF.Exp,
                          scale=-0.5).then_inc(p.ac.h, 1)
            p.ac.n += 1
        rowlen = row.shape[-1]
        for a in range(4):
            src = colsT[32 * a:32 * a + NT, 0:32]
            dst = bass.AP(row, row_off + 32 * a, [[rowlen, 1], [128, NT], [1, 32]])
            s_.dma_start(out=dst, in_=src).then_inc(p.ds.h, 16)
            p.ds.n += 16
        s_.wait_ge(p.ds.h, p.ds.n)
        s_.nop().then_inc(p.ac.h, 1)
        p.ac.n += 1
    blk.scalar(sc)


def emit_row_rstd(p, blk, row, off_in, off_out, TD, scale_to_var=1.0 / D,
                  eps=EPS):
    def sc(s_):
        s_.activation(row[0:1, off_out:off_out + TD], row[0:1, off_in:off_in + TD],
                      AF.Ln, bias=eps, scale=scale_to_var).then_inc(p.ac.h, 1)
        p.ac.n += 1
        s_.wait_ge(p.ac.h, p.ac.n)
        s_.activation(row[0:1, off_out:off_out + TD], row[0:1, off_out:off_out + TD],
                      AF.Exp, scale=-0.5).then_inc(p.ac.h, 1)
        p.ac.n += 1
    blk.scalar(sc)


def emit_bcast(p, blk, row, n0, T, ones1, pbc, on_engine="tensor"):
    nc = p.nc

    def t_(e):
        for q0 in range(0, T, 512):
            w = min(512, T - q0)
            last_mm = nc.tensor.matmul(pbc[:, q0:q0 + w], ones1[0:1, :],
                             row[0:1, n0 + q0:n0 + q0 + w],
                             start=True, stop=True)
        last_mm.then_inc(p.pe.h, 1)
        p.pe.n += 1
    blk.tensor(t_)


def emit_preln(p, io, xs, xl, TD, consts, row, row_off, g_col=None,
               xb_dtype=BF16, xb=None):
    nc = p.nc
    if xb is None:
        xb = p.sbuf("xbcast", [128, 8 * TD], xb_dtype)
    with p.phase() as blk:
        def cast(v):
            for kc in range(8):
                v.tensor_copy(xb[:, kc * TD:(kc + 1) * TD],
                              xs[:, kc * TD:(kc + 1) * TD])
        blk.vector(cast)
    with p.phase() as blk:
        emit_sumsq_row(p, blk, xb, TD, consts["ident"], row, row_off)
    with p.phase() as blk:
        emit_row_rstd(p, blk, row, row_off, row_off, TD)
    with p.phase() as blk:
        pb = p.psum("lnbc", [128, 512], F32)
        pb2 = p.psum("lnbc2", [128, 512], F32)
        pbs = [pb, pb2]
        pe0, dv0 = p.pe.n, p.dv.n
        nseg = TD // 512

        def t_(e):
            for si in range(nseg):
                if si >= 2:
                    e.wait_ge(p.dv.h, dv0 + si - 1)
                last_mm = nc.tensor.matmul(pbs[si % 2][:, :], consts["ones1"][0:1, :],
                                 row[0:1, row_off + si * 512:row_off + si * 512 + 512],
                                 start=True, stop=True)
                last_mm.then_inc(p.pe.h, 1)
        blk.tensor(t_)
        p.pe.n = pe0 + nseg

        def v_(v):
            for si in range(nseg):
                v.wait_ge(p.pe.h, pe0 + si + 1)
                last = None
                for kc in range(8):
                    a = xb[:, kc * TD + si * 512: kc * TD + si * 512 + 512]
                    bb = pbs[si % 2][:, :]
                    o = xl[:, kc * TD + si * 512: kc * TD + si * 512 + 512]
                    if g_col is None:
                        last = v.tensor_mul(o, a, bb)
                    else:
                        last = v.scalar_tensor_tensor(o, a, g_col, bb,
                                                      op0=ALU.mult, op1=ALU.mult)
                last.then_inc(p.dv.h, 1)
        blk.vector(v_)
        p.dv.n = dv0 + nseg
    return xb


def emit_postln_residual(p, io, y_sb, xs, TD, consts, row, roff, ones_row_dt):
    nc = p.nc
    o_ssq, o_mu, o_rstd, o_tmp = roff, roff + TD, roff + 2 * TD, roff + 3 * TD
    with p.phase() as blk:
        emit_sumsq_row(p, blk, y_sb, TD, consts["ident"], row, o_ssq)
    with p.phase() as blk:
        pb = p.psum("mub", [128, 512], F32)
        pe0 = p.pe.n

        def t_(e):
            for si in range(TD // 512):
                if si >= 1:
                    e.wait_ge(p.dv.h, p.dv.n + si)
                for kc in range(8):
                    last_mm = nc.tensor.matmul(pb[:, :], ones_row_dt[0:1, :],
                                     y_sb[:, kc * TD + si * 512:kc * TD + si * 512 + 512],
                                     start=(kc == 0), stop=(kc == 7))
                last_mm.then_inc(p.pe.h, 1)
        blk.tensor(t_)
        p.pe.n = pe0 + TD // 512
        dv0 = p.dv.n

        def v_(v):
            for si in range(TD // 512):
                v.wait_ge(p.pe.h, pe0 + si + 1)
                v.tensor_scalar_mul(row[0:1, o_mu + si * 512:o_mu + si * 512 + 512],
                                    pb[0:1, :], 1.0 / D).then_inc(p.dv.h, 1)
        blk.vector(v_)
        p.dv.n = dv0 + TD // 512
    with p.phase() as blk:
        def s_(a):
            a.activation(row[0:1, o_tmp:o_tmp + TD], row[0:1, o_mu:o_mu + TD],
                         AF.Square).then_inc(p.ac.h, 1)
            p.ac.n += 1
        blk.scalar(s_)

        def v_(v):
            v.wait_ge(p.ac.h, p.ac.n)
            v.tensor_scalar(row[0:1, o_ssq:o_ssq + TD], row[0:1, o_ssq:o_ssq + TD],
                            1.0 / D, None, op0=ALU.mult)
            v.tensor_sub(row[0:1, o_ssq:o_ssq + TD], row[0:1, o_ssq:o_ssq + TD],
                         row[0:1, o_tmp:o_tmp + TD]).then_inc(p.dv.h, 1)
            p.dv.n += 1
        blk.vector(v_)

        def s2_(a):
            a.wait_ge(p.dv.h, p.dv.n)
            a.activation(row[0:1, o_rstd:o_rstd + TD], row[0:1, o_ssq:o_ssq + TD],
                         AF.Ln, bias=EPS)
            a.activation(row[0:1, o_rstd:o_rstd + TD], row[0:1, o_rstd:o_rstd + TD],
                         AF.Exp, scale=-0.5).then_inc(p.ac.h, 1)
            p.ac.n += 1
        blk.scalar(s2_)
    with p.phase() as blk:
        pmu = p.psum("pmu", [128, 512], F32)
        prs = p.psum("prs", [128, 512], F32)
        scr = p.sbuf("plr", [128, 512], F32)
        pe0 = p.pe.n
        dv0 = p.dv.n
        nseg = TD // 512

        def t_(e):
            for si in range(nseg):
                if si >= 1:
                    e.wait_ge(p.dv.h, dv0 + si)
                last_mm = nc.tensor.matmul(pmu[:, :], consts["ones1"][0:1, :],
                                 row[0:1, o_mu + si * 512:o_mu + si * 512 + 512],
                                 start=True, stop=True)
                last_mm = nc.tensor.matmul(prs[:, :], consts["ones1"][0:1, :],
                                 row[0:1, o_rstd + si * 512:o_rstd + si * 512 + 512],
                                 start=True, stop=True)
                last_mm.then_inc(p.pe.h, 1)
        blk.tensor(t_)
        p.pe.n = pe0 + nseg

        def v_(v):
            for si in range(nseg):
                v.wait_ge(p.pe.h, pe0 + si + 1)
                last = None
                for kc in range(8):
                    y = y_sb[:, kc * TD + si * 512: kc * TD + si * 512 + 512]
                    x = xs[:, kc * TD + si * 512: kc * TD + si * 512 + 512]
                    v.tensor_sub(scr[:, :], y, pmu[:, :])
                    v.tensor_mul(scr[:, :], scr[:, :], prs[:, :])
                    last = v.tensor_add(x, x, scr[:, :])
                last.then_inc(p.dv.h, 1)
        blk.vector(v_)
        p.dv.n = dv0 + nseg



def stream_io_tensors(p, xin=True, xout=True, mu_in=False, mu_out=False):
    if FUSE["active"]:
        return FUSE["io"]
    nc = p.nc
    io = {}
    if xin:
        io["xin"] = nc.dram_tensor("xin", [D, TCORE], F32, kind="ExternalInput")
    if xout:
        io["xout"] = nc.dram_tensor("xout", [D, TCORE], F32, kind="ExternalOutput")
    if mu_in:
        io["mu0in"] = nc.dram_tensor("mu0in", [1, TCORE], F32, kind="ExternalInput")
    if mu_out:
        io["mu0"] = nc.dram_tensor("mu0", [1, TCORE], F32, kind="ExternalOutput")
    io["ident_f32"] = nc.dram_tensor("ident_f32", [128, 128], F32, kind="ExternalInput")
    io["ones1_f32"] = nc.dram_tensor("ones1_f32", [1, 128], F32, kind="ExternalInput")
    io["constv"] = nc.dram_tensor("constv", [128, len(CONST_VALS)], F32,
                                  kind="ExternalInput")
    return io


def emit_load_xl(p, blk, io, xl, cast=True):
    gk = list(range(8))
    def g_(g):
        for kc in gk:
            dma(g, p.da, xl[:, kc * TCORE:(kc + 1) * TCORE],
                io["xin"].ap()[kc * 128:(kc + 1) * 128, :])
        dma_wait_all(g, p.da)
    blk.gpsimd(g_)


def emit_load_stream(p, blk, io, xs, murow=None):
    def _(g):
        for kc in range(8):
            dma(g, p.da, xs[:, kc * TCORE:(kc + 1) * TCORE],
                io["xin"].ap()[kc * 128:(kc + 1) * 128, :])
        if murow is not None:
            dma(g, p.da, murow[0:1, 0:TCORE], io["mu0in"][0:1, :])
        dma_wait_all(g, p.da)
    blk.gpsimd(_)


def emit_store_stream(p, blk, io, xs, murow=None):
    def _(g):
        for kc in range(8):
            dma(g, p.da, io["xout"].ap()[kc * 128:(kc + 1) * 128, :],
                xs[:, kc * TCORE:(kc + 1) * TCORE])
        if murow is not None:
            dma(g, p.da, io["mu0"][0:1, :], murow[0:1, 0:TCORE])
        dma_wait_all(g, p.da)
    blk.gpsimd(_)



def build_prep():
    p = Prog()
    nc = p.nc
    xinh = nc.dram_tensor("xinh", [D, TCORE], F16, kind="ExternalInput")
    io = stream_io_tensors(p, xin=False, mu_out=True)
    xh = p.sbuf("xh", [128, 8 * TCORE], F16)
    xs = p.sbuf("xs", [128, 8 * TCORE], F32)
    murow = p.sbuf("murow", [1, TCORE], F32)

    with p.phase() as blk:
        c = load_consts(p, blk, io, want_f16=True)

        def _(g):
            for kc in range(8):
                dma(g, p.da, xh[:, kc * TCORE:(kc + 1) * TCORE],
                    xinh.ap()[kc * 128:(kc + 1) * 128, :])
            dma_wait_all(g, p.da)
        blk.gpsimd(_)
    with p.phase() as blk:
        pb = p.psum("pb", [1, 512], F32)
        pe0 = p.pe.n

        def t_(e):
            for si in range(TCORE // 512):
                if si >= 1:
                    e.wait_ge(p.dv.h, p.dv.n + si)
                for kc in range(8):
                    last_mm = nc.tensor.matmul(pb[:, :], c["onesch"][:, 0:1],
                                     xh[:, kc * TCORE + si * 512:kc * TCORE + si * 512 + 512],
                                     start=(kc == 0), stop=(kc == 7))
                last_mm.then_inc(p.pe.h, 1)
        blk.tensor(t_)
        p.pe.n = pe0 + TCORE // 512
        dv0 = p.dv.n

        def v_(v):
            for si in range(TCORE // 512):
                v.wait_ge(p.pe.h, pe0 + si + 1)
                v.tensor_scalar_mul(murow[0:1, si * 512:si * 512 + 512],
                                    pb[0:1, :], 1.0 / D).then_inc(p.dv.h, 1)
        blk.vector(v_)
        p.dv.n = dv0 + TCORE // 512
    with p.phase() as blk:
        pmu = p.psum("pmu", [128, 512], F32)
        pe0 = p.pe.n
        dv0 = p.dv.n

        def t_(e):
            for si in range(TCORE // 512):
                if si >= 1:
                    e.wait_ge(p.dv.h, dv0 + si)
                last_mm = nc.tensor.matmul(pmu[:, :], c["ones1"][0:1, :],
                                 murow[0:1, si * 512:si * 512 + 512],
                                 start=True, stop=True)
                last_mm.then_inc(p.pe.h, 1)
        blk.tensor(t_)
        p.pe.n = pe0 + TCORE // 512

        def v_(v):
            for si in range(TCORE // 512):
                v.wait_ge(p.pe.h, pe0 + si + 1)
                last = None
                for kc in range(8):
                    last = v.tensor_sub(
                        xs[:, kc * TCORE + si * 512:kc * TCORE + si * 512 + 512],
                        xh[:, kc * TCORE + si * 512:kc * TCORE + si * 512 + 512],
                        pmu[:, :])
                last.then_inc(p.dv.h, 1)
        blk.vector(v_)
        p.dv.n = dv0 + TCORE // 512
    with p.phase() as blk:
        emit_store_stream(p, blk, io, xs, murow)
    return p.finish()


def ins_prep(c, H, inputs, state):
    return {"xinh": H["x_fm"][c], "ident_f32": H["ident_f32"],
            "ones1_f32": H["ones1_f32"], "constv": H["constv"]}



def build_final():
    p = Prog()
    nc = p.nc
    io = stream_io_tensors(p, mu_in=True)
    xs = p.sbuf("xs", [128, 8 * TCORE], F32)
    murow = p.sbuf("murow", [1, TCORE], F32)
    with p.phase() as blk:
        c = load_consts(p, blk, io)
        emit_load_stream(p, blk, io, xs, murow)
    with p.phase() as blk:
        pmu = p.psum("pmu", [128, 512], F32)
        pe0 = p.pe.n
        dv0 = p.dv.n

        def t_(e):
            for si in range(TCORE // 512):
                if si >= 1:
                    e.wait_ge(p.dv.h, dv0 + si)
                last_mm = nc.tensor.matmul(pmu[:, :], c["ones1"][0:1, :],
                                 murow[0:1, si * 512:si * 512 + 512],
                                 start=True, stop=True)
                last_mm.then_inc(p.pe.h, 1)
        blk.tensor(t_)
        p.pe.n = pe0 + TCORE // 512

        def v_(v):
            for si in range(TCORE // 512):
                v.wait_ge(p.pe.h, pe0 + si + 1)
                last = None
                for kc in range(8):
                    sl = slice(kc * TCORE + si * 512, kc * TCORE + si * 512 + 512)
                    last = v.tensor_add(xs[:, sl], xs[:, sl], pmu[:, :])
                last.then_inc(p.dv.h, 1)
        blk.vector(v_)
        p.dv.n = dv0 + TCORE // 512
    with p.phase() as blk:
        emit_store_stream(p, blk, io, xs)
    return p.finish()


def ins_final(c, H, inputs, state):
    return {"xin": state["stream"][c], "mu0in": state["mu0"][c],
            "ident_f32": H["ident_f32"], "ones1_f32": H["ones1_f32"],
            "constv": H["constv"]}





def emit_proj_tm(p, blk, w_sb, NOUT, xl, TD, out_sb, bias_row=None, nk=8,
                 ones1=None, out_dtype_width=None, bias_row_idx=0,
                 out_col0=0):
    nc = p.nc
    NT = TD // 128
    nf = NOUT // 512
    ps = [p.psum(f"ptm{i}", [128, 512], F32) for i in range(2)]
    pe0, ac0 = p.pe.n, p.ac.n

    def tloop(t_):
        i = 0
        for tt in range(NT):
            for fi in range(nf):
                if i >= 2:
                    t_.wait_ge(p.ac.h, ac0 + i - 1)
                pp = ps[i % 2]
                for kc in range(nk):
                    last_mm = nc.tensor.matmul(
                        pp[:, :],
                        xl[:, kc * TD + tt * 128: kc * TD + tt * 128 + 128],
                        w_sb[:, kc * NOUT + fi * 512: kc * NOUT + fi * 512 + 512],
                        start=(kc == 0),
                        stop=(kc == nk - 1 and bias_row is None))
                if bias_row is not None:
                    b0 = bias_row_idx * NOUT
                    last_mm = nc.tensor.matmul(
                        pp[:, :], ones1[0:1, :],
                        bias_row[0:1, b0 + fi * 512:b0 + fi * 512 + 512],
                        start=False, stop=True)
                last_mm.then_inc(p.pe.h, 1)
                i += 1
    blk.tensor(tloop)
    p.pe.n = pe0 + NT * nf

    def aloop(a_):
        i = 0
        for tt in range(NT):
            for fi in range(nf):
                a_.wait_ge(p.pe.h, pe0 + i + 1)
                o0 = out_col0 + tt * NOUT + fi * 512
                a_.activation(out_sb[:, o0:o0 + 512],
                              ps[i % 2][:, :], AF.Copy).then_inc(p.ac.h, 1)
                i += 1
    blk.scalar(aloop)
    p.ac.n = ac0 + NT * nf



def emit_flash(p, cfg):
    nc = p.nc
    TD = cfg["TD"]
    dt = cfg["dt"]
    kvg = cfg["kvg"]
    xl, WqT, bq = cfg["xl"], cfg["WqT_sb"], cfg["bq_col"]
    ctxn = cfg["ctxn"]
    scale = cfg["scale"]
    ones1 = cfg["ones1"]
    RANKBLK = 4096 * 1024
    VOFF = 2048 * 1024

    kqs = [p.sbuf(f"kqs{i}", [128, 2048], dt) for i in range(2)]
    vqs = [p.sbuf(f"vqs{i}", [128, 16 * 130], dt) for i in range(2)]
    q_sb = p.sbuf("q_sb", [128, 512], dt)
    p_sb = [p.sbuf(f"p_sb{i}", [128, 1024], dt) for i in range(2)]
    rrow = p.sbuf("rrow", [1, 1024], F32)
    sbc = p.sbuf("sbc", [128, 512], F32)
    ps_q = p.psum("fl_q", [128, 512], F32)
    ps_s = [p.psum(f"fl_s{i}", [128, 1024], F32) for i in range(2)]
    ps_av = [p.psum(f"fl_av{i}", [65, 512], F32) for i in range(2)]
    ps_bc = p.psum("fl_bc", [128, 512], F32)

    iters = [(hp, qt) for hp in range(8) for qt in range(4)]
    scheds = [cfg["quarters_for"](hp, qt) for hp, qt in iters]

    with p.phase() as blk:
        def vinit(v):
            for i in range(2):
                v.memset(bass.AP(vqs[i], 64,
                                 [[16 * 130, 128], [130, 16], [65, 2], [1, 1]]), 1.0)
        blk.vector(vinit)

    with p.phase() as blk:
        pe0, ac0, dv0 = p.pe.n, p.ac.n, p.dv.n
        pe_at, ac_at, dv_at = [], [], []
        pe_c, ac_c, dv_c = pe0, ac0, dv0
        for sched in scheds:
            nkt = sum(len(ch) * 8 for _, ch in sched)
            pe_at.append(pe_c)
            ac_at.append(ac_c)
            dv_at.append(dv_c)
            pe_c += 1 + 2 * nkt + 1
            ac_c += 1 + nkt
            dv_c += 7
        pe_at.append(pe_c); ac_at.append(ac_c); dv_at.append(dv_c)
        stages = []
        for it, sched in enumerate(scheds):
            hp, qt = iters[it]
            for quarter, chs in sched:
                stages.append((it, hp, quarter, chs))
        w2c = [p.w2[0].n, p.w2[1].n]
        stage_w2 = []
        for si, (it, hp, quarter, chs) in enumerate(stages):
            par = si % 2
            w2c[par] += 16 * 3 * len(chs)
            stage_w2.append((par, w2c[par]))
        iter_tiles = []
        sgi = 0
        for it, sched in enumerate(scheds):
            tiles = []
            for j, (quarter, chs) in enumerate(sched):
                for kti in range(8 * len(chs)):
                    tiles.append((sgi + j, kti))
            sgi += len(sched)
            iter_tiles.append(tiles)
        L = len(scheds)
        qp_abs = [0] * L
        bonus_abs = [0] * L
        s_abs, av_abs = [], []
        stage_pe_done = [0] * len(stages)
        pos = pe0
        for it in range(L):
            n = len(iter_tiles[it])
            pos += 1; qp_abs[it] = pos
            sa = [0] * n
            aa = [0] * n
            for k in (0, 1):
                if k < n:
                    pos += 1; sa[k] = pos
            if it > 0:
                pos += 1; bonus_abs[it - 1] = pos
            for k in range(n):
                pos += 1; aa[k] = pos
                if k + 2 < n:
                    pos += 1; sa[k + 2] = pos
            s_abs.append(sa)
            av_abs.append(aa)
            for (sg, kti), a_pe in zip(iter_tiles[it], aa):
                stage_pe_done[sg] = max(stage_pe_done[sg], a_pe)
        pos += 1; bonus_abs[L - 1] = pos
        assert pos == pe_at[-1], (pos, pe_at[-1])

        def gq(g_):
            for si, (it, hp, quarter, chs) in enumerate(stages):
                par = si % 2
                if si >= 2:
                    g_.wait_ge(p.pe.h, stage_pe_done[si - 2])
                for ci, cq in enumerate(chs):
                    r, s = owner_of_chunk(cq)
                    koff = r * RANKBLK + (hp * 128) * 2048 + s * 1024
                    g_.dma_start(
                        out=kqs[par][:, ci * 1024:(ci + 1) * 1024],
                        in_=bass.AP(kvg, koff, [[2048, 128], [1, 1024]])
                    ).then_inc(p.w2[par].h, 16)
                    voff = r * RANKBLK + VOFF + (s * 1024) * 1024 + hp * 128
                    for h in range(2):
                        g_.dma_start(
                            out=bass.AP(vqs[par], ci * 8 * 130 + h * 65,
                                        [[16 * 130, 128], [130, 8], [1, 64]]),
                            in_=bass.AP(kvg, voff + h * 64,
                                        [[1024, 128], [131072, 8], [1, 64]])
                        ).then_inc(p.w2[par].h, 16)
        blk.gpsimd(gq)
        p.w2[0].n, p.w2[1].n = w2c

        def emit_bonus(e, it):
            e.wait_ge(p.dv.h, dv_at[it] + 3)
            for h in range(2):
                mm = nc.tensor.matmul(
                    ps_bc[h * 64:(h + 1) * 64, :],
                    ones1[0:1, 0:64],
                    rrow[0:1, h * 512:(h + 1) * 512],
                    start=True, stop=True)
            mm.then_inc(p.pe.h, 1)

        def t_(e):
            for it, sched in enumerate(scheds):
                hp, qt = iters[it]
                acb = ac_at[it]
                tiles = iter_tiles[it]
                nkt_tot = len(tiles)
                if it >= 1:
                    e.wait_ge(p.ac.h, ac_at[it - 1] + 1)
                for kc in range(8):
                    mm = nc.tensor.matmul(
                        ps_q[:, :],
                        WqT[:, kc * 1024 + hp * 128: kc * 1024 + hp * 128 + 128],
                        xl[:, kc * TD + qt * 512: kc * TD + qt * 512 + 512],
                        start=(kc == 0), stop=(kc == 7))
                mm.then_inc(p.pe.h, 1)
                seen_stage = set()

                def emit_S(k):
                    sg, kt = tiles[k]
                    par_st, w2t = stage_w2[sg]
                    if sg not in seen_stage:
                        seen_stage.add(sg)
                        e.wait_ge(p.w2[par_st].h, w2t)
                    if k == 0:
                        e.wait_ge(p.ac.h, acb + 1)
                    par = k % 2
                    for h in range(2):
                        mm = nc.tensor.matmul(
                            ps_s[par][:, h * 512:(h + 1) * 512],
                            kqs[par_st][h * 64:(h + 1) * 64,
                                        kt * 128:(kt + 1) * 128],
                            q_sb[h * 64:(h + 1) * 64, :],
                            start=True, stop=True)
                    mm.then_inc(p.pe.h, 1)

                def emit_AV(k):
                    sg, kt = tiles[k]
                    par_st, _ = stage_w2[sg]
                    par = k % 2
                    e.wait_ge(p.ac.h, acb + 2 + k)
                    if k == 0 and it >= 1:
                        e.wait_ge(p.dv.h, dv_at[it])
                    for h in range(2):
                        mm = nc.tensor.matmul(
                            ps_av[h][:, :],
                            vqs[par_st][:, kt * 130 + h * 65: kt * 130 + h * 65 + 65],
                            p_sb[par][:, h * 512:(h + 1) * 512],
                            start=(k == 0), stop=(k == nkt_tot - 1))
                    mm.then_inc(p.pe.h, 1)

                for k in (0, 1):
                    if k < nkt_tot:
                        emit_S(k)
                if it >= 1:
                    emit_bonus(e, it - 1)
                for k in range(nkt_tot):
                    emit_AV(k)
                    if k + 2 < nkt_tot:
                        emit_S(k + 2)
            emit_bonus(e, len(scheds) - 1)
        blk.tensor(t_)
        p.pe.n = pe_at[-1]

        def a_(a):
            for it, sched in enumerate(scheds):
                hp, qt = iters[it]
                acb = ac_at[it]
                a.wait_ge(p.pe.h, qp_abs[it])
                qb = bq[:, hp:hp + 1] if bq is not None else 0.0
                a.activation(q_sb[:, :], ps_q[:, :],
                             AF.Identity if bq is not None else AF.Copy,
                             bias=qb).then_inc(p.ac.h, 1)
                nkt_tot = len(iter_tiles[it])
                for kglob in range(nkt_tot):
                    par = kglob % 2
                    a.wait_ge(p.pe.h, s_abs[it][kglob])
                    a.activation(p_sb[par][:, :], ps_s[par][:, :], AF.Exp,
                                 scale=scale).then_inc(p.ac.h, 1)
        blk.scalar(a_)
        p.ac.n = ac_at[-1]

        def v_(v):
            for it, sched in enumerate(scheds):
                hp, qt = iters[it]
                dvb = dv_at[it]
                v.wait_ge(p.pe.h, av_abs[it][-1])
                v.tensor_copy(rrow[0:1, 0:512], ps_av[0][64:65, :]).then_inc(p.dv.h, 1)
                v.tensor_copy(rrow[0:1, 512:1024], ps_av[1][64:65, :]).then_inc(p.dv.h, 1)
                v.wait_ge(p.dv.h, dvb + 2)
                v.reciprocal(rrow[0:1, :], rrow[0:1, :]).then_inc(p.dv.h, 1)
                v.wait_ge(p.pe.h, bonus_abs[it])
                v.tensor_copy(sbc[0:64, :], ps_bc[0:64, :]).then_inc(p.dv.h, 1)
                v.tensor_copy(sbc[64:128, :], ps_bc[64:128, :]).then_inc(p.dv.h, 1)
                v.wait_ge(p.dv.h, dvb + 5)
                col = hp * TD + qt * 512
                v.tensor_mul(ctxn[0:64, col:col + 512], ps_av[0][0:64, :],
                             sbc[0:64, :]).then_inc(p.dv.h, 1)
                v.tensor_mul(ctxn[64:128, col:col + 512], ps_av[1][0:64, :],
                             sbc[64:128, :]).then_inc(p.dv.h, 1)
        blk.vector(v_)
        p.dv.n = dv_at[-1]



def emit_proj_stream(p, blk, wtiled, KTOT, NOUT, xl, TD, epilogue, G=2,
                     ttile=512, x_col0=0, wdtype=BF16, x_stride=None,
                     g0=0, g1=None, kc_lo=0, kc_hi=None, x_kc_base=0,
                     x_waits=None):
    nc = p.nc
    assert TD == ttile
    x_stride = x_stride if x_stride is not None else TD
    KC = KTOT // 128
    FC = NOUT // 128
    NGtot = FC // G
    g1 = g1 if g1 is not None else NGtot
    kc_hi = kc_hi if kc_hi is not None else KC
    glist = list(range(g0, g1))
    NG = len(glist)
    wst = [p.sbuf(f"wst{i}", [128, 8192], wdtype) for i in range(2)]
    assert KC * G * 128 <= 8192
    ps = [p.psum(f"prs{i}", [128, ttile], F32) for i in range(G)]
    pe0, ac0 = p.pe.n, p.ac.n
    w20 = [p.w2[0].n, p.w2[1].n]
    stage_w2 = []
    for i, gi in enumerate(glist):
        w20[i % 2] += 16
        stage_w2.append((i % 2, w20[i % 2]))

    def gq(g_):
        for i, gi in enumerate(glist):
            par = i % 2
            g_.wait_ge(p.pe.h, max(0, pe0 + (i - 1) * G))
            g_.dma_start(out=wst[par][:, 0:KC * G * 128],
                         in_=wtiled.ap()[gi * 128:(gi + 1) * 128, :]
                         ).then_inc(p.w2[par].h, 16)
    blk.gpsimd(gq)
    p.w2[0].n, p.w2[1].n = w20

    def tloop(t_):
        for w in (x_waits or []):
            t_.wait_ge(w[0], w[1])
        t_.wait_ge(p.ac.h, ac0)
        for i, gi in enumerate(glist):
            par, w2t = stage_w2[i]
            t_.wait_ge(p.w2[par].h, w2t)
            w = wst[par]
            for fi in range(G):
                t_.wait_ge(p.ac.h, max(0, ac0 + i * G + fi - G + 1))
                pp = ps[fi]
                kcl = list(range(kc_lo, kc_hi))
                for jk, kc in enumerate(kcl):
                    xkc = kc - x_kc_base
                    mm = nc.tensor.matmul(
                        pp[:, :],
                        w[:, kc * G * 128 + fi * 128: kc * G * 128 + fi * 128 + 128],
                        xl[:, xkc * x_stride + x_col0: xkc * x_stride + x_col0 + ttile],
                        start=(jk == 0), stop=(jk == len(kcl) - 1))
                mm.then_inc(p.pe.h, 1)
    blk.tensor(tloop)
    p.pe.n = pe0 + NG * G

    def aloop(a_):
        for i, gi in enumerate(glist):
            for fi in range(G):
                a_.wait_ge(p.pe.h, pe0 + i * G + fi + 1)
                epilogue(a_, ps[fi], gi * G + fi, 0).then_inc(p.ac.h, 1)
    blk.scalar(aloop)
    p.ac.n = ac0 + NG * G


def host_tile_weight(wT, G=2):
    KTOT, NOUT = wT.shape
    KC, FC = KTOT // 128, NOUT // 128
    NG = FC // G
    t = wT.reshape(KC, 128, NG, G * 128).transpose(2, 1, 0, 3)
    return np.ascontiguousarray(t.reshape(NG * 128, KC * G * 128)).astype(np.float32)


def emit_mul_pass(p, blk, dst, a, b, n, width, op=ALU.mult):
    ac_prev = p.ac.n

    def v_(v):
        v.wait_ge(p.ac.h, ac_prev)
        last = None
        for i in range(n):
            last = v.tensor_tensor(dst[:, i * width:(i + 1) * width],
                                   a[:, i * width:(i + 1) * width],
                                   b[:, i * width:(i + 1) * width], op=op)
        last.then_inc(p.dv.h, 1)
        p.dv.n += 1
    blk.vector(v_)



def build_l2_ffn():
    p = Prog()
    nc = p.nc
    io = stream_io_tensors(p)
    wT = {}
    for nm, K, N in [("w1T", D, 4096), ("w2T", D, 4096),
                     ("h1T", 4096, 4096), ("h2T", 4096, 4096),
                     ("w3T", 4096, D)]:
        NG = (N // 128) // 2
        wT[nm] = nc.dram_tensor(nm, [NG * 128, (K // 128) * 2 * 128], BF16,
                                kind="ExternalInput")
    scols = {}
    for nm, n in [("s1", 4096), ("s2", 4096), ("sh1", 4096), ("sh2", 4096),
                  ("s3", D)]:
        scols[nm] = nc.dram_tensor(nm + "c", [128, n // 128], F32,
                                   kind="ExternalInput")

    y = p.sbuf("y", [128, 8 * TCORE], BF16)
    row = p.sbuf("row", [1, 3 * TCORE], F32)
    sc = {k: p.sbuf("sc" + k, [128, scols[k].shape[-1]], F32) for k in scols}

    with p.phase() as blk:
        c = load_consts(p, blk, io)

        def _(g):
            for k in sc:
                dma(g, p.da, sc[k][:, :], scols[k][:, :])
            dma_wait_all(g, p.da)
        blk.gpsimd(_)

    with p.scope():
        xl = p.sbuf("xl", [128, 8 * TCORE], BF16)
        with p.phase() as blk:
            emit_load_xl(p, blk, io, xl)
        with p.phase() as blk:
            emit_sumsq_row(p, blk, xl, TCORE, c["ident"], row, 0)
        with p.phase() as blk:
            emit_row_rstd(p, blk, row, 0, 0, TCORE)
        with p.phase() as blk:
            pb = [p.psum(f"lnb{i}", [128, 512], F32) for i in range(2)]
            pe0, dv0 = p.pe.n, p.dv.n

            def t_(e):
                for si in range(4):
                    if si >= 2:
                        e.wait_ge(p.dv.h, dv0 + si - 1)
                    last_mm = nc.tensor.matmul(pb[si % 2][:, :], c["ones1"][0:1, :],
                                     row[0:1, si * 512:si * 512 + 512],
                                     start=True, stop=True)
                    last_mm.then_inc(p.pe.h, 1)
            blk.tensor(t_)
            p.pe.n = pe0 + 4

            def v_(v):
                for si in range(4):
                    v.wait_ge(p.pe.h, pe0 + si + 1)
                    last = None
                    for kc in range(8):
                        sl = slice(kc * TCORE + si * 512, kc * TCORE + si * 512 + 512)
                        last = v.tensor_mul(xl[:, sl], xl[:, sl], pb[si % 2][:, :])
                    last.then_inc(p.dv.h, 1)
            blk.vector(v_)
            p.dv.n = dv0 + 4

        g1 = p.sbuf("g1", [128, 32 * 512], BF16)
        g2 = p.sbuf("g2", [128, 32 * 512], BF16)
        with p.phase() as blk:
            for q in range(4):
                q0 = q * 512
                emit_proj_stream(p, blk, wT["w1T"], D, 4096, xl, 512,
                                 act_epi(g1, 512, func=AF.Silu, scale=(sc["s1"],)),
                                 G=2, x_col0=q0, x_stride=TCORE)
                emit_proj_stream(p, blk, wT["w2T"], D, 4096, xl, 512,
                                 act_epi(g2, 512, func=AF.Copy, scale=(sc["s2"],)),
                                 G=2, x_col0=q0, x_stride=TCORE)
                emit_mul_pass(p, blk, g1, g1, g2, 16, 1024)
                emit_proj_stream(p, blk, wT["h1T"], 4096, 4096, g1, 512,
                                 act_epi(g2, 512, func=AF.Gelu, scale=(sc["sh1"],)),
                                 G=2, x_waits=[(p.dv.h, p.dv.n)])
                emit_proj_stream(p, blk, wT["h2T"], 4096, 4096, g2, 512,
                                 act_epi(g1, 512, func=AF.Gelu, scale=(sc["sh2"],)),
                                 G=2)
                emit_proj_stream(p, blk, wT["w3T"], 4096, D, g1, 512,
                                 act_epi(y, TCORE, func=AF.Copy, scale=(sc["s3"],),
                                         out_col0=q0),
                                 G=2)

    with p.scope():
        emit_postln_residual_streamed(p, io, y, c, row)
    return p.finish()


def emit_postln_residual_streamed(p, io, y_sb, c, row, TD=TCORE,
                                  ones_dt_key="onescb"):
    nc = p.nc
    o_ssq, o_mu, o_rstd = 0, TD, 2 * TD
    with p.phase() as blk:
        emit_sumsq_row(p, blk, y_sb, TD, c["ident"], row, o_ssq)
    with p.phase() as blk:
        pb = [p.psum(f"mub{i}", [1, 512], F32) for i in range(2)]
        pe0, dv0 = p.pe.n, p.dv.n

        def t_(e):
            for si in range(TD // 512):
                if si >= 2:
                    e.wait_ge(p.dv.h, dv0 + si - 1)
                for kc in range(8):
                    last_mm = nc.tensor.matmul(pb[si % 2][:, :], c[ones_dt_key][:, 0:1],
                                     y_sb[:, kc * TD + si * 512:kc * TD + si * 512 + 512],
                                     start=(kc == 0), stop=(kc == 7))
                last_mm.then_inc(p.pe.h, 1)
        blk.tensor(t_)
        p.pe.n = pe0 + TD // 512

        def v_(v):
            for si in range(TD // 512):
                v.wait_ge(p.pe.h, pe0 + si + 1)
                v.tensor_scalar_mul(row[0:1, o_mu + si * 512:o_mu + si * 512 + 512],
                                    pb[si % 2][0:1, :], 1.0 / D).then_inc(p.dv.h, 1)
        blk.vector(v_)
        p.dv.n = dv0 + TD // 512
    with p.phase() as blk:
        def s_(a):
            a.activation(row[0:1, o_rstd:o_rstd + TD], row[0:1, o_mu:o_mu + TD],
                         AF.Square).then_inc(p.ac.h, 1)
            p.ac.n += 1
        blk.scalar(s_)

        def v_(v):
            v.wait_ge(p.ac.h, p.ac.n)
            v.tensor_scalar(row[0:1, o_ssq:o_ssq + TD], row[0:1, o_ssq:o_ssq + TD],
                            1.0 / D, None, op0=ALU.mult).then_inc(p.dv.h, 1)
            p.dv.n += 1
            v.wait_ge(p.dv.h, p.dv.n)
            v.tensor_sub(row[0:1, o_ssq:o_ssq + TD], row[0:1, o_ssq:o_ssq + TD],
                         row[0:1, o_rstd:o_rstd + TD]).then_inc(p.dv.h, 1)
            p.dv.n += 1
        blk.vector(v_)

        def s2_(a):
            a.wait_ge(p.dv.h, p.dv.n)
            a.activation(row[0:1, o_rstd:o_rstd + TD], row[0:1, o_ssq:o_ssq + TD],
                         AF.Ln, bias=EPS).then_inc(p.ac.h, 1)
            p.ac.n += 1
            a.wait_ge(p.ac.h, p.ac.n)
            a.activation(row[0:1, o_rstd:o_rstd + TD], row[0:1, o_rstd:o_rstd + TD],
                         AF.Exp, scale=-0.5).then_inc(p.ac.h, 1)
            p.ac.n += 1
        blk.scalar(s2_)
    nq = TD // 512
    xsq = [p.sbuf(f"xsq{i}", [128, 8 * 512], F32) for i in range(2)]
    scrt = p.sbuf("plr", [128, 512], F32)
    with p.phase() as blk:
        pmu = [p.psum(f"pmu{i}", [128, 512], F32) for i in range(2)]
        prs = [p.psum(f"prs2{i}", [128, 512], F32) for i in range(2)]
        pe0, dv0 = p.pe.n, p.dv.n
        ld_at, st_at = [], []

        def g_(g):
            for q in range(nq):
                par = q % 2
                if q >= 2:
                    g.wait_ge(p.da.h, st_at[q - 2])
                dma(g, p.da, xsq[par][:, :],
                    io["xin"].ap()[:, q * 512:(q + 1) * 512]
                    .rearrange("(kc p) t -> p kc t", p=128))
                ld_at.append(p.da.n)
                if q >= 1:
                    qq = q - 1
                    g.wait_ge(p.dv.h, dv0 + 24 * (qq + 1))
                    dma(g, p.da, io["xout"].ap()[:, qq * 512:(qq + 1) * 512]
                        .rearrange("(kc p) t -> p kc t", p=128),
                        xsq[qq % 2][:, :])
                    st_at.append(p.da.n)
            qq = nq - 1
            g.wait_ge(p.dv.h, dv0 + 24 * nq)
            dma(g, p.da, io["xout"].ap()[:, qq * 512:(qq + 1) * 512]
                .rearrange("(kc p) t -> p kc t", p=128), xsq[qq % 2][:, :])
            st_at.append(p.da.n)
            g.wait_ge(p.da.h, p.da.n)
        blk.gpsimd(g_)

        def t_(e):
            for q in range(nq):
                par = q % 2
                if q >= 2:
                    e.wait_ge(p.dv.h, dv0 + 24 * (q - 1))
                h0 = q * 512
                nc.tensor.matmul(pmu[par][:, :], c["ones1"][0:1, :],
                                 row[0:1, o_mu + h0:o_mu + h0 + 512],
                                 start=True, stop=True)
                mm = nc.tensor.matmul(prs[par][:, :], c["ones1"][0:1, :],
                                      row[0:1, o_rstd + h0:o_rstd + h0 + 512],
                                      start=True, stop=True)
                mm.then_inc(p.pe.h, 1)
        blk.tensor(t_)
        p.pe.n = pe0 + nq

        def v_(v):
            for q in range(nq):
                par = q % 2
                h0 = q * 512
                v.wait_ge(p.pe.h, pe0 + q + 1)
                v.wait_ge(p.da.h, ld_at[q])
                for kc in range(8):
                    yv = y_sb[:, kc * TD + h0:kc * TD + h0 + 512]
                    xv = xsq[par][:, kc * 512:(kc + 1) * 512]
                    v.tensor_sub(scrt[:, :], yv,
                                 pmu[par][:, :]).then_inc(p.dv.h, 1)
                    p.dv.n += 1
                    v.wait_ge(p.dv.h, p.dv.n)
                    v.tensor_mul(scrt[:, :], scrt[:, :],
                                 prs[par][:, :]).then_inc(p.dv.h, 1)
                    p.dv.n += 1
                    v.wait_ge(p.dv.h, p.dv.n)
                    v.tensor_add(xv, xv, scrt[:, :]).then_inc(p.dv.h, 1)
                    p.dv.n += 1
                    v.wait_ge(p.dv.h, p.dv.n)
        blk.vector(v_)


def host_prep_l2ffn(H, inputs):
    f32 = np.float32
    H["l2_w1T"] = host_tile_weight(np.asarray(inputs["L2_ffn_w1_w"]).T).astype(BF)
    H["l2_w2T"] = host_tile_weight(np.asarray(inputs["L2_ffn_w2_w"]).T).astype(BF)
    H["l2_h1T"] = host_tile_weight(np.asarray(inputs["L2_ffn_h1_w"]).T).astype(BF)
    H["l2_h2T"] = host_tile_weight(np.asarray(inputs["L2_ffn_h2_w"]).T).astype(BF)
    H["l2_w3T"] = host_tile_weight(np.asarray(inputs["L2_ffn_w3_w"]).T).astype(BF)
    for nm, key in [("s1", "L2_ffn_w1_s"), ("s2", "L2_ffn_w2_s"),
                    ("sh1", "L2_ffn_h1_s"), ("sh2", "L2_ffn_h2_s"),
                    ("s3", "L2_ffn_w3_s")]:
        s = np.asarray(inputs[key]).astype(f32)
        H["l2_" + nm] = np.ascontiguousarray(s.reshape(-1, 128).T)
    return H


def ins_l2_ffn(cidx, H, inputs, state):
    m = {"xin": state["stream"][cidx], "ident_f32": H["ident_f32"],
         "ones1_f32": H["ones1_f32"], "constv": H["constv"],
         "w1T": H["l2_w1T"], "w2T": H["l2_w2T"], "h1T": H["l2_h1T"],
         "h2T": H["l2_h2T"], "w3T": H["l2_w3T"]}
    for nm in ["s1", "s2", "sh1", "sh2", "s3"]:
        m[nm + "c"] = H["l2_" + nm]
    return m




def build_l1_attn():
    p = Prog()
    nc = p.nc
    io = stream_io_tensors(p)
    wd = {nm: nc.dram_tensor(nm, [D, D], BF16, kind="ExternalInput")
          for nm in ["WqT", "WkT", "WvT", "WoT"]}
    bqc = nc.dram_tensor("bqc", [128, 8], F32, kind="ExternalInput")
    bkc = nc.dram_tensor("bkc", [128, 8], F32, kind="ExternalInput")
    boc = nc.dram_tensor("boc", [128, 8], F32, kind="ExternalInput")
    bvr = nc.dram_tensor("bvr", [1, D], F32, kind="ExternalInput")
    kvin = nc.dram_tensor("kvin", [4096, 1024], BF16)
    kvg = nc.dram_tensor("kvg", [16384, 1024], BF16)

    row = p.sbuf("row", [1, 3 * TCORE], F32)
    xl = p.sbuf("xl", [128, 8 * TCORE], BF16)
    y = p.sbuf("y", [128, 8 * TCORE], BF16)

    with p.phase() as blk:
        c = load_consts(p, blk, io)
        sb_b = {}
        def _(g):
            for nm, t, shape in [("bqc", bqc, [128, 8]), ("bkc", bkc, [128, 8]),
                                 ("boc", boc, [128, 8]), ("bvr", bvr, [1, D])]:
                sb_b[nm] = p.sbuf(nm + "s", shape, F32)
                dma(g, p.da, sb_b[nm][:, :], t[:, :])
            dma_wait_all(g, p.da)
        blk.gpsimd(_)

    with p.scope():
        with p.phase() as blk:
            emit_load_xl(p, blk, io, xl)
        with p.phase() as blk:
            emit_sumsq_row(p, blk, xl, TCORE, c["ident"], row, 0)
        with p.phase() as blk:
            emit_row_rstd(p, blk, row, 0, 0, TCORE)
        with p.phase() as blk:
            pb = [p.psum(f"lnb{i}", [128, 512], F32) for i in range(2)]
            pe0, dv0 = p.pe.n, p.dv.n

            def t_(e):
                for si in range(4):
                    if si >= 2:
                        e.wait_ge(p.dv.h, dv0 + si - 1)
                    mm = nc.tensor.matmul(pb[si % 2][:, :], c["ones1"][0:1, :],
                                          row[0:1, si * 512:si * 512 + 512],
                                          start=True, stop=True)
                    mm.then_inc(p.pe.h, 1)
            blk.tensor(t_)
            p.pe.n = pe0 + 4

            def v_(v):
                for si in range(4):
                    v.wait_ge(p.pe.h, pe0 + si + 1)
                    last = None
                    for kc in range(8):
                        sl = slice(kc * TCORE + si * 512, kc * TCORE + si * 512 + 512)
                        last = v.tensor_mul(xl[:, sl], xl[:, sl], pb[si % 2][:, :])
                    last.then_inc(p.dv.h, 1)
            blk.vector(v_)
            p.dv.n = dv0 + 4

    with p.scope():
        with p.phase() as blk:
            wk = load_weight(p, blk, wd["WkT"], D, D, "wk")
        kfm = p.sbuf("kfm", [128, 8 * TCORE], BF16)
        with p.phase() as blk:
            emit_proj(p, blk, wk, D, xl, TCORE,
                      act_epi(kfm, TCORE, bias=sb_b["bkc"]))
        with p.phase() as blk:
            def _(g):
                dma(g, p.da,
                    bass.AP(kvin, 0, [[2048, 128], [128 * 2048, 8], [1, 2048]]),
                    kfm[:, :])
                dma_wait_all(g, p.da)
            blk.gpsimd(_)
    with p.scope():
        with p.phase() as blk:
            wv = load_weight(p, blk, wd["WvT"], D, D, "wv")
        vtm = p.sbuf("vtm", [128, 16 * 1024], BF16)
        with p.phase() as blk:
            emit_proj_tm(p, blk, wv, D, xl, TCORE, vtm,
                         bias_row=sb_b["bvr"], ones1=c["ones1"])
        with p.phase() as blk:
            def _(g):
                dma(g, p.da,
                    bass.AP(kvin, 2048 * 1024,
                            [[1024, 128], [128 * 1024, 16], [1, 1024]]),
                    vtm[:, :])
                dma_wait_all(g, p.da)
            blk.gpsimd(_)
    with p.phase() as blk:
        def _(g):
            g.collective_compute(
                "AllGather", ALU.bypass, replica_groups=GROUPS,
                ins=[kvin.ap().opt()], outs=[kvg.ap().opt()]).then_inc(p.cc.h, 1)
            p.cc.n += 1
            g.wait_ge(p.cc.h, p.cc.n)
        blk.gpsimd(_)
    ctxn = p.sbuf("ctxn", [128, 8 * TCORE], BF16)
    with p.scope():
        with p.phase() as blk:
            wq = load_weight(p, blk, wd["WqT"], D, D, "wq")
        emit_flash(p, {
            "xl": xl, "WqT_sb": wq, "bq_col": sb_b["bqc"], "kvg": kvg,
            "ctxn": ctxn, "scale": 0.125, "TD": TCORE, "dt": BF16,
            "ones1": c["ones1"],
            "quarters_for": lambda hp, qt: [(q, [2 * q, 2 * q + 1])
                                            for q in range(4)],
        })
    with p.scope():
        with p.phase() as blk:
            wo = load_weight(p, blk, wd["WoT"], D, D, "wo")
        with p.phase() as blk:
            emit_proj(p, blk, wo, D, ctxn, TCORE,
                      act_epi(y, TCORE, bias=sb_b["boc"]))
    with p.scope():
        emit_postln_residual_streamed(p, io, y, c, row)
    return p.finish()


def host_prep_l1attn(H, inputs):
    f32 = np.float32
    for nm, key in [("WqT", "L1_attn_Wq"), ("WkT", "L1_attn_Wk"),
                    ("WvT", "L1_attn_Wv"), ("WoT", "L1_attn_Wo")]:
        H["l1_" + nm] = np.ascontiguousarray(
            np.asarray(inputs[key], f32).T).astype(BF)
    for nm, key in [("bqc", "L1_attn_bq"), ("bkc", "L1_attn_bk"),
                    ("boc", "L1_attn_bo")]:
        H["l1_" + nm] = np.ascontiguousarray(
            np.asarray(inputs[key], f32).reshape(8, 128).T)
    H["l1_bvr"] = np.asarray(inputs["L1_attn_bv"], f32).reshape(1, D)
    return H


def ins_l1_attn(cx, H, inputs, state):
    return {"xin": state["stream"][cx], "ident_f32": H["ident_f32"],
            "ones1_f32": H["ones1_f32"], "constv": H["constv"],
            "WqT": H["l1_WqT"], "WkT": H["l1_WkT"], "WvT": H["l1_WvT"],
            "WoT": H["l1_WoT"], "bqc": H["l1_bqc"], "bkc": H["l1_bkc"],
            "boc": H["l1_boc"], "bvr": H["l1_bvr"]}




def build_l0_ffn():
    p = Prog()
    nc = p.nc
    io = stream_io_tensors(p)
    wiTt = nc.dram_tensor("wiTt", [16 * 128, 8 * 2 * 128], BF16,
                          kind="ExternalInput")
    woTt = nc.dram_tensor("woTt", [4 * 128, 32 * 2 * 128], BF16,
                          kind="ExternalInput")
    A0T = nc.dram_tensor("A0T", [D, 16], BF16, kind="ExternalInput")
    A1T = nc.dram_tensor("A1T", [D, 16], BF16, kind="ExternalInput")
    B0T = nc.dram_tensor("B0T", [16, 4096], BF16, kind="ExternalInput")
    B1T = nc.dram_tensor("B1T", [16, 4096], BF16, kind="ExternalInput")
    vcol = nc.dram_tensor("vcol", [128, 2], F32, kind="ExternalInput")

    row = p.sbuf("row", [1, 3 * TCORE], F32)
    xl = p.sbuf("xl", [128, 8 * TCORE], BF16)
    y = p.sbuf("y", [128, 8 * TCORE], BF16)

    with p.phase() as blk:
        c = load_consts(p, blk, io)
        sb = {}
        def _(g):
            for nm, t, shape, dt2 in [("B0", B0T, [16, 4096], BF16),
                                      ("B1", B1T, [16, 4096], BF16),
                                      ("vc", vcol, [128, 2], F32)]:
                sb[nm] = p.sbuf(nm + "s", shape, dt2)
                dma(g, p.da, sb[nm][:, :], t[:, :])
            dma_wait_all(g, p.da)
        blk.gpsimd(_)
    with p.phase() as blk:
        sb["nB0"] = p.sbuf("nB0", [16, 4096], BF16)
        def v_(v):
            v.wait_ge(p.da.h, p.da.n)
            for i in range(2):
                v.tensor_scalar_mul(sb["nB0"][:, i * 2048:(i + 1) * 2048],
                                    sb["B0"][:, i * 2048:(i + 1) * 2048], -1.0)
        blk.vector(v_)

    with p.scope():
        emit_preln_std(p, io, c, xl, row)

    rk = p.sbuf("rk", [16, 2 * TCORE], BF16)
    with p.scope():
        with p.phase() as blk:
            aw = {}
            def _(g):
                for i, t in enumerate([A0T, A1T]):
                    aw[i] = p.sbuf(f"aw{i}", [128, 8 * 16], BF16)
                    dma(g, p.da, aw[i][:, :],
                        t.ap().rearrange("(kc p) n -> p kc n", p=128))
                dma_wait_all(g, p.da)
            blk.gpsimd(_)
        with p.phase() as blk:
            pslo = [p.psum(f"lor{i}", [16, 512], F32) for i in range(2)]
            pe0, ac0 = p.pe.n, p.ac.n

            def t_(e):
                i = 0
                for k in range(2):
                    for tt in range(4):
                        if i >= 2:
                            e.wait_ge(p.ac.h, ac0 + i - 1)
                        for kc in range(8):
                            mm = nc.tensor.matmul(
                                pslo[i % 2][:, :],
                                aw[k][:, kc * 16:(kc + 1) * 16],
                                xl[:, kc * TCORE + tt * 512: kc * TCORE + tt * 512 + 512],
                                start=(kc == 0), stop=(kc == 7))
                        mm.then_inc(p.pe.h, 1)
                        i += 1
            blk.tensor(t_)
            p.pe.n = pe0 + 8

            def a_(a):
                i = 0
                for k in range(2):
                    for tt in range(4):
                        a.wait_ge(p.pe.h, pe0 + i + 1)
                        a.activation(rk[0:16, k * TCORE + tt * 512:
                                        k * TCORE + tt * 512 + 512],
                                     pslo[i % 2][:, :], AF.Copy).then_inc(p.ac.h, 1)
                        i += 1
            blk.scalar(a_)
            p.ac.n = ac0 + 8

    with p.scope():
        wst = [p.sbuf(f"wil{i}", [128, 8 * 256], BF16) for i in range(2)]
        cbuf = p.sbuf("cbuf", [128, 32 * 512], BF16)
        r0t = p.sbuf("r0t", [128, 2 * 512], BF16)
        r1t = p.sbuf("r1t", [128, 2 * 512], BF16)
        tmpc = p.sbuf("tmpc", [128, 512], F32)
        for q in range(4):
            q0 = q * 512
            with p.phase() as blk:
                ps = [p.psum(f"l0m{i}", [128, 512], F32) for i in range(4)]
                pe0, ac0, dv0 = p.pe.n, p.ac.n, p.dv.n
                w20 = [p.w2[0].n, p.w2[1].n]

                def gq(g_):
                    for gi in range(16):
                        if gi >= 2:
                            g_.wait_ge(p.pe.h, pe0 + 4 * (gi - 1))
                        g_.dma_start(out=wst[gi % 2][:, :],
                                     in_=wiTt.ap()[gi * 128:(gi + 1) * 128, :]
                                     ).then_inc(p.w2[gi % 2].h, 16)
                blk.gpsimd(gq)
                stage_w2 = []
                for gi in range(16):
                    w20[gi % 2] += 16
                    stage_w2.append(w20[gi % 2])
                p.w2[0].n, p.w2[1].n = w20

                def t_(e):
                    for fc in range(32):
                        par = fc % 2
                        gi = fc // 2
                        if fc % 2 == 0:
                            e.wait_ge(p.w2[gi % 2].h, stage_w2[gi])
                        if fc >= 2:
                            e.wait_ge(p.dv.h, dv0 + 2 * (fc - 1))
                        for k in range(2):
                            pp = ps[2 * par + k]
                            for kc in range(8):
                                nc.tensor.matmul(
                                    pp[:, :],
                                    wst[gi % 2][:, kc * 256 + (fc % 2) * 128:
                                                kc * 256 + (fc % 2) * 128 + 128],
                                    xl[:, kc * TCORE + q0: kc * TCORE + q0 + 512],
                                    start=(kc == 0), stop=False)
                            mm = nc.tensor.matmul(
                                pp[:, :],
                                sb["B0" if k == 0 else "B1"][0:16,
                                                             fc * 128:fc * 128 + 128],
                                rk[0:16, k * TCORE + q0: k * TCORE + q0 + 512],
                                start=False, stop=True)
                            mm.then_inc(p.pe.h, 1)
                blk.tensor(t_)
                p.pe.n = pe0 + 64

                def a_(a):
                    for fc in range(32):
                        par = fc % 2
                        a.wait_ge(p.pe.h, pe0 + 2 * fc + 1)
                        a.activation(r0t[:, par * 512:(par + 1) * 512],
                                     ps[2 * par][:, :], AF.Relu).then_inc(p.ac.h, 1)
                        a.wait_ge(p.pe.h, pe0 + 2 * fc + 2)
                        a.activation(r1t[:, par * 512:(par + 1) * 512],
                                     ps[2 * par + 1][:, :], AF.Relu).then_inc(p.ac.h, 1)
                blk.scalar(a_)
                p.ac.n = ac0 + 64

                def v_(v):
                    for fc in range(32):
                        par = fc % 2
                        v.wait_ge(p.ac.h, ac0 + 2 * fc + 2)
                        if fc >= 1:
                            v.wait_ge(p.dv.h, dv0 + 2 * fc)
                        v.tensor_scalar_mul(tmpc[:, :],
                                            r1t[:, par * 512:(par + 1) * 512],
                                            sb["vc"][:, 1:2]).then_inc(p.dv.h, 1)
                        v.wait_ge(p.dv.h, dv0 + 2 * fc + 1)
                        v.scalar_tensor_tensor(
                            cbuf[:, fc * 512:(fc + 1) * 512],
                            r0t[:, par * 512:(par + 1) * 512],
                            sb["vc"][:, 0:1], tmpc[:, :],
                            op0=ALU.mult, op1=ALU.add).then_inc(p.dv.h, 1)
                blk.vector(v_)
                p.dv.n = dv0 + 64
            with p.phase() as blk:
                emit_proj_stream(p, blk, woTt, 4096, D, cbuf, 512,
                                 act_epi(y, TCORE, func=AF.Copy, out_col0=q0),
                                 G=2)
    with p.scope():
        emit_postln_residual_streamed(p, io, y, c, row)
    return p.finish()


def emit_preln_std(p, io, c, xl, row):
    nc = p.nc
    with p.phase() as blk:
        emit_load_xl(p, blk, io, xl)
    with p.phase() as blk:
        emit_sumsq_row(p, blk, xl, TCORE, c["ident"], row, 0)
    with p.phase() as blk:
        emit_row_rstd(p, blk, row, 0, 0, TCORE)
    with p.phase() as blk:
        pb = [p.psum(f"lnb{i}", [128, 512], F32) for i in range(2)]
        pe0, dv0 = p.pe.n, p.dv.n

        def t_(e):
            for si in range(4):
                if si >= 2:
                    e.wait_ge(p.dv.h, dv0 + si - 1)
                mm = nc.tensor.matmul(pb[si % 2][:, :], c["ones1"][0:1, :],
                                      row[0:1, si * 512:si * 512 + 512],
                                      start=True, stop=True)
                mm.then_inc(p.pe.h, 1)
        blk.tensor(t_)
        p.pe.n = pe0 + 4

        def v_(v):
            for si in range(4):
                v.wait_ge(p.pe.h, pe0 + si + 1)
                last = None
                for kc in range(8):
                    sl = slice(kc * TCORE + si * 512, kc * TCORE + si * 512 + 512)
                    last = v.tensor_mul(xl[:, sl], xl[:, sl], pb[si % 2][:, :])
                last.then_inc(p.dv.h, 1)
        blk.vector(v_)
        p.dv.n = dv0 + 4


def host_prep_l0ffn(H, inputs):
    f32 = np.float32
    H["l0f_wiTt"] = host_tile_weight(
        np.ascontiguousarray(np.asarray(inputs["L0_ffn_wi_w"], f32).T)).astype(BF)
    H["l0f_woTt"] = host_tile_weight(
        np.ascontiguousarray(np.asarray(inputs["L0_ffn_wo_w"], f32).T)).astype(BF)
    idx = np.asarray(inputs["L0_ffn_top_k_indices"]).astype(np.int64)
    As = np.asarray(inputs["L0_ffn_lora_As"], f32)
    Bs = np.asarray(inputs["L0_ffn_lora_Bs"], f32)
    H["l0f_A0T"] = np.ascontiguousarray(As[idx[0]].T).astype(BF)
    H["l0f_A1T"] = np.ascontiguousarray(As[idx[1]].T).astype(BF)
    H["l0f_B0T"] = np.ascontiguousarray(Bs[idx[0]].T).astype(BF)
    H["l0f_B1T"] = np.ascontiguousarray(Bs[idx[1]].T).astype(BF)
    v = np.asarray(inputs["L0_ffn_top_k_values"], f32)
    H["l0f_vcol"] = np.broadcast_to(v.reshape(1, 2), (128, 2)).copy()
    return H


def ins_l0_ffn(cx, H, inputs, state):
    return {"xin": state["stream"][cx], "ident_f32": H["ident_f32"],
            "ones1_f32": H["ones1_f32"], "constv": H["constv"],
            "wiTt": H["l0f_wiTt"], "woTt": H["l0f_woTt"],
            "A0T": H["l0f_A0T"], "A1T": H["l0f_A1T"],
            "B0T": H["l0f_B0T"], "B1T": H["l0f_B1T"], "vcol": H["l0f_vcol"]}




def build_l3_ffn():
    p = Prog()
    nc = p.nc
    io = stream_io_tensors(p)
    routerT = nc.dram_tensor("routerT", [D, 32], F32, kind="ExternalInput")
    rbias = nc.dram_tensor("rbias", [1, 32], F32, kind="ExternalInput")
    gateTt = nc.dram_tensor("gateTt", [16 * 128, 8 * 256], BF16, kind="ExternalInput")
    upTt = nc.dram_tensor("upTt", [16 * 128, 8 * 256], BF16, kind="ExternalInput")
    downTt = nc.dram_tensor("downTt", [4 * 128, 32 * 256], BF16, kind="ExternalInput")
    sgateT = nc.dram_tensor("sgateT", [D, 512], BF16, kind="ExternalInput")
    supT = nc.dram_tensor("supT", [D, 512], BF16, kind="ExternalInput")
    sdownT = nc.dram_tensor("sdownT", [512, D], BF16, kind="ExternalInput")
    wscr = nc.dram_tensor("wscr", [32, TCORE], F32)

    row = p.sbuf("row", [1, 3 * TCORE], F32)
    xl = p.sbuf("xl", [128, 8 * TCORE], BF16)
    y = p.sbuf("y", [128, 8 * TCORE], BF16)
    wrows = p.sbuf("wrows", [32, TCORE], F32)

    with p.phase() as blk:
        c = load_consts(p, blk, io)
        sb = {}
        def _(g):
            sb["rt"] = p.sbuf("rt", [128, 8 * 32], F32)
            dma(g, p.da, sb["rt"][:, :],
                routerT.ap().rearrange("(kc p) n -> p kc n", p=128))
            sb["rb"] = p.sbuf("rb", [1, 32], F32)
            dma(g, p.da, sb["rb"][:, :], rbias[:, :])
            dma_wait_all(g, p.da)
        blk.gpsimd(_)

    wtok = p.sbuf("wtok", [128, 16 * 32], F32)
    rstdc = p.sbuf("rstdc", [128, 16], F32)
    with p.scope():
        xs = p.sbuf("xs", [128, 8 * TCORE], F32)
        with p.phase() as blk:
            emit_load_xl(p, blk, io, xs, cast=False)
        with p.phase() as blk:
            def v_(v):
                for kc in range(8):
                    for t0 in range(0, TCORE, 1024):
                        v.tensor_copy(xl[:, kc * TCORE + t0:kc * TCORE + t0 + 1024],
                                      xs[:, kc * TCORE + t0:kc * TCORE + t0 + 1024])
            blk.vector(v_)
        with p.phase() as blk:
            emit_sumsq_row(p, blk, xl, TCORE, c["ident"], row, 0, rstd_cols=rstdc)
        with p.phase() as blk:
            emit_row_rstd(p, blk, row, 0, 0, TCORE)
        with p.phase() as blk:
            pb = [p.psum(f"lnb{i}", [128, 512], F32) for i in range(2)]
            pe0, dv0 = p.pe.n, p.dv.n

            def t_(e):
                for si in range(4):
                    if si >= 2:
                        e.wait_ge(p.dv.h, dv0 + si - 1)
                    mm = nc.tensor.matmul(pb[si % 2][:, :], c["ones1"][0:1, :],
                                          row[0:1, si * 512:si * 512 + 512],
                                          start=True, stop=True)
                    mm.then_inc(p.pe.h, 1)
            blk.tensor(t_)
            p.pe.n = pe0 + 4

            def v_(v):
                for si in range(4):
                    v.wait_ge(p.pe.h, pe0 + si + 1)
                    last = None
                    for kc in range(8):
                        sl = slice(kc * TCORE + si * 512, kc * TCORE + si * 512 + 512)
                        last = v.tensor_mul(xl[:, sl], xl[:, sl], pb[si % 2][:, :])
                    last.then_inc(p.dv.h, 1)
            blk.vector(v_)
            p.dv.n = dv0 + 4
        rbb = p.sbuf("rbb", [128, 32], F32)
        with p.phase() as blk:
            psr = [p.psum(f"rps{i}", [128, 512], F32) for i in range(2)]
            pe0, ac0, dv0 = p.pe.n, p.ac.n, p.dv.n

            def t_(e):
                mm = nc.tensor.matmul(psr[0][:, 0:32], c["ones1"][0:1, :],
                                      sb["rb"][0:1, :], start=True, stop=True)
                mm.then_inc(p.pe.h, 1)
            blk.tensor(t_)
            p.pe.n = pe0 + 1

            def v0_(v):
                v.wait_ge(p.pe.h, pe0 + 1)
                v.tensor_copy(rbb[:, :], psr[0][:, 0:32]).then_inc(p.dv.h, 1)
                p.dv.n += 1
            blk.vector(v0_)
        with p.phase() as blk:
            psr = [p.psum(f"rps{i}", [128, 512], F32) for i in range(2)]
            pe0, ac0, dv0 = p.pe.n, p.ac.n, p.dv.n

            def t_(e):
                for tt in range(16):
                    if tt >= 2:
                        e.wait_ge(p.ac.h, ac0 + tt - 1)
                    pp = psr[tt % 2]
                    for kc in range(8):
                        mm = nc.tensor.matmul(
                            pp[:, 0:32],
                            xs[:, kc * TCORE + tt * 128: kc * TCORE + tt * 128 + 128],
                            sb["rt"][:, kc * 32:(kc + 1) * 32],
                            start=(kc == 0), stop=(kc == 7))
                    mm.then_inc(p.pe.h, 1)
            blk.tensor(t_)
            p.pe.n = pe0 + 16

            def a_(a):
                for tt in range(16):
                    a.wait_ge(p.pe.h, pe0 + tt + 1)
                    a.activation(wtok[:, tt * 32:(tt + 1) * 32], psr[tt % 2][:, 0:32],
                                 AF.Identity,
                                 scale=rstdc[:, tt:tt + 1]).then_inc(p.ac.h, 1)
            blk.scalar(a_)
            p.ac.n = ac0 + 16
        with p.phase() as blk:
            def v_(v):
                for tt in range(16):
                    v.tensor_add(wtok[:, tt * 32:(tt + 1) * 32],
                                 wtok[:, tt * 32:(tt + 1) * 32],
                                 rbb[:, :]).then_inc(p.dv.h, 1)
                    p.dv.n += 1
                    v.wait_ge(p.dv.h, p.dv.n)
            blk.vector(v_)
    with p.scope():
        with p.phase() as blk:
            mcol = p.sbuf("mcol", [128, 64], F32)
            ebuf = p.sbuf("ebuf", [128, 512], F32)
            emask = p.sbuf("emask", [128, 512], F32)
            sel = p.sbuf("selw", [128, 512], F32)
            dv0 = p.dv.n

            def lg(tt):
                return wtok[:, tt * 32:(tt + 1) * 32]

            def eb(tt):
                return ebuf[:, tt * 32:(tt + 1) * 32]

            def em(tt):
                return emask[:, tt * 32:(tt + 1) * 32]

            def sl(tt):
                return sel[:, tt * 32:(tt + 1) * 32]

            def m(tt, j):
                return mcol[:, tt * 4 + j:tt * 4 + j + 1]

            def dv(v):
                def batch(fn):
                    last = None
                    for tt in range(16):
                        last = fn(tt)
                    last.then_inc(p.dv.h, 1)
                    p.dv.n += 1
                    v.wait_ge(p.dv.h, p.dv.n)
                batch(lambda tt: v.reduce_max(m(tt, 0), lg(tt), axis=AX.X))
                batch(lambda tt: v.tensor_scalar_mul(m(tt, 1), m(tt, 0), -1.0))
                p.ac.n += 1
                v.wait_ge(p.ac.h, p.ac.n)
                batch(lambda tt: v.tensor_copy(em(tt), eb(tt)))
                for k in range(3):
                    batch(lambda tt: v.reduce_max(m(tt, 2), em(tt), axis=AX.X))
                    batch(lambda tt: v.tensor_scalar(sl(tt), em(tt), m(tt, 2),
                                                     None, op0=ALU.is_ge))
                    batch(lambda tt: v.scalar_tensor_tensor(
                        em(tt), sl(tt), -1e9, em(tt),
                        op0=ALU.mult, op1=ALU.add))
                batch(lambda tt: v.reduce_max(m(tt, 2), em(tt), axis=AX.X))
                batch(lambda tt: v.tensor_scalar(sl(tt), eb(tt), m(tt, 2),
                                                 None, op0=ALU.is_ge))
                batch(lambda tt: v.tensor_mul(sl(tt), sl(tt), eb(tt)))
                batch(lambda tt: v.reduce_sum(m(tt, 3), sl(tt), axis=AX.X))
                batch(lambda tt: v.tensor_scalar_max(m(tt, 3), m(tt, 3), 1e-12))
                batch(lambda tt: v.reciprocal(m(tt, 3), m(tt, 3)))
                batch(lambda tt: v.tensor_scalar_mul(lg(tt), sl(tt), m(tt, 3)))

            def ac_(a):
                a.wait_ge(p.dv.h, dv0 + 2)
                last = None
                for tt in range(16):
                    last = a.activation(eb(tt), lg(tt), AF.Exp,
                                        bias=m(tt, 1))
                last.then_inc(p.ac.h, 1)
            blk.scalar(ac_)
            blk.vector(dv)
        with p.phase() as blk:
            wtT = p.sbuf("wtT", [128, 32], F32)

            ds0v = p.ds.n

            def v_(v):
                for tt in range(16):
                    if tt >= 1:
                        v.wait_ge(p.ds.h, ds0v + 64 * tt)
                    v.transpose(wtT[:, :], wtok[:, tt * 32:(tt + 1) * 32]) \
                        .then_inc(p.dv.h, 1)
                    p.dv.n += 1
            blk.vector(v_)

            def s_(s):
                for tt in range(16):
                    s.wait_ge(p.dv.h, p.dv.n - 16 + tt + 1)
                    for a in range(4):
                        src_ap = wtT[32 * a:32 * a + 32, 0:32]
                        dst_ap = bass.AP(wscr, tt * 128 + 32 * a,
                                         [[TCORE, 32], [1, 32]])
                        s.dma_start(out=dst_ap, in_=src_ap).then_inc(p.ds.h, 16)
                        p.ds.n += 16
                s.wait_ge(p.ds.h, p.ds.n)
                s.dma_start(out=wrows[:, :], in_=wscr[:, :]).then_inc(p.ds.h, 16)
                p.ds.n += 16
                s.wait_ge(p.ds.h, p.ds.n)
            blk.scalar(s_)

    with p.scope():
        sgw = None
        with p.phase() as blk:
            sgw = load_weight(p, blk, sgateT, D, 512, "sgw")
            suw = load_weight(p, blk, supT, D, 512, "suw")
        sg = p.sbuf("sg", [128, 4 * TCORE], BF16)
        su = p.sbuf("su", [128, 4 * TCORE], BF16)
        with p.phase() as blk:
            emit_proj(p, blk, sgw, 512, xl, TCORE, act_epi(sg, TCORE, func=AF.Silu))
        with p.phase() as blk:
            emit_proj(p, blk, suw, 512, xl, TCORE, act_epi(su, TCORE, func=AF.Copy))
        with p.phase() as blk:
            emit_mul_pass(p, blk, sg, sg, su, 8, 1024)
        with p.phase() as blk:
            sdw = load_weight(p, blk, sdownT, 512, D, "sdw")
        with p.phase() as blk:
            emit_proj(p, blk, sdw, D, sg, TCORE, act_epi(y, TCORE, func=AF.Copy),
                      nk=4, ttile=512)
    with p.scope():
        gbuf = p.sbuf("gbuf", [128, 16 * 512], BF16)
        ubuf = p.sbuf("ubuf", [128, 16 * 512], BF16)
        y2 = p.sbuf("y2", [128, 8 * 512], BF16)
        with p.phase() as blk:
            for q in range(4):
                q0 = q * 512
                for eh in range(2):
                    emit_proj_stream(p, blk, gateTt, D, 4096, xl, 512,
                                     act_epi(gbuf, 512, func=AF.Silu,
                                             out_col0=-eh * 16 * 512),
                                     G=2, x_col0=q0, x_stride=TCORE,
                                     g0=eh * 8, g1=eh * 8 + 8)
                    emit_proj_stream(p, blk, upTt, D, 4096, xl, 512,
                                     act_epi(ubuf, 512, func=AF.Copy,
                                             out_col0=-eh * 16 * 512),
                                     G=2, x_col0=q0, x_stride=TCORE,
                                     g0=eh * 8, g1=eh * 8 + 8)
                    emit_mul_pass(p, blk, gbuf, gbuf, ubuf, 8, 1024)
                    psw = [p.psum(f"wb{i}", [128, 512], F32) for i in range(2)]
                    pe0, dv0 = p.pe.n, p.dv.n

                    def t_(e, eh=eh, q0=q0, psw=psw, pe0=pe0, dv0=dv0):
                        for ei in range(16):
                            eg = eh * 16 + ei
                            e.wait_ge(p.dv.h, max(0, dv0 + ei - 1))
                            mm = nc.tensor.matmul(
                                psw[ei % 2][:, :],
                                bass.AP(c["ident"], eg, [[128, 32], [0, 128]]),
                                wrows[:, q0:q0 + 512], start=True, stop=True)
                            mm.then_inc(p.pe.h, 1)
                    blk.tensor(t_)
                    p.pe.n = pe0 + 16

                    def v_(v, psw=psw, pe0=pe0):
                        for ei in range(16):
                            v.wait_ge(p.pe.h, pe0 + ei + 1)
                            v.tensor_mul(gbuf[:, ei * 512:(ei + 1) * 512],
                                         gbuf[:, ei * 512:(ei + 1) * 512],
                                         psw[ei % 2][:, :]).then_inc(p.dv.h, 1)
                    blk.vector(v_)
                    p.dv.n = dv0 + 16
                    emit_proj_stream(p, blk, downTt, 4096, D, gbuf, 512,
                                     act_epi(y2, 512, func=AF.Copy), G=2,
                                     kc_lo=eh * 16, kc_hi=eh * 16 + 16,
                                     x_kc_base=eh * 16,
                                     x_waits=[(p.dv.h, p.dv.n)])
                    ac_prev = p.ac.n

                    def v2_(v, q0=q0, ac_prev=ac_prev):
                        v.wait_ge(p.ac.h, ac_prev)
                        for kc in range(8):
                            v.tensor_add(y[:, kc * TCORE + q0:kc * TCORE + q0 + 512],
                                         y[:, kc * TCORE + q0:kc * TCORE + q0 + 512],
                                         y2[:, kc * 512:(kc + 1) * 512]) \
                                .then_inc(p.dv.h, 1)
                            p.dv.n += 1
                            v.wait_ge(p.dv.h, p.dv.n)
                    blk.vector(v2_)
    with p.scope():
        emit_postln_residual_streamed(p, io, y, c, row)
    return p.finish()


def host_prep_l3ffn(H, inputs):
    f32 = np.float32
    H["l3f_routerT"] = np.ascontiguousarray(
        np.asarray(inputs["L3_ffn_router_w"], f32).T)
    H["l3f_rbias"] = np.asarray(inputs["L3_ffn_router_bias"], f32).reshape(1, 32)
    gw = np.asarray(inputs["L3_ffn_gate_w"], f32).reshape(4096, 1024)
    uw = np.asarray(inputs["L3_ffn_up_w"], f32).reshape(4096, 1024)
    dw = np.asarray(inputs["L3_ffn_down_w"], f32).transpose(0, 2, 1).reshape(4096, 1024)
    H["l3f_gateTt"] = host_tile_weight(np.ascontiguousarray(gw.T)).astype(BF)
    H["l3f_upTt"] = host_tile_weight(np.ascontiguousarray(uw.T)).astype(BF)
    H["l3f_downTt"] = host_tile_weight(dw).astype(BF)
    H["l3f_sgateT"] = np.ascontiguousarray(
        np.asarray(inputs["L3_ffn_sgate_w"], f32).T).astype(BF)
    H["l3f_supT"] = np.ascontiguousarray(
        np.asarray(inputs["L3_ffn_sup_w"], f32).T).astype(BF)
    H["l3f_sdownT"] = np.ascontiguousarray(
        np.asarray(inputs["L3_ffn_sdown_w"], f32).T).astype(BF)
    return H


def ins_l3_ffn(cx, H, inputs, state):
    return {"xin": state["stream"][cx], "ident_f32": H["ident_f32"],
            "ones1_f32": H["ones1_f32"], "constv": H["constv"],
            "routerT": H["l3f_routerT"], "rbias": H["l3f_rbias"],
            "gateTt": H["l3f_gateTt"], "upTt": H["l3f_upTt"],
            "downTt": H["l3f_downTt"], "sgateT": H["l3f_sgateT"],
            "supT": H["l3f_supT"], "sdownT": H["l3f_sdownT"]}




PADT = 1028

def build_l2_attn():
    p = Prog()
    nc = p.nc
    io = stream_io_tensors(p)
    W_inxT = nc.dram_tensor("W_inxT", [D, D], F16, kind="ExternalInput")
    W_inzT = nc.dram_tensor("W_inzT", [D, D], F16, kind="ExternalInput")
    wg = {nm: nc.dram_tensor(nm, [D, D], F16, kind="ExternalInput")
          for nm in ["WaT", "WbT", "WcT", "WdT", "WoutT"]}
    bcols = nc.dram_tensor("bcols", [128, 40], F32, kind="ExternalInput")
    binx = nc.dram_tensor("binx", [128, 8], F32, kind="ExternalInput")
    binz = nc.dram_tensor("binz", [128, 8], F32, kind="ExternalInput")
    gcol = nc.dram_tensor("gcol", [128, 8], F32, kind="ExternalInput")
    lbc = nc.dram_tensor("lbc", [128, 8], F32, kind="ExternalInput")
    ckc = nc.dram_tensor("ckc", [128, 40], F32, kind="ExternalInput")
    cbc = nc.dram_tensor("cbc", [128, 8], F32, kind="ExternalInput")
    selh = nc.dram_tensor("selh", [32, 8], F32, kind="ExternalInput")
    selc = nc.dram_tensor("selc", [128, 16], F32, kind="ExternalInput")
    hin = nc.dram_tensor("hin", [8, 1025], F32)
    hg = nc.dram_tensor("hg", [32, 1025], F32)
    sumin = nc.dram_tensor("sumin", [128, 32], F32)
    sumg = nc.dram_tensor("sumg", [512, 32], F32)
    zscr = nc.dram_tensor("zscr", [D, TCORE], F16)

    row = p.sbuf("row", [1, 3 * TCORE], F32)
    xp = p.sbuf("xp", [128, 8 * 2 * PADT], F16)
    sumsb = p.sbuf("sumsb", [128, 32], F32)
    carry = p.sbuf("carry", [128, 16], F32)
    bxcol = p.sbuf("bxcol", [128, 8], F32)
    bzcol = p.sbuf("bzcol", [128, 8], F32)

    with p.phase() as blk:
        c = load_consts(p, blk, io, want_f16=True)
        sb = {}
        def _(g):
            for nm, t, shape in [("bcols", bcols, [128, 40]),
                                 ("binx", binx, [128, 8]),
                                 ("binz", binz, [128, 8]),
                                 ("gcol", gcol, [128, 8]),
                                 ("lbc", lbc, [128, 8]),
                                 ("ckc", ckc, [128, 40]),
                                 ("cbc", cbc, [128, 8]),
                                 ("selc", selc, [128, 16])]:
                sb[nm] = p.sbuf(nm + "s", shape, F32)
                dma(g, p.da, sb[nm][:, :], t[:, :])
            sb["selh"] = p.sbuf("selhs", [32, 8], F32)
            dma(g, p.da, sb["selh"][:, :], selh[:, :])
            dma_wait_all(g, p.da)
        blk.gpsimd(_)
        sb["lbch"] = p.sbuf("lbch", [128, 8], F16)
        def v16(v):
            v.wait_ge(p.da.h, p.da.n)
            v.tensor_copy(sb["lbch"][:, :], sb["lbc"][:, :])
        blk.vector(v16)

    with p.scope():
        xg = p.sbuf("xg", [128, 8 * TCORE], F16)
        with p.phase() as blk:
            emit_load_xl(p, blk, io, xg)
        with p.phase() as blk:
            emit_sumsq_row(p, blk, xg, TCORE, c["ident"], row, 0)
        with p.phase() as blk:
            def s_(a):
                a.activation(row[0:1, TCORE:2 * TCORE], row[0:1, 0:TCORE],
                             AF.Ln, bias=EPS, scale=1.0 / D).then_inc(p.ac.h, 1)
                p.ac.n += 1
                a.wait_ge(p.ac.h, p.ac.n)
                a.activation(row[0:1, TCORE:2 * TCORE], row[0:1, TCORE:2 * TCORE],
                             AF.Exp, scale=-0.5).then_inc(p.ac.h, 1)
                p.ac.n += 1
            blk.scalar(s_)
            def v_(v):
                v.wait_ge(p.ac.h, p.ac.n)
                v.tensor_mul(row[0:1, 2 * TCORE:3 * TCORE],
                             row[0:1, TCORE:2 * TCORE],
                             row[0:1, TCORE:2 * TCORE]).then_inc(p.dv.h, 1)
                p.dv.n += 1
                v.wait_ge(p.dv.h, p.dv.n)
                v.tensor_mul(row[0:1, 2 * TCORE:3 * TCORE],
                             row[0:1, 2 * TCORE:3 * TCORE],
                             row[0:1, 0:TCORE]).then_inc(p.dv.h, 1)
                p.dv.n += 1
            blk.vector(v_)
            def s2_(a):
                a.wait_ge(p.dv.h, p.dv.n)
                a.activation(row[0:1, 2 * TCORE:3 * TCORE],
                             row[0:1, 2 * TCORE:3 * TCORE],
                             AF.Ln, bias=EPS, scale=1.0 / D).then_inc(p.ac.h, 1)
                p.ac.n += 1
                a.wait_ge(p.ac.h, p.ac.n)
                a.activation(row[0:1, 2 * TCORE:3 * TCORE],
                             row[0:1, 2 * TCORE:3 * TCORE],
                             AF.Exp, scale=-0.5).then_inc(p.ac.h, 1)
                p.ac.n += 1
            blk.scalar(s2_)
            def v2_(v):
                v.wait_ge(p.ac.h, p.ac.n)
                v.tensor_mul(row[0:1, 2 * TCORE:3 * TCORE],
                             row[0:1, 2 * TCORE:3 * TCORE],
                             row[0:1, TCORE:2 * TCORE]).then_inc(p.dv.h, 1)
                p.dv.n += 1
            blk.vector(v2_)
        with p.phase() as blk:
            def _(g):
                for j, (s, w) in enumerate([(0, 0), (0, 1), (0, 1022), (0, 1023),
                                            (1, 0), (1, 1), (1, 1022), (1, 1023)]):
                    tcol = s * CHT + w
                    g.dma_start(
                        out=bass.AP(hin, j * 1025, [[1025, 1], [1, 1024]]),
                        in_=bass.AP(io["xin"], tcol, [[1, 1], [TCORE, 1024]]),
                        allow_slow_non_contiguous=True).then_inc(p.da.h, 16)
                    p.da.n += 16
                dma_wait_all(g, p.da)
            blk.gpsimd(_)
            def s_(a):
                a.wait_ge(p.ac.h, p.ac.n)
                a.wait_ge(p.dv.h, p.dv.n)
                for j, (s, w) in enumerate([(0, 0), (0, 1), (0, 1022), (0, 1023),
                                            (1, 0), (1, 1), (1, 1022), (1, 1023)]):
                    tcol = s * CHT + w
                    a.dma_start(out=bass.AP(hin, j * 1025 + 1024, [[1, 1], [1, 1]]),
                                in_=row[0:1, tcol:tcol + 1]).then_inc(p.ds.h, 16)
                    p.ds.n += 16
                a.wait_ge(p.ds.h, p.ds.n)
            blk.scalar(s_)
        with p.phase() as blk:
            def _(g):
                g.collective_compute("AllGather", ALU.bypass, replica_groups=GROUPS,
                                     ins=[hin.ap().opt()], outs=[hg.ap().opt()]
                                     ).then_inc(p.cc.h, 1)
                p.cc.n += 1
                g.wait_ge(p.cc.h, p.cc.n)
            blk.gpsimd(_)
        with p.scope():
          with p.phase() as blk:
            pb = [p.psum(f"lnb{i}", [128, 512], F32) for i in range(2)]
            pe0, dv0 = p.pe.n, p.dv.n
            def t_(e):
                for si in range(4):
                    if si >= 2:
                        e.wait_ge(p.dv.h, dv0 + si - 1)
                    mm = nc.tensor.matmul(pb[si % 2][:, :], c["ones1"][0:1, :],
                                          row[0:1, 2 * TCORE + si * 512:
                                              2 * TCORE + si * 512 + 512],
                                          start=True, stop=True)
                    mm.then_inc(p.pe.h, 1)
            blk.tensor(t_)
            p.pe.n = pe0 + 4
            def v_(v):
                for si in range(4):
                    v.wait_ge(p.pe.h, pe0 + si + 1)
                    last = None
                    for kc in range(8):
                        sl = slice(kc * TCORE + si * 512, kc * TCORE + si * 512 + 512)
                        last = v.scalar_tensor_tensor(
                            xg[:, sl], xg[:, sl], sb["gcol"][:, kc:kc + 1],
                            pb[si % 2][:, :], op0=ALU.mult, op1=ALU.mult)
                    last.then_inc(p.dv.h, 1)
            blk.vector(v_)
            p.dv.n = dv0 + 4
        with p.scope():
            with p.phase() as blk:
                winx = load_weight(p, blk, W_inxT, D, D, "winx", dtype=F16)
            with p.phase() as blk:
                emit_bias_cols(p, blk, winx, D, sb["lbch"], sb["binx"], bxcol)
            for slot in range(2):
                with p.phase() as blk:
                    emit_proj(p, blk, winx, D, xg, TCORE,
                              act_epi(xp, 2 * PADT, bias=bxcol,
                                      out_col0=slot * PADT + 2),
                              x_col0=slot * CHT, ttile=512, tokens=CHT)
        with p.scope():
            with p.phase() as blk:
                winz = load_weight(p, blk, W_inzT, D, D, "winz", dtype=F16)
            with p.phase() as blk:
                emit_bias_cols(p, blk, winz, D, sb["lbch"], sb["binz"], bzcol)
            zst = p.sbuf("zst", [128, 8 * 512], F16)
            for q in range(4):
                q0 = q * 512
                with p.phase() as blk:
                    emit_proj(p, blk, winz, D, xg, 512,
                              act_epi(zst, 512, func=AF.Sigmoid, bias=bzcol),
                              x_col0=q0, x_stride=TCORE, ttile=512, tokens=512)
                with p.phase() as blk:
                    def _(g, q0=q0):
                        dma(g, p.da,
                            bass.AP(zscr, q0, [[TCORE, 128], [128 * TCORE, 8],
                                               [1, 512]]),
                            zst[:, :])
                        dma_wait_all(g, p.da)
                    blk.gpsimd(_)
        with p.scope():
            hgs = p.sbuf("hgs", [32, 1025], F32)
            hxg = p.sbuf("hxg", [128, 8 * 8], F16)
            hrr = p.sbuf("hrr", [1, 16], F32)
            hbc = p.sbuf("hbc", [128, 8], F32)
            with p.phase() as blk:
                def _(g):
                    dma(g, p.da, hgs[:, :], hg.ap())
                    dma_wait_all(g, p.da)
                blk.gpsimd(_)
            with p.phase() as blk:
                psr = p.psum("hpr", [1, 512], F32)
                pe0, ac0, dv0 = p.pe.n, p.ac.n, p.dv.n
                def t_(e):
                    mm = nc.tensor.matmul(psr[:, 0:8], hgs[0:32, 1024:1025],
                                          sb["selh"][:, :], start=True, stop=True)
                    mm.then_inc(p.pe.h, 1)
                blk.tensor(t_)
                p.pe.n = pe0 + 1
                def s_(a):
                    a.wait_ge(p.pe.h, pe0 + 1)
                    a.activation(hrr[0:1, 0:8], psr[0:1, 0:8], AF.Ln,
                                 bias=EPS, scale=1.0 / D).then_inc(p.ac.h, 1)
                    p.ac.n += 1
                    a.wait_ge(p.ac.h, p.ac.n)
                    a.activation(hrr[0:1, 0:8], hrr[0:1, 0:8], AF.Exp,
                                 scale=-0.5).then_inc(p.ac.h, 1)
                    p.ac.n += 1
                blk.scalar(s_)
                def v_(v):
                    v.wait_ge(p.ac.h, ac0 + 2)
                    v.tensor_mul(hrr[0:1, 8:16], hrr[0:1, 0:8],
                                 hrr[0:1, 0:8]).then_inc(p.dv.h, 1)
                    p.dv.n += 1
                    v.wait_ge(p.dv.h, p.dv.n)
                    v.tensor_mul(hrr[0:1, 8:16], hrr[0:1, 8:16],
                                 psr[0:1, 0:8]).then_inc(p.dv.h, 1)
                    p.dv.n += 1
                blk.vector(v_)
                def s2_(a):
                    a.wait_ge(p.dv.h, dv0 + 2)
                    a.activation(hrr[0:1, 8:16], hrr[0:1, 8:16], AF.Ln,
                                 bias=EPS, scale=1.0 / D).then_inc(p.ac.h, 1)
                    p.ac.n += 1
                    a.wait_ge(p.ac.h, p.ac.n)
                    a.activation(hrr[0:1, 8:16], hrr[0:1, 8:16], AF.Exp,
                                 scale=-0.5).then_inc(p.ac.h, 1)
                    p.ac.n += 1
                blk.scalar(s2_)
                def v2_(v):
                    v.wait_ge(p.ac.h, ac0 + 4)
                    v.tensor_mul(hrr[0:1, 8:16], hrr[0:1, 8:16],
                                 hrr[0:1, 0:8]).then_inc(p.dv.h, 1)
                    p.dv.n += 1
                blk.vector(v2_)
            with p.phase() as blk:
                pss = p.psum("hps", [128, 512], F32)
                prr = p.psum("hprr", [128, 512], F32)
                srr = p.sbuf("hsrr", [128, 8], F32)
                pe0, dv0 = p.pe.n, p.dv.n
                def t_(e):
                    mm = nc.tensor.matmul(prr[:, 0:8], c["ones1"][0:1, :],
                                          hrr[0:1, 8:16], start=True, stop=True)
                    mm.then_inc(p.pe.h, 1)
                blk.tensor(t_)
                p.pe.n = pe0 + 1
                def v0_(v):
                    v.wait_ge(p.pe.h, pe0 + 1)
                    v.tensor_copy(srr[:, :], prr[:, 0:8]).then_inc(p.dv.h, 1)
                    p.dv.n += 1
                blk.vector(v0_)
                pe1 = p.pe.n
                def t2_(e):
                    for dc in range(8):
                        if dc >= 1:
                            e.wait_ge(p.dv.h, p.dv.n + dc)
                        mm = nc.tensor.matmul(pss[:, 0:8],
                                              hgs[0:32, dc * 128:(dc + 1) * 128],
                                              sb["selh"][:, :],
                                              start=True, stop=True)
                        mm.then_inc(p.pe.h, 1)
                blk.tensor(t2_)
                p.pe.n = pe1 + 8
                def v2_(v):
                    v.wait_ge(p.dv.h, p.dv.n)
                    for dc in range(8):
                        v.wait_ge(p.pe.h, pe1 + dc + 1)
                        v.scalar_tensor_tensor(
                            hxg[:, dc * 8:(dc + 1) * 8], pss[:, 0:8],
                            sb["gcol"][:, dc:dc + 1], srr[:, :],
                            op0=ALU.mult, op1=ALU.mult).then_inc(p.dv.h, 1)
                blk.vector(v2_)
                p.dv.n += 8
            with p.phase() as blk:
                winx2 = load_weight(p, blk, W_inxT, D, D, "winx", dtype=F16)
            with p.phase() as blk:
                psh = [p.psum(f"hpp{i}", [128, 512], F32) for i in range(2)]
                pe0, ac0 = p.pe.n, p.ac.n
                def t_(e):
                    for fc in range(8):
                        if fc >= 2:
                            e.wait_ge(p.ac.h, ac0 + (fc - 2) * 4 + 4)
                        for kc in range(8):
                            mm = nc.tensor.matmul(
                                psh[fc % 2][:, 0:8],
                                winx2[:, kc * D + fc * 128: kc * D + fc * 128 + 128],
                                hxg[:, kc * 8:(kc + 1) * 8],
                                start=(kc == 0), stop=(kc == 7))
                        mm.then_inc(p.pe.h, 1)
                blk.tensor(t_)
                p.pe.n = pe0 + 8
                def a_(a):
                    for fc in range(8):
                        a.wait_ge(p.pe.h, pe0 + fc + 1)
                        for jj, dst0 in enumerate([0, 1026, PADT + 0, PADT + 1026]):
                            a.activation(
                                xp[:, fc * 2 * PADT + dst0: fc * 2 * PADT + dst0 + 2],
                                psh[fc % 2][:, jj * 2:jj * 2 + 2], AF.Identity,
                                bias=bxcol[:, fc:fc + 1]).then_inc(p.ac.h, 1)
                blk.scalar(a_)
                p.ac.n = ac0 + 32

    xc = p.sbuf("xc", [128, 8 * TCORE], F16)
    big1 = p.sbuf("big1", [128, 8 * TCORE], F16)
    ubuf = p.sbuf("ubuf", [128, 8 * TCORE], F16)
    with p.scope():
        dgk = p.sbuf("dgk", [128, 40 * 128], F16)
        with p.phase() as blk:
            def v_(v):
                for i in range(40):
                    v.tensor_scalar_mul(dgk[:, i * 128:(i + 1) * 128],
                                        c["ident"][:, :], sb["ckc"][:, i:i + 1]) \
                        .then_inc(p.dv.h, 1)
                    p.dv.n += 1
            blk.vector(v_)
        with p.phase() as blk:
            psc = [p.psum(f"cps{i}", [128, 512], F32) for i in range(2)]
            pe0, ac0 = p.pe.n, p.ac.n
            i = 0
            def t_(e):
                ii = 0
                for dc in range(8):
                    for slot in range(2):
                        for th in range(2):
                            if ii >= 2:
                                e.wait_ge(p.ac.h, ac0 + ii - 1)
                            base = dc * 2 * PADT + slot * PADT + 2 + th * 512
                            for tap in range(5):
                                mm = nc.tensor.matmul(
                                    psc[ii % 2][:, :],
                                    dgk[:, (dc * 5 + tap) * 128:
                                        (dc * 5 + tap) * 128 + 128],
                                    xp[:, base + tap - 2: base + tap - 2 + 512],
                                    start=(tap == 0), stop=(tap == 4))
                            mm.then_inc(p.pe.h, 1)
                            ii += 1
            blk.tensor(t_)
            p.pe.n = pe0 + 32
            def a_(a):
                ii = 0
                for dc in range(8):
                    for slot in range(2):
                        for th in range(2):
                            a.wait_ge(p.pe.h, pe0 + ii + 1)
                            o0 = dc * TCORE + slot * CHT + th * 512
                            a.activation(xc[:, o0:o0 + 512], psc[ii % 2][:, :],
                                         AF.Identity,
                                         bias=sb["cbc"][:, dc:dc + 1]) \
                                .then_inc(p.ac.h, 1)
                            ii += 1
            blk.scalar(a_)
            p.ac.n = ac0 + 32

    with p.scope():
        with p.phase() as blk:
            wa = load_weight(p, blk, wg["WaT"], D, D, "wgt", dtype=F16)
        with p.phase() as blk:
            emit_proj(p, blk, wa, D, xc, TCORE,
                      act_epi(big1, TCORE, func=AF.Tanh,
                              bias=slice_cols(sb["bcols"], 0)))
    with p.scope():
        with p.phase() as blk:
            wb = load_weight(p, blk, wg["WbT"], D, D, "wgt", dtype=F16)
        with p.phase() as blk:
            emit_proj(p, blk, wb, D, xc, TCORE,
                      act_epi(ubuf, TCORE, func=AF.Sigmoid,
                              bias=slice_cols(sb["bcols"], 1)))
        with p.phase() as blk:
            def v_(v):
                for dc in range(8):
                    for slot in range(2):
                        v.tensor_mul(
                            ubuf[:, dc * TCORE + slot * CHT:
                                 dc * TCORE + slot * CHT + CHT],
                            ubuf[:, dc * TCORE + slot * CHT:
                                 dc * TCORE + slot * CHT + CHT],
                            xp[:, dc * 2 * PADT + slot * PADT + 2:
                               dc * 2 * PADT + slot * PADT + 2 + CHT]) \
                            .then_inc(p.dv.h, 1)
                        p.dv.n += 1
                        v.wait_ge(p.dv.h, p.dv.n)
            blk.vector(v_)

    onesb = p.sbuf("onesb", [128, CHT], F32)
    scn = p.sbuf("scn", [128, CHT], F32)
    with p.phase() as blk:
        def v_(v):
            v.memset(onesb[:, :], 1.0).then_inc(p.dv.h, 1)
            p.dv.n += 1
            for dc in range(8):
                for slot in range(2):
                    a_sl = big1[:, dc * TCORE + slot * CHT:
                                dc * TCORE + slot * CHT + CHT]
                    u_sl = ubuf[:, dc * TCORE + slot * CHT:
                                dc * TCORE + slot * CHT + CHT]
                    v.wait_ge(p.dv.h, p.dv.n)
                    v.tensor_tensor_scan(scn[:, :], a_sl, onesb[:, :], 1.0,
                                         op0=ALU.mult, op1=ALU.mult) \
                        .then_inc(p.dv.h, 1)
                    p.dv.n += 1
                    v.wait_ge(p.dv.h, p.dv.n)
                    v.tensor_copy(sumsb[:, slot * 16 + dc:slot * 16 + dc + 1],
                                  scn[:, CHT - 1:CHT]).then_inc(p.dv.h, 1)
                    p.dv.n += 1
                    v.wait_ge(p.dv.h, p.dv.n)
                    v.tensor_tensor_scan(scn[:, :], a_sl, u_sl, 0.0,
                                         op0=ALU.mult, op1=ALU.add) \
                        .then_inc(p.dv.h, 1)
                    p.dv.n += 1
                    v.wait_ge(p.dv.h, p.dv.n)
                    v.tensor_copy(sumsb[:, slot * 16 + 8 + dc:slot * 16 + 8 + dc + 1],
                                  scn[:, CHT - 1:CHT]).then_inc(p.dv.h, 1)
                    p.dv.n += 1
        blk.vector(v_)
    with p.phase() as blk:
        def _(g):
            dma(g, p.da, sumin.ap(), sumsb[:, :])
            dma_wait_all(g, p.da)
            g.collective_compute("AllGather", ALU.bypass, replica_groups=GROUPS,
                                 ins=[sumin.ap().opt()], outs=[sumg.ap().opt()]
                                 ).then_inc(p.cc.h, 1)
            p.cc.n += 1
            g.wait_ge(p.cc.h, p.cc.n)
        blk.gpsimd(_)
    with p.phase() as blk:
        gsum = p.sbuf("gsum", [128, 4 * 32], F32)
        car_j = p.sbuf("car_j", [128, 8], F32)
        def _(g):
            for r in range(4):
                dma(g, p.da, gsum[:, r * 32:(r + 1) * 32],
                    sumg.ap()[r * 128:(r + 1) * 128, :])
            dma_wait_all(g, p.da)
        blk.gpsimd(_)
        def v_(v):
            v.wait_ge(p.da.h, p.da.n)
            v.memset(car_j[:, :], 0.0).then_inc(p.dv.h, 1)
            p.dv.n += 1
            v.wait_ge(p.dv.h, p.dv.n)
            v.memset(carry[:, :], 0.0).then_inc(p.dv.h, 1)
            p.dv.n += 1
            v.wait_ge(p.dv.h, p.dv.n)
            for j in range(8):
                r, s = owner_of_chunk(j)
                A_j = gsum[:, r * 32 + s * 16:r * 32 + s * 16 + 8]
                s_j = gsum[:, r * 32 + s * 16 + 8:r * 32 + s * 16 + 16]
                for slot in range(2):
                    v.scalar_tensor_tensor(
                        carry[:, slot * 8:(slot + 1) * 8], car_j[:, :],
                        sb["selc"][:, slot * 8 + j:slot * 8 + j + 1],
                        carry[:, slot * 8:(slot + 1) * 8],
                        op0=ALU.mult, op1=ALU.add).then_inc(p.dv.h, 1)
                    p.dv.n += 1
                    v.wait_ge(p.dv.h, p.dv.n)
                if j < 7:
                    v.tensor_mul(car_j[:, :], car_j[:, :], A_j).then_inc(p.dv.h, 1)
                    p.dv.n += 1
                    v.wait_ge(p.dv.h, p.dv.n)
                    v.tensor_add(car_j[:, :], car_j[:, :], s_j).then_inc(p.dv.h, 1)
                    p.dv.n += 1
                    v.wait_ge(p.dv.h, p.dv.n)
        blk.vector(v_)
    with p.phase() as blk:
        def v_(v):
            for dc in range(8):
                for slot in range(2):
                    a_sl = big1[:, dc * TCORE + slot * CHT:
                                dc * TCORE + slot * CHT + CHT]
                    u_sl = ubuf[:, dc * TCORE + slot * CHT:
                                dc * TCORE + slot * CHT + CHT]
                    v.tensor_tensor_scan(u_sl, a_sl, u_sl,
                                         carry[:, slot * 8 + dc:slot * 8 + dc + 1],
                                         op0=ALU.mult, op1=ALU.add) \
                        .then_inc(p.dv.h, 1)
                    p.dv.n += 1
                    v.wait_ge(p.dv.h, p.dv.n)
        blk.vector(v_)

    with p.scope():
        with p.phase() as blk:
            wc = load_weight(p, blk, wg["WcT"], D, D, "wgt", dtype=F16)
        with p.phase() as blk:
            emit_proj(p, blk, wc, D, xc, TCORE,
                      act_epi(big1, TCORE, func=AF.Sigmoid,
                              bias=slice_cols(sb["bcols"], 2)))
        with p.phase() as blk:
            emit_mul_pass(p, blk, big1, big1, ubuf, 16, 1024)
    with p.scope():
        with p.phase() as blk:
            wdt = load_weight(p, blk, wg["WdT"], D, D, "wgt", dtype=F16)
        with p.phase() as blk:
            emit_proj(p, blk, wdt, D, xc, TCORE,
                      act_epi(ubuf, TCORE, func=AF.Identity,
                              bias=slice_cols(sb["bcols"], 3)))
        with p.phase() as blk:
            def v_(v):
                for dc in range(8):
                    for slot in range(2):
                        sl = slice(dc * TCORE + slot * CHT,
                                   dc * TCORE + slot * CHT + CHT)
                        v.tensor_mul(ubuf[:, sl], ubuf[:, sl],
                                     xp[:, dc * 2 * PADT + slot * PADT + 2:
                                        dc * 2 * PADT + slot * PADT + 2 + CHT]) \
                            .then_inc(p.dv.h, 1)
                        p.dv.n += 1
                        v.wait_ge(p.dv.h, p.dv.n)
                        v.tensor_add(big1[:, sl], big1[:, sl], ubuf[:, sl]) \
                            .then_inc(p.dv.h, 1)
                        p.dv.n += 1
                        v.wait_ge(p.dv.h, p.dv.n)
            blk.vector(v_)
    with p.phase() as blk:
        zt = p.sbuf("zt", [128, 8 * 512], F16)
        def _(g):
            for q in range(4):
                q0 = q * 512
                if q >= 1:
                    g.wait_ge(p.dv.h, p.dv.n + q * 8)
                dma(g, p.da, zt[:, :],
                    bass.AP(zscr, q0, [[TCORE, 128], [128 * TCORE, 8], [1, 512]]))
        blk.gpsimd(_)
        da0 = p.da.n - 4 * 16
        def v_(v):
            for q in range(4):
                q0 = q * 512
                v.wait_ge(p.da.h, da0 + (q + 1) * 16)
                for kc in range(8):
                    v.tensor_mul(xc[:, kc * TCORE + q0:kc * TCORE + q0 + 512],
                                 big1[:, kc * TCORE + q0:kc * TCORE + q0 + 512],
                                 zt[:, kc * 512:(kc + 1) * 512]) \
                        .then_inc(p.dv.h, 1)
                    p.dv.n += 1
                    v.wait_ge(p.dv.h, p.dv.n)
        blk.vector(v_)
    with p.scope():
        with p.phase() as blk:
            wo = load_weight(p, blk, wg["WoutT"], D, D, "wgt", dtype=F16)
        with p.phase() as blk:
            emit_proj(p, blk, wo, D, xc, TCORE,
                      act_epi(big1, TCORE, func=AF.Copy,
                              bias=slice_cols(sb["bcols"], 4)))
    with p.scope():
        xsq = p.sbuf("xsq2", [128, 8 * 512], F32)
        scrt = p.sbuf("plr2", [128, 512], F32)
        for q in range(4):
            q0 = q * 512
            with p.phase() as blk:
                def _(g, q0=q0):
                    dma(g, p.da, xsq[:, :],
                        io["xin"].ap()[:, q0:q0 + 512]
                        .rearrange("(kc p) t -> p kc t", p=128))
                    dma_wait_all(g, p.da)
                blk.gpsimd(_)
            with p.phase() as blk:
                prs = p.psum("prsx", [128, 512], F32)
                pe0 = p.pe.n
                def t_(e, q0=q0):
                    mm = nc.tensor.matmul(prs[:, :], c["ones1"][0:1, :],
                                          row[0:1, TCORE + q0:TCORE + q0 + 512],
                                          start=True, stop=True)
                    mm.then_inc(p.pe.h, 1)
                blk.tensor(t_)
                p.pe.n = pe0 + 1
                def v_(v, q0=q0):
                    v.wait_ge(p.pe.h, p.pe.n)
                    for kc in range(8):
                        v.tensor_mul(scrt[:, :], xsq[:, kc * 512:(kc + 1) * 512],
                                     prs[:, :]).then_inc(p.dv.h, 1)
                        p.dv.n += 1
                        v.wait_ge(p.dv.h, p.dv.n)
                        v.tensor_add(big1[:, kc * TCORE + q0:kc * TCORE + q0 + 512],
                                     big1[:, kc * TCORE + q0:kc * TCORE + q0 + 512],
                                     scrt[:, :]).then_inc(p.dv.h, 1)
                        p.dv.n += 1
                        v.wait_ge(p.dv.h, p.dv.n)
                blk.vector(v_)
    with p.scope():
        emit_postln_residual_streamed(p, io, big1, c, row,
                                      ones_dt_key="onesch")
    return p.finish()


def slice_cols(t, blkidx):
    return t[:, blkidx * 8:(blkidx + 1) * 8]


def emit_bias_cols(p, blk, w_sb, NOUT, lnb_col, bin_col, out_col):
    nc = p.nc
    psb = p.psum("bcps", [128, 512], F32)
    pe0 = p.pe.n

    def t_(e):
        for fc in range(8):
            for kc in range(8):
                mm = nc.tensor.matmul(
                    psb[:, fc:fc + 1],
                    w_sb[:, kc * NOUT + fc * 128: kc * NOUT + fc * 128 + 128],
                    lnb_col[:, kc:kc + 1],
                    start=(kc == 0), stop=(kc == 7))
        mm.then_inc(p.pe.h, 1)
    blk.tensor(t_)
    p.pe.n = pe0 + 1

    def v_(v):
        v.wait_ge(p.pe.h, p.pe.n)
        v.tensor_add(out_col[:, :], psb[:, 0:8], bin_col[:, :]).then_inc(p.dv.h, 1)
        p.dv.n += 1
    blk.vector(v_)



def host_prep_l2attn(H, inputs):
    f32 = np.float32
    H["l2a_W_inxT"] = np.ascontiguousarray(
        np.asarray(inputs["L2_attn_W_in"], f32)[:, 0:1024]).astype(np.float16)
    H["l2a_W_inzT"] = np.ascontiguousarray(
        np.asarray(inputs["L2_attn_W_in"], f32)[:, 1024:2048]).astype(np.float16)
    for nm, key in [("WaT", "L2_attn_Wa"), ("WbT", "L2_attn_Wb"),
                    ("WcT", "L2_attn_Wc"), ("WdT", "L2_attn_Wd"),
                    ("WoutT", "L2_attn_W_out")]:
        H["l2a_" + nm] = np.ascontiguousarray(
            np.asarray(inputs[key], f32)).astype(np.float16)
    def col(v):
        return np.ascontiguousarray(np.asarray(v, f32).reshape(8, 128).T)
    H["l2a_bcols"] = np.concatenate([
        col(np.asarray(inputs["L2_attn_ba"], f32) + np.asarray(inputs["L2_attn_a0"], f32)),
        col(np.asarray(inputs["L2_attn_bb"], f32) + np.asarray(inputs["L2_attn_b0"], f32)),
        col(np.asarray(inputs["L2_attn_bc"], f32) + np.asarray(inputs["L2_attn_c0"], f32)),
        col(np.asarray(inputs["L2_attn_bd"], f32) + np.asarray(inputs["L2_attn_d0"], f32)),
        col(inputs["L2_attn_b_out"])], axis=1)
    bin_ = np.asarray(inputs["L2_attn_b_in"], f32)
    H["l2a_binx"] = col(bin_[0:1024])
    H["l2a_binz"] = col(bin_[1024:2048])
    H["l2a_gcol"] = col(inputs["L2_attn_ln_g"])
    H["l2a_lbc"] = col(inputs["L2_attn_ln_b"])
    ck = np.asarray(inputs["L2_attn_conv_k"], f32)[:, 0, :]
    ckc = np.zeros((128, 40), f32)
    for dc in range(8):
        for tap in range(5):
            ckc[:, dc * 5 + tap] = ck[tap, dc * 128:(dc + 1) * 128]
    H["l2a_ckc"] = ckc
    H["l2a_cbc"] = col(inputs["L2_attn_conv_b"])
    H["l2a_selh"] = []
    H["l2a_selc"] = []
    for cx in range(NCORES):
        sel = np.zeros((32, 8), f32)
        for slot, cg in enumerate(chunks_of_core(cx)):
            if cg - 1 >= 0:
                r, s = owner_of_chunk(cg - 1)
                sel[r * 8 + s * 4 + 2, slot * 4 + 0] = 1.0
                sel[r * 8 + s * 4 + 3, slot * 4 + 1] = 1.0
            if cg + 1 < 8:
                r, s = owner_of_chunk(cg + 1)
                sel[r * 8 + s * 4 + 0, slot * 4 + 2] = 1.0
                sel[r * 8 + s * 4 + 1, slot * 4 + 3] = 1.0
        H["l2a_selh"].append(sel)
        sc_ = np.zeros((128, 16), f32)
        for slot, cg in enumerate(chunks_of_core(cx)):
            sc_[:, slot * 8 + cg] = 1.0
        H["l2a_selc"].append(sc_)
    return H


def ins_l2_attn(cx, H, inputs, state):
    m = {"xin": state["stream"][cx], "ident_f32": H["ident_f32"],
         "ones1_f32": H["ones1_f32"], "constv": H["constv"],
         "W_inxT": H["l2a_W_inxT"], "W_inzT": H["l2a_W_inzT"],
         "bcols": H["l2a_bcols"], "binx": H["l2a_binx"], "binz": H["l2a_binz"],
         "gcol": H["l2a_gcol"], "lbc": H["l2a_lbc"], "ckc": H["l2a_ckc"],
         "cbc": H["l2a_cbc"], "selh": H["l2a_selh"][cx],
         "selc": H["l2a_selc"][cx]}
    for nm in ["WaT", "WbT", "WcT", "WdT", "WoutT"]:
        m[nm] = H["l2a_" + nm]
    return m




def rope_sched(qcg, half):
    return None


def l0_scheds(g, hp_unused=None):
    out = {}
    for slot, qcg in enumerate([g, 7 - g]):
        for half in range(2):
            chunks = list(range(qcg + 1))
            stages = []
            for i in range(0, len(chunks), 2):
                chs = chunks[i:i + 2]
                kts = []
                for ci, cq in enumerate(chs):
                    if cq < qcg:
                        for kt in range(8):
                            kts.append((ci, kt, None))
                    else:
                        for kt in range(8):
                            a = kt - 4 * half
                            if kt < 4 * half:
                                kts.append((ci, kt, None))
                            elif 0 <= a <= 3:
                                kts.append((ci, kt, a))
                stages.append((chs, kts))
            out[(slot, half)] = stages
    return out


def build_l0_attn():
    p = Prog()
    nc = p.nc
    io = stream_io_tensors(p)
    wd = {nm: nc.dram_tensor(nm, [D, D], F16, kind="ExternalInput")
          for nm in ["WqT", "WkT", "WvT", "WoT"]}
    brows = nc.dram_tensor("brows", [1, 4 * D], F16, kind="ExternalInput")
    bocol = nc.dram_tensor("bocol", [128, 8], F32, kind="ExternalInput")
    ctab = nc.dram_tensor("ctab", [32, TCORE], F16, kind="ExternalInput")
    stab = nc.dram_tensor("stab", [32, TCORE], F16, kind="ExternalInput")
    marc = nc.dram_tensor("marc", [128, 4 * 512], F32, kind="ExternalInput")
    kvin = nc.dram_tensor("kvin", [4096, 1024], F16)
    kvg = nc.dram_tensor("kvg", [16384, 1024], F16)

    row = p.sbuf("row", [1, 3 * TCORE], F32)
    xl = p.sbuf("xl", [128, 8 * TCORE], F16)
    RANKBLK = 4096 * 1024
    VOFF = 2048 * 1024

    with p.phase() as blk:
        c = load_consts(p, blk, io, want_f16=True)
        sb = {}
        def _(g_):
            sb["br"] = p.sbuf("brs", [1, 4 * D], F16)
            dma(g_, p.da, sb["br"][:, :], brows[:, :])
            sb["boc"] = p.sbuf("bocs", [128, 8], F32)
            dma(g_, p.da, sb["boc"][:, :], bocol[:, :])
            sb["ct"] = p.sbuf("cts", [32, TCORE], F16)
            dma(g_, p.da, sb["ct"][:, :], ctab[:, :])
            sb["st"] = p.sbuf("sts", [32, TCORE], F16)
            dma(g_, p.da, sb["st"][:, :], stab[:, :])
            sb["ma"] = p.sbuf("mas", [128, 4 * 512], F32)
            dma(g_, p.da, sb["ma"][:, :], marc[:, :])
            dma_wait_all(g_, p.da)
        blk.gpsimd(_)
        sb["or512"] = p.sbuf("or512", [1, 512], F16)
        def vi(v):
            v.memset(sb["or512"][:, :], 1.0)
        blk.vector(vi)

    with p.scope():
        with p.phase() as blk:
            emit_load_xl(p, blk, io, xl)
        with p.phase() as blk:
            emit_sumsq_row(p, blk, xl, TCORE, c["ident"], row, 0)
        with p.phase() as blk:
            emit_row_rstd(p, blk, row, 0, 0, TCORE)
        with p.phase() as blk:
            pb = [p.psum(f"lnb{i}", [128, 512], F32) for i in range(2)]
            pe0, dv0 = p.pe.n, p.dv.n
            def t_(e):
                for si in range(4):
                    if si >= 2:
                        e.wait_ge(p.dv.h, dv0 + si - 1)
                    mm = nc.tensor.matmul(pb[si % 2][:, :], c["ones1"][0:1, :],
                                          row[0:1, si * 512:si * 512 + 512],
                                          start=True, stop=True)
                    mm.then_inc(p.pe.h, 1)
            blk.tensor(t_)
            p.pe.n = pe0 + 4
            def v_(v):
                for si in range(4):
                    v.wait_ge(p.pe.h, pe0 + si + 1)
                    last = None
                    for kc in range(8):
                        sl = slice(kc * TCORE + si * 512, kc * TCORE + si * 512 + 512)
                        last = v.tensor_mul(xl[:, sl], xl[:, sl], pb[si % 2][:, :])
                    last.then_inc(p.dv.h, 1)
            blk.vector(v_)
            p.dv.n = dv0 + 4

    with p.scope():
        with p.phase() as blk:
            wk = load_weight(p, blk, wd["WkT"], D, D, "wk", dtype=F16)
        kfm = p.sbuf("kfm", [128, 8 * TCORE], F16)
        stg = p.sbuf("ropestg", [32, 4 * 512], F16)
        with p.phase() as blk:
            psk = [p.psum(f"kps{i}", [128, 512], F32) for i in range(2)]
            pe0, ac0, dv0 = p.pe.n, p.ac.n, p.dv.n
            def t_(e):
                i = 0
                for fc in range(8):
                    for tt in range(4):
                        if i >= 2:
                            e.wait_ge(p.ac.h, ac0 + 4 * (i - 1))
                        pp = psk[i % 2]
                        for kc in range(8):
                            nc.tensor.matmul(
                                pp[:, :],
                                wk[:, kc * D + fc * 128: kc * D + fc * 128 + 128],
                                xl[:, kc * TCORE + tt * 512: kc * TCORE + tt * 512 + 512],
                                start=(kc == 0), stop=False)
                        mm = nc.tensor.matmul(
                            pp[:, :],
                            sb["br"][0:1, 1 * D + fc * 128:1 * D + fc * 128 + 128],
                            sb["or512"][0:1, :],
                            start=False, stop=True)
                        mm.then_inc(p.pe.h, 1)
                        i += 1
            blk.tensor(t_)
            p.pe.n = pe0 + 32
            def a_(a):
                i = 0
                for fc in range(8):
                    for tt in range(4):
                        a.wait_ge(p.pe.h, pe0 + i + 1)
                        if i >= 1:
                            a.wait_ge(p.dv.h, dv0 + 12 * i)
                        for s4 in range(4):
                            a.activation(stg[:, s4 * 512:(s4 + 1) * 512],
                                         psk[i % 2][s4 * 32:(s4 + 1) * 32, :],
                                         AF.Copy).then_inc(p.ac.h, 1)
                        i += 1
            blk.scalar(a_)
            p.ac.n = ac0 + 128
            def v_(v):
                i = 0
                for fc in range(8):
                    for tt in range(4):
                        v.wait_ge(p.ac.h, ac0 + 4 * i + 4)
                        emit_rope_pair(p, v, stg, kfm, fc * TCORE + tt * 512,
                                       sb["ct"], sb["st"], tt * 512, 512)
                        i += 1
            blk.vector(v_)
        with p.phase() as blk:
            def _(g_):
                dma(g_, p.da,
                    bass.AP(kvin, 0, [[2048, 128], [128 * 2048, 8], [1, 2048]]),
                    kfm[:, :])
                dma_wait_all(g_, p.da)
            blk.gpsimd(_)
    with p.scope():
        with p.phase() as blk:
            wv = load_weight(p, blk, wd["WvT"], D, D, "wv", dtype=F16)
        vtm = p.sbuf("vtm", [128, 16 * 1024], F16)
        with p.phase() as blk:
            emit_proj_tm(p, blk, wv, D, xl, TCORE, vtm,
                         bias_row=sb["br"], ones1=c["ones1h"], bias_row_idx=2)
        with p.phase() as blk:
            def _(g_):
                dma(g_, p.da,
                    bass.AP(kvin, 2048 * 1024,
                            [[1024, 128], [128 * 1024, 16], [1, 1024]]),
                    vtm[:, :])
                dma_wait_all(g_, p.da)
            blk.gpsimd(_)
    with p.phase() as blk:
        def _(g_):
            g_.collective_compute(
                "AllGather", ALU.bypass, replica_groups=GROUPS,
                ins=[kvin.ap().opt()], outs=[kvg.ap().opt()]).then_inc(p.cc.h, 1)
            p.cc.n += 1
            g_.wait_ge(p.cc.h, p.cc.n)
        blk.gpsimd(_)

    ctxn = p.sbuf("ctxn", [128, 8 * TCORE], F16)
    with p.scope():
        with p.phase() as blk:
            wq = load_weight(p, blk, wd["WqT"], D, D, "wq", dtype=F16)
        emit_flash_l0(p, {"xl": xl, "WqT_sb": wq, "kvg": kvg, "ctxn": ctxn,
                          "sb": sb, "c": c})
    y = p.sbuf("y", [128, 8 * TCORE], BF16)
    with p.scope():
        with p.phase() as blk:
            wo = load_weight(p, blk, wd["WoT"], D, D, "wo", dtype=F16)
        with p.phase() as blk:
            emit_proj(p, blk, wo, D, ctxn, TCORE,
                      act_epi(y, TCORE, bias=sb["boc"]))
    with p.scope():
        emit_postln_residual_streamed(p, io, y, c, row)
    return p.finish()


def emit_rope_pair(p, v, stg, dst, dst0, ct, st, t0, T, sem=None,
                   tmp_name="ropetmp"):
    sem = sem if sem is not None else p.dv
    tmp = p.sbuf(f"{tmp_name}_{T}", [32, 4 * T], F32)
    base0 = sem.n
    tA = tmp[0:32, 0:T]
    tB = tmp[0:32, T:2 * T]
    tC = tmp[0:32, 2 * T:3 * T]
    tD = tmp[0:32, 3 * T:4 * T]

    def W(at):
        v.wait_ge(sem.h, base0 + at)

    def I(x):
        x.then_inc(sem.h, 1)
        sem.n += 1
    W(0)
    for h in range(2):
        k1 = stg[:, (2 * h) * T:(2 * h) * T + T]
        k2 = stg[:, (2 * h + 1) * T:(2 * h + 1) * T + T]
        base = (2 * h) * 32
        cs = ct[0:32, t0:t0 + T]
        ss = st[0:32, t0:t0 + T]
        if h == 1:
            W(3)
        I(v.tensor_mul(tA, k1, cs))
        I(v.tensor_mul(tB, k2, ss))
        W(2 + 6 * h)
        I(v.tensor_sub(dst[base:base + 32, dst0:dst0 + T], tA, tB))
        I(v.tensor_mul(tC, k1, ss))
        I(v.tensor_mul(tD, k2, cs))
        W(5 + 6 * h)
        I(v.tensor_add(dst[base + 32:base + 64, dst0:dst0 + T], tC, tD))


def emit_flash_l0(p, cfg):
    nc = p.nc
    xl, WqT, kvg, ctxn = cfg["xl"], cfg["WqT_sb"], cfg["kvg"], cfg["ctxn"]
    sb, c = cfg["sb"], cfg["c"]
    TD = TCORE
    RANKBLK = 4096 * 1024
    VOFF = 2048 * 1024

    kqs = [p.sbuf(f"kqs{i}_h", [128, 2048], F16) for i in range(2)]
    vqs = [p.sbuf(f"vqs{i}_h", [128, 16 * 130], F16) for i in range(2)]
    q_sb = p.sbuf("q_sbh", [128, 512], F16)
    qstg = p.sbuf("qstgh", [32, 4 * 512], F16)
    p_sb = [p.sbuf(f"p_sbh{i}", [128, 1024], F16) for i in range(2)]
    rrow = p.sbuf("rrowh", [1, 1024], F32)
    sbc = p.sbuf("sbch", [128, 512], F32)
    ps_q = p.psum("fl_qh", [128, 512], F32)
    ps_s = [p.psum(f"fl_sh{i}", [128, 1024], F32) for i in range(2)]
    ps_av = [p.psum(f"fl_avh{i}", [65, 512], F32) for i in range(2)]
    ps_bc = p.psum("fl_bch", [128, 512], F32)

    with p.phase() as blk:
        def vinit(v):
            for i in range(2):
                v.memset(bass.AP(vqs[i], 64,
                                 [[16 * 130, 128], [130, 16], [65, 2], [1, 1]]), 1.0)
        blk.vector(vinit)

    with p.phase() as blk:
        pe0, ac0, dv0 = p.pe.n, p.ac.n, p.dv.n
        w20 = [p.w2[0].n, p.w2[1].n]
        variants = []
        for g in range(4):
            sch = l0_scheds(g)
            variants.append(sch)
        def totals(sch):
            pe = ac = dv = 0
            st_dmas = 0
            nst = 0
            for hp in range(8):
                for slot in range(2):
                    for half in range(2):
                        stages = sch[(slot, half)]
                        nkt = sum(len(k) for _, k in stages)
                        pe += 2 + 2 * nkt
                        ac += 4 + nkt
                        dv += 19
                        nst += len(stages)
            return pe, ac, dv, nst
        t0s = [totals(v) for v in variants]
        assert len(set(t0s)) == 1, t0s

        def body(which_engine):
            pass

        def t_(e):
            pid = e.partition_id()
            if True:
                for g in e.Switch(pid % 4, 4):
                    emit_l0_variant_tensor(p, e, variants[g], g, xl, WqT, kqs,
                                           vqs, q_sb, p_sb, ps_q, ps_s, ps_av,
                                           ps_bc, sb, c, rrow,
                                           pe0, ac0, dv0, w20)
        blk.tensor(t_)

        def a_(a):
            pid = a.partition_id()
            if True:
                for g in a.Switch(pid % 4, 4):
                    emit_l0_variant_scalar(p, a, variants[g], g, qstg, ps_q,
                                           ps_s, p_sb, sb, pe0, ac0, dv0)
        blk.scalar(a_)

        def v_(v):
            pid = v.partition_id()
            if True:
                for g in v.Switch(pid % 4, 4):
                    emit_l0_variant_vector(p, v, variants[g], g, qstg, q_sb,
                                           ctxn, ps_av, ps_bc, ps_s, rrow, sbc,
                                           sb, c, pe0, ac0, dv0)
        blk.vector(v_)

        def g_(g_eng):
            pid = g_eng.partition_id()
            if True:
                for g in g_eng.Switch(pid % 4, 4):
                    emit_l0_variant_gpsimd(p, g_eng, variants[g], g, kvg, kqs,
                                           vqs, pe0, w20)
        blk.gpsimd(g_)

        pe_t, ac_t, dv_t, nst = t0s[0]
        p.pe.n = pe0 + pe_t
        p.ac.n = ac0 + ac_t
        p.dv.n = dv0 + dv_t
        par_cnt = [0, 0]
        for si in range(nst):
            par_cnt[si % 2] += 6 * 16
        p.w2[0].n = w20[0] + par_cnt[0]
        p.w2[1].n = w20[1] + par_cnt[1]



def _l0_iter_tables(sched, pe0, ac0, dv0):
    pe_at, ac_at, dv_at, nkts, stages_all = [], [], [], [], []
    pe_c, ac_c, dv_c = pe0, ac0, dv0
    for hp in range(8):
        for slot in range(2):
            for half in range(2):
                stages = sched[(slot, half)]
                nkt = sum(len(k) for _, k in stages)
                pe_at.append(pe_c)
                ac_at.append(ac_c)
                dv_at.append(dv_c)
                nkts.append(nkt)
                stages_all.append((hp, slot, half, stages))
                pe_c += 2 + 2 * nkt
                ac_c += 4 + nkt
                dv_c += 19
    pe_at.append(pe_c); ac_at.append(ac_c); dv_at.append(dv_c)
    return pe_at, ac_at, dv_at, nkts, stages_all


def _l0_sim(nkts, pe0, bonus_after_qp=False):
    L = len(nkts)
    qp_abs = [0] * L
    bonus_abs = [0] * L
    s_abs, av_abs = [], []
    pos = pe0
    for it in range(L):
        n = nkts[it]
        pos += 1
        qp_abs[it] = pos
        sa = [0] * n
        aa = [0] * n
        if bonus_after_qp and it > 0:
            pos += 1
            bonus_abs[it - 1] = pos
        for k in (0, 1):
            if k < n:
                pos += 1
                sa[k] = pos
        if (not bonus_after_qp) and it > 0:
            pos += 1
            bonus_abs[it - 1] = pos
        for k in range(n):
            pos += 1
            aa[k] = pos
            if k + 2 < n:
                pos += 1
                sa[k + 2] = pos
        s_abs.append(sa)
        av_abs.append(aa)
    pos += 1
    bonus_abs[L - 1] = pos
    return qp_abs, bonus_abs, s_abs, av_abs


def _l0_stage_list(stages_all, pe_at, nkts):
    _, _, _, av_abs = _l0_sim(nkts, pe_at[0], bonus_after_qp=True)
    out = []
    for it, (hp, slot, half, stages) in enumerate(stages_all):
        kdone = 0
        for (chs, kts) in stages:
            kdone += len(kts)
            pe_done = av_abs[it][kdone - 1]
            out.append((it, hp, slot, half, chs, kts, pe_done))
    return out


def emit_l0_variant_gpsimd(p, g_eng, sched, g, kvg, kqs, vqs, pe0, w20):
    pe_at, ac_at, dv_at, nkts, stages_all = _l0_iter_tables(sched, pe0, 0, 0)
    stages = _l0_stage_list(stages_all, pe_at, nkts)
    RANKBLK = 4096 * 1024
    VOFF = 2048 * 1024
    for si, (it, hp, slot, half, chs, kts, pe_done) in enumerate(stages):
        par = si % 2
        if si >= 2:
            g_eng.wait_ge(p.pe.h, stages[si - 2][6])
        for ci in range(2):
            cq = chs[ci] if ci < len(chs) else chs[-1]
            r, s = owner_of_chunk(cq)
            koff = r * RANKBLK + (hp * 128) * 2048 + s * 1024
            g_eng.dma_start(
                out=kqs[par][:, ci * 1024:(ci + 1) * 1024],
                in_=bass.AP(kvg, koff, [[2048, 128], [1, 1024]])
            ).then_inc(p.w2[par].h, 16)
            voff = r * RANKBLK + VOFF + (s * 1024) * 1024 + hp * 128
            for h in range(2):
                g_eng.dma_start(
                    out=bass.AP(vqs[par], ci * 8 * 130 + h * 65,
                                [[16 * 130, 128], [130, 8], [1, 64]]),
                    in_=bass.AP(kvg, voff + h * 64,
                                [[1024, 128], [131072, 8], [1, 64]])
                ).then_inc(p.w2[par].h, 16)


def emit_l0_variant_tensor(p, e, sched, g, xl, WqT, kqs, vqs, q_sb, p_sb,
                           ps_q, ps_s, ps_av, ps_bc, sb, c, rrow,
                           pe0, ac0, dv0, w20):
    nc = p.nc
    pe_at, ac_at, dv_at, nkts, stages_all = _l0_iter_tables(sched, pe0, ac0, dv0)
    stages = _l0_stage_list(stages_all, pe_at, nkts)
    w2c = [w20[0], w20[1]]
    stage_w2 = []
    for si in range(len(stages)):
        w2c[si % 2] += 6 * 16
        stage_w2.append((si % 2, w2c[si % 2]))

    def emit_bonus(it):
        e.wait_ge(p.dv.h, dv_at[it] + 12 + 3)
        for h in range(2):
            mm = nc.tensor.matmul(
                ps_bc[h * 64:(h + 1) * 64, :],
                c["ones1"][0:1, 0:64],
                rrow[0:1, h * 512:(h + 1) * 512],
                start=True, stop=True)
        mm.then_inc(p.pe.h, 1)

    si = 0
    for it, (hp, slot, half, stgs) in enumerate(stages_all):
        acb, dvb = ac_at[it], dv_at[it]
        nkt = nkts[it]
        qt0 = slot * CHT + half * 512
        if it >= 1:
            e.wait_ge(p.ac.h, ac_at[it - 1] + 4)
        for kc in range(8):
            nc.tensor.matmul(
                ps_q[:, :],
                WqT[:, kc * D + hp * 128: kc * D + hp * 128 + 128],
                xl[:, kc * TCORE + qt0: kc * TCORE + qt0 + 512],
                start=(kc == 0), stop=False)
        mm = nc.tensor.matmul(
            ps_q[:, :], sb["br"][0:1, hp * 128:hp * 128 + 128],
            sb["or512"][0:1, :], start=False, stop=True)
        mm.then_inc(p.pe.h, 1)
        if it >= 1:
            emit_bonus(it - 1)
        tiles = []
        for j, (chs, kts) in enumerate(stgs):
            for (ci, kt, arch) in kts:
                tiles.append((si + j, ci, kt, arch))
        si += len(stgs)
        seen_stage = set()

        def emit_S(k):
            sg, ci, kt, arch = tiles[k]
            par_st, w2t = stage_w2[sg]
            if sg not in seen_stage:
                seen_stage.add(sg)
                e.wait_ge(p.w2[par_st].h, w2t)
            if k == 0:
                e.wait_ge(p.dv.h, dvb + 12)
                e.wait_ge(p.ac.h, acb + 4)
            par = k % 2
            col = ci * 1024 + kt * 128
            for h in range(2):
                mm = nc.tensor.matmul(
                    ps_s[par][:, h * 512:(h + 1) * 512],
                    kqs[par_st][h * 64:(h + 1) * 64, col:col + 128],
                    q_sb[h * 64:(h + 1) * 64, :],
                    start=True, stop=(arch is None))
                if arch is not None:
                    mm = nc.tensor.matmul(
                        ps_s[par][:, h * 512:(h + 1) * 512],
                        c["ident"][:, :],
                        sb["ma"][:, arch * 512:arch * 512 + 512],
                        start=False, stop=True)
            mm.then_inc(p.pe.h, 1)

        def emit_AV(k):
            sg, ci, kt, arch = tiles[k]
            par_st, _ = stage_w2[sg]
            par = k % 2
            e.wait_ge(p.ac.h, acb + 4 + k + 1)
            for h in range(2):
                mm = nc.tensor.matmul(
                    ps_av[h][:, :],
                    vqs[par_st][:, (ci * 8 + kt) * 130 + h * 65:
                                (ci * 8 + kt) * 130 + h * 65 + 65],
                    p_sb[par][:, h * 512:(h + 1) * 512],
                    start=(k == 0), stop=(k == nkt - 1))
            mm.then_inc(p.pe.h, 1)

        for k in (0, 1):
            if k < nkt:
                emit_S(k)
        for k in range(nkt):
            emit_AV(k)
            if k + 2 < nkt:
                emit_S(k + 2)
    emit_bonus(len(stages_all) - 1)


def emit_l0_variant_scalar(p, a, sched, g, qstg, ps_q, ps_s, p_sb, sb,
                           pe0, ac0, dv0):
    pe_at, ac_at, dv_at, nkts, stages_all = _l0_iter_tables(sched, pe0, ac0, dv0)
    qp_abs, bonus_abs, s_abs, av_abs = _l0_sim(nkts, pe0, bonus_after_qp=True)
    for it, (hp, slot, half, stgs) in enumerate(stages_all):
        nkt = nkts[it]
        a.wait_ge(p.pe.h, qp_abs[it])
        if it >= 1:
            a.wait_ge(p.dv.h, dv_at[it - 1] + 12)
        for s4 in range(4):
            a.activation(qstg[:, s4 * 512:(s4 + 1) * 512],
                         ps_q[s4 * 32:(s4 + 1) * 32, :],
                         AF.Copy).then_inc(p.ac.h, 1)
        for kglob in range(nkt):
            par = kglob % 2
            a.wait_ge(p.pe.h, s_abs[it][kglob])
            a.activation(p_sb[par][:, :], ps_s[par][:, :], AF.Exp,
                         scale=0.125).then_inc(p.ac.h, 1)


def emit_l0_variant_vector(p, v, sched, g, qstg, q_sb, ctxn, ps_av, ps_bc,
                           ps_s, rrow, sbc, sb, c, pe0, ac0, dv0):
    pe_at, ac_at, dv_at, nkts, stages_all = _l0_iter_tables(sched, pe0, ac0, dv0)
    qp_abs, bonus_abs, s_abs, av_abs = _l0_sim(nkts, pe0, bonus_after_qp=True)
    for it, (hp, slot, half, stgs) in enumerate(stages_all):
        acb, dvb = ac_at[it], dv_at[it]
        nkt = nkts[it]
        qt0 = slot * CHT + half * 512
        v.wait_ge(p.ac.h, acb + 4)
        p.dv.n = dvb
        emit_rope_pair(p, v, qstg, q_sb, 0, sb["ct"], sb["st"], qt0, 512)
        assert p.dv.n == dvb + 12
        v.wait_ge(p.pe.h, av_abs[it][-1])
        v.tensor_copy(rrow[0:1, 0:512], ps_av[0][64:65, :]).then_inc(p.dv.h, 1)
        v.tensor_copy(rrow[0:1, 512:1024], ps_av[1][64:65, :]).then_inc(p.dv.h, 1)
        v.wait_ge(p.dv.h, dvb + 14)
        v.reciprocal(rrow[0:1, :], rrow[0:1, :]).then_inc(p.dv.h, 1)
        v.wait_ge(p.pe.h, bonus_abs[it])
        v.tensor_copy(sbc[0:64, :], ps_bc[0:64, :]).then_inc(p.dv.h, 1)
        v.tensor_copy(sbc[64:128, :], ps_bc[64:128, :]).then_inc(p.dv.h, 1)
        v.wait_ge(p.dv.h, dvb + 17)
        col = hp * TCORE + qt0
        v.tensor_mul(ctxn[0:64, col:col + 512], ps_av[0][0:64, :],
                     sbc[0:64, :]).then_inc(p.dv.h, 1)
        v.tensor_mul(ctxn[64:128, col:col + 512], ps_av[1][0:64, :],
                     sbc[64:128, :]).then_inc(p.dv.h, 1)
        p.dv.n = dvb + 19



def host_prep_l0attn(H, inputs):
    f16 = np.float16
    for nm, key in [("WqT", "L0_attn_Wq"), ("WkT", "L0_attn_Wk"),
                    ("WvT", "L0_attn_Wv"), ("WoT", "L0_attn_Wo")]:
        H["l0a_" + nm] = np.ascontiguousarray(np.asarray(inputs[key], f16))
    H["l0a_brows"] = np.concatenate(
        [np.asarray(inputs[k], f16).reshape(1, D) for k in
         ["L0_attn_bq", "L0_attn_bk", "L0_attn_bv", "L0_attn_bo"]], axis=1)
    H["l0a_bocol"] = np.ascontiguousarray(
        np.asarray(inputs["L0_attn_bo"], np.float32).reshape(8, 128).T)
    Dh = 64
    pos = np.arange(S, dtype=np.float32)
    inv = np.exp(np.arange(0, Dh, 2, dtype=np.float32) * (-np.log(10000.0) / Dh))
    ang = pos[:, None] * inv[None, :]
    sinf = np.sin(ang).astype(f16)
    cosf = np.cos(ang).astype(f16)
    H["l0a_ctab"] = []
    H["l0a_stab"] = []
    for cx in range(NCORES):
        idx = np.concatenate([np.arange(cg * CHT, (cg + 1) * CHT)
                              for cg in chunks_of_core(cx)])
        H["l0a_ctab"].append(np.ascontiguousarray(cosf[idx].T))
        H["l0a_stab"].append(np.ascontiguousarray(sinf[idx].T))
    ma = np.zeros((128, 4 * 512), np.float32)
    pp_, jj = np.arange(128)[:, None], np.arange(512)[None, :]
    for a in range(4):
        ma[:, a * 512:(a + 1) * 512] = np.where(pp_ <= jj - 128 * a, 0.0, -80000.0)
    H["l0a_marc"] = ma
    return H


def ins_l0_attn(cx, H, inputs, state):
    return {"xin": state["stream"][cx], "ident_f32": H["ident_f32"],
            "ones1_f32": H["ones1_f32"], "constv": H["constv"],
            "WqT": H["l0a_WqT"], "WkT": H["l0a_WkT"], "WvT": H["l0a_WvT"],
            "WoT": H["l0a_WoT"], "brows": H["l0a_brows"],
            "bocol": H["l0a_bocol"], "ctab": H["l0a_ctab"][cx],
            "stab": H["l0a_stab"][cx], "marc": H["l0a_marc"]}




def l3_scheds(g):
    out = {}
    for slot, cg in enumerate([g, 7 - g]):
        for half in range(2):
            P = cg * 1024 + half * 512
            t0 = P // 128
            band = [t for t in range(t0 - 1, t0 + 5) if 0 <= t < 64]
            chunks = sorted({t // 8 for t in band})
            assert len(chunks) <= 2
            kts = []
            for t in band:
                o = t - t0
                ci = chunks.index(t // 8)
                arch = (8 if t == 0 else o + 1)
                kts.append((ci, t % 8, arch, False))
            if 0 not in band:
                kts.append((0, 0, 7, True))
            while len(kts) < 7:
                kts.append((0, band[0] % 8, 6, False))
            while len(chunks) < 2:
                chunks.append(chunks[-1])
            out[(slot, half)] = [(chunks, kts)]
    return out


def _l3_iter_tables(sched, pe0, ac0, dv0):
    pe_at, ac_at, dv_at, nkts, stages_all = [], [], [], [], []
    pe_c, ac_c, dv_c = pe0, ac0, dv0
    for hp in range(8):
        for slot in range(2):
            for half in range(2):
                stages = sched[(slot, half)]
                nkt = sum(len(k) for _, k in stages)
                pe_at.append(pe_c)
                ac_at.append(ac_c)
                dv_at.append(dv_c)
                nkts.append(nkt)
                stages_all.append((hp, slot, half, stages))
                pe_c += 2 + 2 * nkt
                ac_c += 1 + nkt
                dv_c += 7
    pe_at.append(pe_c); ac_at.append(ac_c); dv_at.append(dv_c)
    return pe_at, ac_at, dv_at, nkts, stages_all


def emit_l3_variant_gpsimd(p, g_eng, sched, kvg, kqs, vqs, pe0, w20):
    pe_at, ac_at, dv_at, nkts, stages_all = _l3_iter_tables(sched, pe0, 0, 0)
    _, _, _, av_abs = _l0_sim(nkts, pe0)
    RANKBLK = 4096 * 1024
    VOFF = 2048 * 1024
    for it, (hp, slot, half, stgs) in enumerate(stages_all):
        si = it
        par = si % 2
        if si >= 2:
            g_eng.wait_ge(p.pe.h, av_abs[si - 2][-1])
        chs = stgs[0][0]
        for ci in range(2):
            cq = chs[ci] if ci < len(chs) else chs[-1]
            r, s = owner_of_chunk(cq)
            koff = r * RANKBLK + (hp * 128) * 2048 + s * 1024
            g_eng.dma_start(
                out=kqs[par][:, ci * 1024:(ci + 1) * 1024],
                in_=bass.AP(kvg, koff, [[2048, 128], [1, 1024]])
            ).then_inc(p.w2[par].h, 16)
            voff = r * RANKBLK + VOFF + (s * 1024) * 1024 + hp * 128
            for h in range(2):
                g_eng.dma_start(
                    out=bass.AP(vqs[par], ci * 8 * 130 + h * 65,
                                [[16 * 130, 128], [130, 8], [1, 64]]),
                    in_=bass.AP(kvg, voff + h * 64,
                                [[1024, 128], [131072, 8], [1, 64]])
                ).then_inc(p.w2[par].h, 16)


def emit_l3_variant_tensor(p, e, sched, xl, WqT, kqs, vqs, q_sb, p_sb,
                           ps_q, ps_s, ps_av, ps_bc, sb, c, rrow,
                           pe0, ac0, dv0, w20):
    nc = p.nc
    pe_at, ac_at, dv_at, nkts, stages_all = _l3_iter_tables(sched, pe0, ac0, dv0)
    w2c = [w20[0], w20[1]]
    stage_w2 = []
    for si in range(len(stages_all)):
        w2c[si % 2] += 6 * 16
        stage_w2.append((si % 2, w2c[si % 2]))
    def emit_bonus(it):
        e.wait_ge(p.dv.h, dv_at[it] + 3)
        for h in range(2):
            mm = nc.tensor.matmul(
                ps_bc[h * 64:(h + 1) * 64, :],
                c["ones1"][0:1, 0:64],
                rrow[0:1, h * 512:(h + 1) * 512],
                start=True, stop=True)
        mm.then_inc(p.pe.h, 1)

    for it, (hp, slot, half, stgs) in enumerate(stages_all):
        acb, dvb = ac_at[it], dv_at[it]
        nkt = nkts[it]
        qt0 = slot * CHT + half * 512
        if it >= 1:
            e.wait_ge(p.ac.h, ac_at[it - 1] + 1)
        for kc in range(8):
            mm = nc.tensor.matmul(
                ps_q[:, :],
                WqT[:, kc * D + hp * 128: kc * D + hp * 128 + 128],
                xl[:, kc * TCORE + qt0: kc * TCORE + qt0 + 512],
                start=(kc == 0), stop=(kc == 7))
        mm.then_inc(p.pe.h, 1)
        par_st, w2t = stage_w2[it]
        e.wait_ge(p.w2[par_st].h, w2t)
        chs, kts = stgs[0]

        def emit_S(k):
            ci, kt, arch, use_v0 = kts[k]
            if k == 0:
                e.wait_ge(p.ac.h, acb + 1)
            par = k % 2
            col = ci * 1024 + kt * 128
            for h in range(2):
                if arch == 7:
                    klhs = sb["k0p"][h * 64:(h + 1) * 64,
                                     hp * 128:hp * 128 + 128]
                else:
                    klhs = kqs[par_st][h * 64:(h + 1) * 64, col:col + 128]
                nc.tensor.matmul(
                    ps_s[par][:, h * 512:(h + 1) * 512],
                    klhs, q_sb[h * 64:(h + 1) * 64, :],
                    start=True, stop=False)
                mm = nc.tensor.matmul(
                    ps_s[par][:, h * 512:(h + 1) * 512],
                    c["ident"][:, :],
                    sb["ma"][:, arch * 512:arch * 512 + 512],
                    start=False, stop=True)
            mm.then_inc(p.pe.h, 1)

        def emit_AV(k):
            ci, kt, arch, use_v0 = kts[k]
            par = k % 2
            e.wait_ge(p.ac.h, acb + 1 + k + 1)
            if k == 0 and it >= 1:
                e.wait_ge(p.dv.h, dv_at[it])
            for h in range(2):
                if use_v0:
                    lhs = sb["v0st"][:, hp * 130 + h * 65:hp * 130 + h * 65 + 65]
                else:
                    lhs = vqs[par_st][:, (ci * 8 + kt) * 130 + h * 65:
                                      (ci * 8 + kt) * 130 + h * 65 + 65]
                mm = nc.tensor.matmul(
                    ps_av[h][:, :], lhs, p_sb[par][:, h * 512:(h + 1) * 512],
                    start=(k == 0), stop=(k == nkt - 1))
            mm.then_inc(p.pe.h, 1)

        for k in (0, 1):
            if k < nkt:
                emit_S(k)
        if it >= 1:
            emit_bonus(it - 1)
        for k in range(nkt):
            emit_AV(k)
            if k + 2 < nkt:
                emit_S(k + 2)
    emit_bonus(len(stages_all) - 1)


def emit_l3_variant_scalar(p, a, sched, ps_q, ps_s, p_sb, q_sb, pe0, ac0, dv0):
    pe_at, ac_at, dv_at, nkts, stages_all = _l3_iter_tables(sched, pe0, ac0, dv0)
    qp_abs, bonus_abs, s_abs, av_abs = _l0_sim(nkts, pe0)
    for it, (hp, slot, half, stgs) in enumerate(stages_all):
        nkt = nkts[it]
        a.wait_ge(p.pe.h, qp_abs[it])
        a.activation(q_sb[:, :], ps_q[:, :], AF.Copy).then_inc(p.ac.h, 1)
        for kglob in range(nkt):
            par = kglob % 2
            a.wait_ge(p.pe.h, s_abs[it][kglob])
            a.activation(p_sb[par][:, :], ps_s[par][:, :], AF.Exp,
                         scale=0.125).then_inc(p.ac.h, 1)


def emit_l3_variant_vector(p, v, sched, ctxn, ps_av, ps_bc, ps_s, rrow, sbc,
                           pe0, ac0, dv0):
    pe_at, ac_at, dv_at, nkts, stages_all = _l3_iter_tables(sched, pe0, ac0, dv0)
    qp_abs, bonus_abs, s_abs, av_abs = _l0_sim(nkts, pe0)
    for it, (hp, slot, half, stgs) in enumerate(stages_all):
        dvb = dv_at[it]
        nkt = nkts[it]
        p.dv.n = dvb
        v.wait_ge(p.pe.h, av_abs[it][-1])
        v.tensor_copy(rrow[0:1, 0:512], ps_av[0][64:65, :]).then_inc(p.dv.h, 1)
        v.tensor_copy(rrow[0:1, 512:1024], ps_av[1][64:65, :]).then_inc(p.dv.h, 1)
        v.wait_ge(p.dv.h, dvb + 2)
        v.reciprocal(rrow[0:1, :], rrow[0:1, :]).then_inc(p.dv.h, 1)
        v.wait_ge(p.pe.h, bonus_abs[it])
        v.tensor_copy(sbc[0:64, :], ps_bc[0:64, :]).then_inc(p.dv.h, 1)
        v.tensor_copy(sbc[64:128, :], ps_bc[64:128, :]).then_inc(p.dv.h, 1)
        v.wait_ge(p.dv.h, dvb + 5)
        col = hp * TCORE + slot * CHT + half * 512
        v.tensor_mul(ctxn[0:64, col:col + 512], ps_av[0][0:64, :],
                     sbc[0:64, :]).then_inc(p.dv.h, 1)
        v.tensor_mul(ctxn[64:128, col:col + 512], ps_av[1][0:64, :],
                     sbc[64:128, :]).then_inc(p.dv.h, 1)
        p.dv.n = dvb + 7


def build_l3_attn():
    p = Prog()
    nc = p.nc
    io = stream_io_tensors(p)
    wd = {nm: nc.dram_tensor(nm, [D, D], BF16, kind="ExternalInput")
          for nm in ["WqT", "WkT", "WvT", "WoT"]}
    marc = nc.dram_tensor("marc", [128, 9 * 512], F32, kind="ExternalInput")
    s01 = nc.dram_tensor("s01", [128, 2], F32, kind="ExternalInput")
    kvin = nc.dram_tensor("kvin", [4096, 1024], BF16)
    kvg = nc.dram_tensor("kvg", [16384, 1024], BF16)
    q0in = nc.dram_tensor("q0in", [1, 1024], BF16)
    q0g = nc.dram_tensor("q0g", [4, 1024], BF16)
    o0in = nc.dram_tensor("o0in", [65, 16], F32)
    o0out = nc.dram_tensor("o0out", [65, 16], F32)

    row = p.sbuf("row", [1, 3 * TCORE], F32)
    xl = p.sbuf("xl", [128, 8 * TCORE], BF16)
    RANKBLK = 4096 * 1024
    VOFF = 2048 * 1024

    with p.phase() as blk:
        c = load_consts(p, blk, io)
        sb = {}
        def _(g_):
            sb["ma"] = p.sbuf("mas3", [128, 9 * 512], F32)
            dma(g_, p.da, sb["ma"][:, :], marc[:, :])
            sb["s01"] = p.sbuf("s01s", [128, 2], F32)
            dma(g_, p.da, sb["s01"][:, :], s01[:, :])
            dma_wait_all(g_, p.da)
        blk.gpsimd(_)

    with p.scope():
        emit_preln_std(p, io, c, xl, row)

    kfm = None
    with p.scope():
        with p.phase() as blk:
            wk = load_weight(p, blk, wd["WkT"], D, D, "wk")
        kfm = p.sbuf("kfm", [128, 8 * TCORE], BF16)
        with p.scope():
            with p.phase() as blk:
                emit_proj(p, blk, wk, D, xl, TCORE, act_epi(kfm, TCORE))
        with p.phase() as blk:
            def _(g_):
                dma(g_, p.da,
                    bass.AP(kvin, 0, [[2048, 128], [128 * 2048, 8], [1, 2048]]),
                    kfm[:, :])
                dma_wait_all(g_, p.da)
            blk.gpsimd(_)
        vtm = p.sbuf("vtm", [128, 16 * 1024], BF16)
        with p.phase() as blk:
            wv = load_weight(p, blk, wd["WvT"], D, D, "wv")
        with p.scope():
            with p.phase() as blk:
                emit_proj_tm(p, blk, wv, D, xl, TCORE, vtm)
        with p.phase() as blk:
            def _(g_):
                dma(g_, p.da,
                    bass.AP(kvin, 2048 * 1024,
                            [[1024, 128], [128 * 1024, 16], [1, 1024]]),
                    vtm[:, :])
                dma_wait_all(g_, p.da)
            blk.gpsimd(_)
        with p.phase() as blk:
            wq0 = load_weight(p, blk, wd["WqT"], D, D, "wq")
            sb["wq"] = wq0
        q0c = p.sbuf("q0c", [128, 8], BF16)
        with p.scope():
          with p.phase() as blk:
            psq0 = p.psum("q0ps", [128, 512], F32)
            pe0 = p.pe.n
            def t_(e):
                for fc in range(8):
                    for kc in range(8):
                        mm = nc.tensor.matmul(
                            psq0[:, fc:fc + 1],
                            sb["wq"][:, kc * D + fc * 128: kc * D + fc * 128 + 128],
                            xl[:, kc * TCORE:kc * TCORE + 1],
                            start=(kc == 0), stop=(kc == 7))
                mm.then_inc(p.pe.h, 1)
            blk.tensor(t_)
            p.pe.n = pe0 + 1
            def v_(v):
                v.wait_ge(p.pe.h, p.pe.n)
                v.tensor_copy(q0c[:, :], psq0[:, 0:8]).then_inc(p.dv.h, 1)
                p.dv.n += 1
            blk.vector(v_)
        with p.phase() as blk:
            def _(g_):
                g_.dma_start(out=bass.AP(q0in, 0, [[1, 128], [128, 8]]),
                             in_=q0c[:, :],
                             allow_slow_non_contiguous=True).then_inc(p.da.h, 16)
                p.da.n += 16
                dma_wait_all(g_, p.da)
            blk.gpsimd(_)
        with p.phase() as blk:
            def _(g_):
                g_.collective_compute(
                    "AllGather", ALU.bypass, replica_groups=GROUPS,
                    ins=[kvin.ap().opt()], outs=[kvg.ap().opt()]).then_inc(p.cc.h, 1)
                p.cc.n += 1
                g_.collective_compute(
                    "AllGather", ALU.bypass, replica_groups=GROUPS,
                    ins=[q0in.ap().opt()], outs=[q0g.ap().opt()]).then_inc(p.cc.h, 1)
                p.cc.n += 1
                g_.wait_ge(p.cc.h, p.cc.n)
            blk.gpsimd(_)
        with p.phase() as blk:
            q0sb = p.sbuf("q0sb", [128, 8], BF16)
            def _(g_):
                g_.dma_start(out=q0sb[:, :],
                             in_=bass.AP(q0g, 0, [[1, 128], [128, 8]]),
                             allow_slow_non_contiguous=True).then_inc(p.da.h, 16)
                p.da.n += 16
                dma_wait_all(g_, p.da)
            blk.gpsimd(_)
        with p.scope():
          with p.phase() as blk:
            ps0 = [p.psum(f"s0ps{i}", [128, 512], F32) for i in range(2)]
            pso = p.psum("o0ps", [64, 512], F32)
            psrho = p.psum("rho0ps", [16, 512], F32)
            psrr = p.psum("rr0ps", [1, 512], F32)
            p0sb = p.sbuf("p0sb", [128, 16], BF16)
            rhosb = p.sbuf("rhosb", [16, 1], F32)
            o0sb = p.sbuf("o0sb", [65, 16], F32)
            pe0, ac0, dv0 = p.pe.n, p.ac.n, p.dv.n
            def t_(e):
                i = 0
                for tt in range(16):
                    par = tt % 2
                    if tt >= 2:
                        e.wait_ge(p.ac.h, ac0 + tt - 1)
                    for h in range(16):
                        fc, hh = h // 2, h % 2
                        mm = nc.tensor.matmul(
                            ps0[par][:, h:h + 1],
                            kfm[hh * 64:(hh + 1) * 64,
                                fc * TCORE + tt * 128:fc * TCORE + tt * 128 + 128],
                            q0sb[hh * 64:(hh + 1) * 64, fc:fc + 1],
                            start=True, stop=True)
                    mm.then_inc(p.pe.h, 1)
                    e.wait_ge(p.ac.h, ac0 + tt + 1)
                    for h in range(16):
                        fc = h // 2
                        mm = nc.tensor.matmul(
                            pso[0:64, h:h + 1],
                            vtm[:, tt * 1024 + h * 64:tt * 1024 + h * 64 + 64],
                            p0sb[:, h:h + 1],
                            start=(tt == 0 and h == 0),
                            stop=(tt == 15 and h == 15))
                    mm2 = nc.tensor.matmul(
                        psrho[:, 0:1], p0sb[:, :], c["onescb"][:, 0:1],
                        start=(tt == 0), stop=(tt == 15))
                    mm2.then_inc(p.pe.h, 1)
            blk.tensor(t_)
            p.pe.n = pe0 + 32
            def a_(a):
                for tt in range(16):
                    par = tt % 2
                    a.wait_ge(p.pe.h, pe0 + 2 * tt + 1)
                    if tt >= 1:
                        a.wait_ge(p.pe.h, pe0 + 2 * tt)
                    a.activation(p0sb[:, :], ps0[par][:, 0:16], AF.Exp,
                                 scale=0.125).then_inc(p.ac.h, 1)
            blk.scalar(a_)
            p.ac.n = ac0 + 16
            def v_(v):
                v.wait_ge(p.pe.h, p.pe.n)
                v.tensor_copy(o0sb[0:64, :], pso[:, 0:16]).then_inc(p.dv.h, 1)
                p.dv.n += 1
                v.wait_ge(p.dv.h, p.dv.n)
                v.tensor_copy(rhosb[:, :], psrho[:, 0:1]).then_inc(p.dv.h, 1)
                p.dv.n += 1
            blk.vector(v_)
            pe1 = p.pe.n
            def t2_(e):
                e.wait_ge(p.dv.h, p.dv.n)
                mm = nc.tensor.matmul(psrr[:, 0:16], rhosb[:, :],
                                      c["ident"][0:16, 0:16],
                                      start=True, stop=True)
                mm.then_inc(p.pe.h, 1)
            blk.tensor(t2_)
            p.pe.n = pe1 + 1
            def v2_(v):
                v.wait_ge(p.pe.h, p.pe.n)
                v.tensor_copy(o0sb[64:65, :], psrr[0:1, 0:16]).then_inc(p.dv.h, 1)
                p.dv.n += 1
            blk.vector(v2_)
        with p.phase() as blk:
            def _(g_):
                dma(g_, p.da, o0in.ap(), o0sb[:, :])
                dma_wait_all(g_, p.da)
                g_.collective_compute(
                    "AllReduce", ALU.add, replica_groups=GROUPS,
                    ins=[o0in.ap().opt()], outs=[o0out.ap().opt()]).then_inc(p.cc.h, 1)
                p.cc.n += 1
                g_.wait_ge(p.cc.h, p.cc.n)
            blk.gpsimd(_)

    ctxn = p.sbuf("ctxn", [128, 8 * TCORE], BF16)
    with p.scope():
        sb["k0"] = p.sbuf("k0sb", [128, 8], BF16)
        sb["k0p"] = p.sbuf("k0p", [128, 8 * 128], BF16)
        sb["v0st"] = p.sbuf("v0st", [128, 8 * 130], BF16)
        with p.phase() as blk:
            def v_(v):
                v.memset(sb["v0st"][:, :], 0.0).then_inc(p.dv.h, 1)
                p.dv.n += 1
                v.wait_ge(p.dv.h, p.dv.n)
                v.memset(sb["k0p"][:, :], 0.0).then_inc(p.dv.h, 1)
                p.dv.n += 1
            blk.vector(v_)
        with p.phase() as blk:
            def _(g_):
                g_.dma_start(out=sb["k0"][:, :],
                             in_=bass.AP(kvg, 0, [[2048, 128], [262144, 8], [1, 1]]),
                             allow_slow_non_contiguous=True).then_inc(p.da.h, 16)
                p.da.n += 16
                for h2 in range(2):
                    dma(g_, p.da,
                        bass.AP(sb["v0st"], h2 * 65, [[8 * 130, 1], [130, 8], [1, 64]]),
                        bass.AP(kvg, 2048 * 1024 + h2 * 64,
                                [[1, 1], [128, 8], [1, 64]]))
                dma_wait_all(g_, p.da)
            blk.gpsimd(_)
            def v2_(v):
                v.wait_ge(p.da.h, p.da.n)
                v.memset(bass.AP(sb["v0st"], 64, [[8 * 130, 1], [65, 16], [1, 1]]),
                         1.0).then_inc(p.dv.h, 1)
                p.dv.n += 1
                v.wait_ge(p.dv.h, p.dv.n)
                for hp in range(8):
                    v.tensor_copy(sb["k0p"][:, hp * 128:hp * 128 + 1],
                                  sb["k0"][:, hp:hp + 1]).then_inc(p.dv.h, 1)
                    p.dv.n += 1
                    v.wait_ge(p.dv.h, p.dv.n)
            blk.vector(v2_)
        with p.phase() as blk:
            wq2 = load_weight(p, blk, wd["WqT"], D, D, "wq")
        kqs = [p.sbuf(f"kqs{i}_3", [128, 2048], BF16) for i in range(2)]
        vqs = [p.sbuf(f"vqs{i}_3", [128, 16 * 130], BF16) for i in range(2)]
        q_sb = p.sbuf("q_sb3", [128, 512], BF16)
        p_sb = [p.sbuf(f"p_sb3{i}", [128, 1024], BF16) for i in range(2)]
        rrow = p.sbuf("rrow3", [1, 1024], F32)
        sbc = p.sbuf("sbc3", [128, 512], F32)
        ps_q = p.psum("fl_q3", [128, 512], F32)
        ps_s = [p.psum(f"fl_s3{i}", [128, 1024], F32) for i in range(2)]
        ps_av = [p.psum(f"fl_av3{i}", [65, 512], F32) for i in range(2)]
        ps_bc = p.psum("fl_bc3", [128, 512], F32)
        with p.phase() as blk:
            def vinit(v):
                for i in range(2):
                    v.memset(bass.AP(vqs[i], 64,
                                     [[16 * 130, 128], [130, 16], [65, 2], [1, 1]]),
                             1.0)
            blk.vector(vinit)
        with p.phase() as blk:
            variants = [l3_scheds(g) for g in range(4)]
            pe0_, ac0_, dv0_ = p.pe.n, p.ac.n, p.dv.n
            w20 = [p.w2[0].n, p.w2[1].n]
            tots = []
            for vv in variants:
                pe_at, ac_at, dv_at, nkts, st = _l3_iter_tables(vv, 0, 0, 0)
                tots.append((pe_at[-1], ac_at[-1], dv_at[-1], len(st)))
            assert len(set(tots)) == 1, tots

            def t_(e):
                pid = e.partition_id()
                for g in e.Switch(pid % 4, 4):
                    emit_l3_variant_tensor(p, e, variants[g], xl, wq2, kqs, vqs,
                                           q_sb, p_sb, ps_q, ps_s, ps_av, ps_bc,
                                           sb, c, rrow, pe0_, ac0_, dv0_, w20)
            blk.tensor(t_)

            def a_(a):
                pid = a.partition_id()
                for g in a.Switch(pid % 4, 4):
                    emit_l3_variant_scalar(p, a, variants[g], ps_q, ps_s, p_sb,
                                           q_sb, pe0_, ac0_, dv0_)
            blk.scalar(a_)

            def v_(v):
                pid = v.partition_id()
                for g in v.Switch(pid % 4, 4):
                    emit_l3_variant_vector(p, v, variants[g], ctxn, ps_av,
                                           ps_bc, ps_s, rrow, sbc,
                                           pe0_, ac0_, dv0_)
            blk.vector(v_)

            def g_(g_eng):
                pid = g_eng.partition_id()
                for g in g_eng.Switch(pid % 4, 4):
                    emit_l3_variant_gpsimd(p, g_eng, variants[g], kvg, kqs, vqs,
                                           pe0_, w20)
            blk.gpsimd(g_)
            pe_t, ac_t, dv_t, nst = tots[0]
            p.pe.n = pe0_ + pe_t
            p.ac.n = ac0_ + ac_t
            p.dv.n = dv0_ + dv_t
            par_cnt = [0, 0]
            for si in range(nst):
                par_cnt[si % 2] += 6 * 16
            p.w2[0].n = w20[0] + par_cnt[0]
            p.w2[1].n = w20[1] + par_cnt[1]
    with p.scope():
        with p.phase() as blk:
            o0f = p.sbuf("o0f", [65, 16], F32)
            o0r = p.sbuf("o0r", [1, 16], F32)
            def _(g_):
                dma(g_, p.da, o0f[:, :], o0out.ap())
                dma_wait_all(g_, p.da)
            blk.gpsimd(_)
            def v_(v):
                v.wait_ge(p.da.h, p.da.n)
                v.reciprocal(o0r[0:1, :], o0f[64:65, :]).then_inc(p.dv.h, 1)
                p.dv.n += 1
            blk.vector(v_)
            psbr = p.psum("brps3", [64, 512], F32)
            sbr = p.sbuf("sbr3", [64, 16], F32)
            pe0b = p.pe.n
            def t_(e):
                e.wait_ge(p.dv.h, p.dv.n)
                mm = nc.tensor.matmul(psbr[:, 0:16], c["ones1"][0:1, 0:64],
                                      o0r[0:1, :], start=True, stop=True)
                mm.then_inc(p.pe.h, 1)
            blk.tensor(t_)
            p.pe.n = pe0b + 1
            def v2_(v):
                v.wait_ge(p.pe.h, p.pe.n)
                v.tensor_copy(sbr[:, :], psbr[:, 0:16]).then_inc(p.dv.h, 1)
                p.dv.n += 1
                v.wait_ge(p.dv.h, p.dv.n)
                bln = p.sbuf("bln3", [64, 1], F32)
                for h in range(16):
                    hp, hh = h // 2, h % 2
                    col = hp * TCORE
                    dst = ctxn[hh * 64:(hh + 1) * 64, col:col + 1]

                    def stp(x):
                        x.then_inc(p.dv.h, 1)
                        p.dv.n += 1
                        v.wait_ge(p.dv.h, p.dv.n)
                    stp(v.tensor_mul(o0f[0:64, h:h + 1], o0f[0:64, h:h + 1],
                                     sbr[:, h:h + 1]))
                    stp(v.tensor_copy(bln[:, :], dst))
                    stp(v.tensor_scalar_mul(bln[:, :], bln[:, :],
                                            sb["s01"][0:64, 1:2]))
                    stp(v.scalar_tensor_tensor(bln[:, :], o0f[0:64, h:h + 1],
                                               sb["s01"][0:64, 0:1], bln[:, :],
                                               op0=ALU.mult, op1=ALU.add))
                    stp(v.tensor_copy(dst, bln[:, :]))
            blk.vector(v2_)
    y = p.sbuf("y3", [128, 8 * TCORE], BF16)
    with p.scope():
        with p.phase() as blk:
            wo = load_weight(p, blk, wd["WoT"], D, D, "wo")
        with p.phase() as blk:
            emit_proj(p, blk, wo, D, ctxn, TCORE, act_epi(y, TCORE))
    with p.scope():
        emit_postln_residual_streamed(p, io, y, c, row)
    return p.finish()


def host_prep_l3attn(H, inputs):
    f32 = np.float32
    for nm, key in [("WqT", "L3_attn_Wq"), ("WkT", "L3_attn_Wk"),
                    ("WvT", "L3_attn_Wv"), ("WoT", "L3_attn_Wo")]:
        H["l3a_" + nm] = np.ascontiguousarray(
            np.asarray(inputs[key], f32).T).astype(BF)
    ma = np.full((128, 9 * 512), -80000.0, np.float32)
    pp_, jj = np.arange(128)[:, None], np.arange(512)[None, :]
    for a in range(6):
        o = a - 1
        ma[:, a * 512:(a + 1) * 512] = np.where(
            np.abs(jj - o * 128 - pp_) <= 128, 0.0, -80000.0)
    m7 = np.full((128, 512), -80000.0, np.float32)
    m7[0, :] = 0.0
    ma[:, 7 * 512:8 * 512] = m7
    m8 = np.where(np.abs(jj - 0 * 128 - pp_) <= 128, 0.0, -80000.0)
    m8[0, :] = 0.0
    ma[:, 8 * 512:9 * 512] = m8
    H["l3a_marc"] = ma
    H["l3a_s01"] = []
    for cx in range(NCORES):
        s = np.zeros((128, 2), f32)
        own0 = 1.0 if 0 in chunks_of_core(cx) else 0.0
        s[:, 0] = own0
        s[:, 1] = 1.0 - own0
        H["l3a_s01"].append(s)
    return H


def ins_l3_attn(cx, H, inputs, state):
    return {"xin": state["stream"][cx], "ident_f32": H["ident_f32"],
            "ones1_f32": H["ones1_f32"], "constv": H["constv"],
            "WqT": H["l3a_WqT"], "WkT": H["l3a_WkT"], "WvT": H["l3a_WvT"],
            "WoT": H["l3a_WoT"], "marc": H["l3a_marc"],
            "s01": H["l3a_s01"][cx]}




LF_PAIRS = 22
LF_W1N = 2 * LF_PAIRS * 128
LF_W2K = LF_PAIRS * 128


def lf_halo_src(g):
    out = []
    for cg in [g, 7 - g]:
        pc = cg - 1
        out.append(None if pc < 0 else owner_of_chunk(pc))
    return out


def emit_cols_to_row(p, blk, cols, colsT, row, row_off, NT, negate=False):
    ds_prev = p.ds.n

    def vec(v_):
        v_.wait_ge(p.ds.h, ds_prev)
        if negate:
            v_.tensor_scalar_mul(cols[:, 0:NT], cols[:, 0:NT],
                                 -1.0).then_inc(p.dv.h, 1)
            p.dv.n += 1
            v_.wait_ge(p.dv.h, p.dv.n)
        if NT < 32:
            v_.memset(cols[:, NT:32], 0.0).then_inc(p.dv.h, 1)
            p.dv.n += 1
            v_.wait_ge(p.dv.h, p.dv.n)
        v_.transpose(colsT[:, :], cols[:, :]).then_inc(p.dv.h, 1)
        p.dv.n += 1
    blk.vector(vec)

    def sc(s_):
        s_.wait_ge(p.dv.h, p.dv.n)
        rowlen = row.shape[-1]
        for a in range(4):
            src = colsT[32 * a:32 * a + NT, 0:32]
            dst = bass.AP(row, row_off + 32 * a, [[rowlen, 1], [128, NT], [1, 32]])
            s_.dma_start(out=dst, in_=src).then_inc(p.ds.h, 16)
            p.ds.n += 16
        s_.wait_ge(p.ds.h, p.ds.n)
        s_.nop().then_inc(p.ac.h, 1)
        p.ac.n += 1
    blk.scalar(sc)


def emit_l2norm_scale(p, blk, buf, tiles, scol, bd2b, bd2t):
    nc = p.nc
    psn = [p.psum(f"psn{j}", [128, 512], F32) for j in range(2)]
    psb = [p.psum(f"psb{j}", [128, 512], F32) for j in range(2)]
    sq = [p.sbuf(f"nsq{j}", [128, 512], BF16) for j in range(2)]
    rsm = [p.sbuf(f"nrs{j}", [2, 512], F32) for j in range(2)]
    pe0, ac0, dv0 = p.pe.n, p.ac.n, p.dv.n
    n = len(tiles)
    ac_sq = [0] * n
    ac_exp = [0] * n
    pe_psn = [0] * n
    pe_psb = [0] * n
    c = ac0
    for k in (0, 1):
        if k < n:
            c += 1
            ac_sq[k] = c
    for i in range(n):
        c += 2
        ac_exp[i] = c
        if i + 2 < n:
            c += 1
            ac_sq[i + 2] = c
    c = pe0
    for k in (0, 1):
        if k < n:
            c += 1
            pe_psn[k] = c
    for i in range(n):
        c += 1
        pe_psb[i] = c
        if i + 2 < n:
            c += 1
            pe_psn[i + 2] = c

    def a_(a):
        def emit_sq(i):
            a.wait_ge(p.pe.h, max(0, pe_psn[i - 2] if i >= 2
                                  else pe0 - 4 + 2 * i))
            a.activation(sq[i % 2][:, :], buf[:, tiles[i]:tiles[i] + 512],
                         AF.Square).then_inc(p.ac.h, 1)
        for k in (0, 1):
            if k < n:
                emit_sq(k)
        for i in range(n):
            a.wait_ge(p.dv.h, dv0 + 2 * i + 1)
            a.activation(rsm[i % 2][:, :], rsm[i % 2][:, :],
                         AF.Ln).then_inc(p.ac.h, 1)
            a.wait_ge(p.ac.h, ac_exp[i] - 1)
            a.activation(rsm[i % 2][:, :], rsm[i % 2][:, :], AF.Exp,
                         scale=-0.5).then_inc(p.ac.h, 1)
            if i + 2 < n:
                emit_sq(i + 2)
    blk.scalar(a_)

    def t_(e):
        def emit_psn(i):
            e.wait_ge(p.ac.h, ac_sq[i])
            e.wait_ge(p.dv.h, max(0, dv0 + 2 * (i - 2) + 1 if i >= 2
                                  else dv0 - 3 + 2 * i))
            nc.tensor.matmul(psn[i % 2][0:2, :], bd2b[:, :], sq[i % 2][:, :],
                             start=True, stop=True).then_inc(p.pe.h, 1)
        for k in (0, 1):
            if k < n:
                emit_psn(k)
        for i in range(n):
            e.wait_ge(p.ac.h, ac_exp[i])
            e.wait_ge(p.dv.h, max(0, dv0 + 2 * (i - 2) + 2 if i >= 2
                                  else dv0 - 2 + 2 * i))
            nc.tensor.matmul(psb[i % 2][:, :], bd2t[0:2, :], rsm[i % 2][:, :],
                             start=True, stop=True).then_inc(p.pe.h, 1)
            if i + 2 < n:
                emit_psn(i + 2)
    blk.tensor(t_)

    def v_(v):
        for i, c0 in enumerate(tiles):
            v.wait_ge(p.pe.h, pe_psn[i])
            v.tensor_scalar(rsm[i % 2][:, :], psn[i % 2][0:2, :], 1e-24, None,
                            op0=ALU.max).then_inc(p.dv.h, 1)
            v.wait_ge(p.pe.h, pe_psb[i])
            v.scalar_tensor_tensor(buf[:, c0:c0 + 512], buf[:, c0:c0 + 512],
                                   scol, psb[i % 2][:, :], op0=ALU.mult,
                                   op1=ALU.mult).then_inc(p.dv.h, 1)
    blk.vector(v_)
    p.pe.n = pe0 + 2 * n
    p.ac.n = ac0 + 3 * n
    p.dv.n = dv0 + 2 * n


def emit_l1f_stats(p, blk, qb, kb, h, mcols):
    nc = p.nc
    fc, hb = h // 2, (h % 2) * 64
    psA = [p.psum(f"psA{i}", [128, 1024], F32) for i in range(2)]
    pe0, dv0 = p.pe.n, p.dv.n

    def t_(e):
        for i in range(16):
            qw, qs = i // 4, i % 4
            s, wdw = qw // 2, qw % 2
            kbase = fc * 3072 + s * 1536 + wdw * 512
            e.wait_ge(p.dv.h, dv0 + i - 1 if i >= 2 else dv0)
            q0 = fc * 2048 + qw * 512 + qs * 128
            ql = qb[hb:hb + 64, q0:q0 + 128]
            pp = psA[i % 2]
            nc.tensor.matmul(pp[:, 0:512], ql, kb[hb:hb + 64, kbase:kbase + 512],
                             start=True, stop=True)
            nc.tensor.matmul(pp[:, 512:1024], ql,
                             kb[hb:hb + 64, kbase + 512:kbase + 1024],
                             start=True, stop=True).then_inc(p.pe.h, 1)
    blk.tensor(t_)
    p.pe.n = pe0 + 16

    def v_(v):
        for i in range(16):
            v.wait_ge(p.pe.h, pe0 + i + 1)
            v.reduce_max(mcols[:, i:i + 1], psA[i % 2][:, :],
                         axis=AX.X).then_inc(p.dv.h, 1)
    blk.vector(v_)
    p.dv.n = dv0 + 16


def emit_l1f_flash_h(p, blk, qb, kb, vtm, ctx, nmr, atl, pm0, identb, c, h,
                     nmr_off=0):
    nc = p.nc
    fc, hb = h // 2, (h % 2) * 64
    pss = [p.psum(f"pfs{i}", [128, 512], F32) for i in range(2)]
    psr = p.psum("pfr", [128, 512], F32)
    psav = p.psum("pfav", [128, 512], F32)
    psbc = p.psum("pfbc", [128, 512], F32)
    pt = [p.sbuf(f"ptl{i}", [128, 512], BF16) for i in range(2)]
    sctx = p.sbuf("sctx", [128, 512], BF16)
    avs = p.sbuf("avsl", [128, 512], BF16)
    rrow = p.sbuf("rrowl", [1, 512], F32)
    pe0, ac0, dv0 = p.pe.n, p.ac.n, p.dv.n

    def kt_list(qw):
        s, wdw = qw // 2, qw % 2
        out = []
        for o in range(8):
            ktg = s * 12 + wdw * 4 + o
            if o < 4:
                if wdw == 0 and s == 0:
                    msk = pm0[:, o * 512:(o + 1) * 512]
                else:
                    msk = atl[:, (4 + o) * 512:(5 + o) * 512]
            else:
                msk = atl[:, (o - 4) * 512:(o - 3) * 512]
            out.append((ktg, msk))
        return out

    def t_(e):
        for m in range(4):
            qw = m
            pei, aci = pe0 + 17 * m, ac0 + 10 * m
            q0 = fc * 2048 + qw * 512
            qsl = qb[hb:hb + 64, q0:q0 + 512]
            kts = kt_list(qw)

            def score(j):
                e.wait_ge(p.ac.h, aci + j - 1 if j >= 2 else aci - 2)
                ktg, msk = kts[j]
                kcol = fc * 3072 + ktg * 128
                nc.tensor.matmul(pss[j % 2][:, :], kb[hb:hb + 64, kcol:kcol + 128],
                                 qsl, start=True, stop=False)
                nc.tensor.matmul(pss[j % 2][:, :], c["ones1b"][0:1, :],
                                 nmr[0:1, nmr_off + qw * 512:
                                     nmr_off + qw * 512 + 512],
                                 start=False, stop=False)
                nc.tensor.matmul(pss[j % 2][:, :], identb[:, :], msk,
                                 start=False, stop=True).then_inc(p.pe.h, 1)

            def ra(j):
                e.wait_ge(p.ac.h, aci + j + 1)
                if j == 0:
                    e.wait_ge(p.dv.h, dv0 + 2 * m)
                ktg, _ = kts[j]
                nc.tensor.matmul(psr[0:1, :], c["onescb"][:, 0:1], pt[j % 2][:, :],
                                 start=(j == 0), stop=(j == 7))
                nc.tensor.matmul(psav[0:64, :],
                                 vtm[:, ktg * 512 + h * 64:ktg * 512 + h * 64 + 64],
                                 pt[j % 2][:, :],
                                 start=(j == 0), stop=(j == 7)).then_inc(p.pe.h, 1)

            score(0)
            score(1)
            for j in range(2, 8):
                ra(j - 2)
                score(j)
            ra(6)
            ra(7)
            e.wait_ge(p.dv.h, dv0 + 2 * m + 1)
            nc.tensor.matmul(psbc[:, :], c["ones1"][0:1, :], rrow[0:1, :],
                             start=True, stop=True).then_inc(p.pe.h, 1)
    blk.tensor(t_)
    p.pe.n = pe0 + 68

    def a_(a):
        for m in range(4):
            qw = m
            pei, aci = pe0 + 17 * m, ac0 + 10 * m
            for j in range(8):
                a.wait_ge(p.pe.h, pei + (1 if j == 0 else 2 * j))
                a.activation(pt[j % 2][:, :], pss[j % 2][:, :],
                             AF.Exp).then_inc(p.ac.h, 1)
            a.wait_ge(p.pe.h, pei + 16)
            a.activation(avs[0:64, :], psav[0:64, :],
                         AF.Copy).then_inc(p.ac.h, 1)
            a.wait_ge(p.dv.h, dv0 + 2 * m + 2)
            col = fc * 2048 + qw * 512
            a.activation(ctx[hb:hb + 64, col:col + 512], sctx[0:64, :],
                         AF.Copy).then_inc(p.ac.h, 1)
    blk.scalar(a_)
    p.ac.n = ac0 + 40

    def v_(v):
        for m in range(4):
            pei, aci = pe0 + 17 * m, ac0 + 10 * m
            v.wait_ge(p.pe.h, pei + 16)
            v.reciprocal(rrow[0:1, :], psr[0:1, :]).then_inc(p.dv.h, 1)
            v.wait_ge(p.pe.h, pei + 17)
            v.wait_ge(p.ac.h, aci + 9)
            v.tensor_mul(sctx[0:64, :], avs[0:64, :],
                         psbc[0:64, :]).then_inc(p.dv.h, 1)
    blk.vector(v_)
    p.dv.n = dv0 + 8


def emit_proj_stream2(p, blk, wtiled, KTOT, NOUT, xl, x_stride, tts, epilogue,
                      G=2, wdtype=BF16, ttile=512):
    nc = p.nc
    KC = KTOT // 128
    FC = NOUT // 128
    NG = FC // G
    wst = [p.sbuf(f"wst{i}", [128, 8192], wdtype) for i in range(2)]
    ps = [p.psum(f"prs{i}", [128, ttile], F32) for i in range(G)]
    pe0, ac0 = p.pe.n, p.ac.n
    nT = len(tts)
    w20 = [p.w2[0].n, p.w2[1].n]
    stage_w2 = []
    for i in range(NG):
        w20[i % 2] += 16
        stage_w2.append((i % 2, w20[i % 2]))

    def gq(g_):
        for i in range(NG):
            par = i % 2
            if i >= 2:
                g_.wait_ge(p.pe.h, pe0 + (i - 1) * G * nT)
            g_.dma_start(out=wst[par][:, 0:KC * G * 128],
                         in_=wtiled.ap()[i * 128:(i + 1) * 128, :]
                         ).then_inc(p.w2[par].h, 16)
    blk.gpsimd(gq)
    p.w2[0].n, p.w2[1].n = w20

    def tloop(t_):
        ii = 0
        for i in range(NG):
            par, w2t = stage_w2[i]
            t_.wait_ge(p.w2[par].h, w2t)
            w = wst[par]
            for ti in range(nT):
                tt = tts[ti]
                for fi in range(G):
                    if ii >= G:
                        t_.wait_ge(p.ac.h, ac0 + ii - G + 1)
                    pp = ps[fi]
                    for kc in range(KC):
                        mm = nc.tensor.matmul(
                            pp[:, :],
                            w[:, kc * G * 128 + fi * 128:kc * G * 128 + fi * 128 + 128],
                            xl[:, kc * x_stride + tt:kc * x_stride + tt + ttile],
                            start=(kc == 0), stop=(kc == KC - 1))
                    mm.then_inc(p.pe.h, 1)
                    ii += 1
    blk.tensor(tloop)
    p.pe.n = pe0 + NG * nT * G

    def aloop(a_):
        ii = 0
        for i in range(NG):
            for ti in range(nT):
                for fi in range(G):
                    a_.wait_ge(p.pe.h, pe0 + ii + 1)
                    epilogue(a_, ps[fi], i * G + fi, ti).then_inc(p.ac.h, 1)
                    ii += 1
    blk.scalar(aloop)
    p.ac.n = ac0 + NG * nT * G


def build_l1_ffn():
    p = Prog()
    nc = p.nc
    io = stream_io_tensors(p)
    wd = {}
    for l in range(2):
        wd[f"wqkT{l}"] = nc.dram_tensor(f"wqkT{l}", [D, 1024], BF16,
                                        kind="ExternalInput")
        wd[f"wvT{l}"] = nc.dram_tensor(f"wvT{l}", [D, 512], BF16,
                                       kind="ExternalInput")
        wd[f"woT{l}"] = nc.dram_tensor(f"woT{l}", [512, 1024], BF16,
                                       kind="ExternalInput")
        wd[f"w1t{l}"] = nc.dram_tensor(f"w1t{l}", [LF_PAIRS * 128, 8 * 2 * 128],
                                       BF16, kind="ExternalInput")
        wd[f"w2t{l}"] = nc.dram_tensor(f"w2t{l}", [4 * 128, LF_PAIRS * 2 * 128],
                                       BF16, kind="ExternalInput")
    atl_d = nc.dram_tensor("atl", [128, 9 * 512], F32, kind="ExternalInput")
    pm0_d = nc.dram_tensor("pm0", [128, 4 * 512], F32, kind="ExternalInput")
    qtab_d = nc.dram_tensor("qtab", [32, 2 * TCORE], F32, kind="ExternalInput")
    ktab_d = nc.dram_tensor("ktab", [32, 2 * 3072], F32, kind="ExternalInput")
    scc_d = nc.dram_tensor("scc", [128, 4], F32, kind="ExternalInput")
    bd_d = nc.dram_tensor("bd", [128, 2], F32, kind="ExternalInput")
    bdt_d = nc.dram_tensor("bdt", [2, 128], F32, kind="ExternalInput")
    lnc_d = nc.dram_tensor("lnc", [128, 48], F32, kind="ExternalInput")
    b1c_d = nc.dram_tensor("b1c", [128, 88], F32, kind="ExternalInput")
    hin1 = nc.dram_tensor("hin1", [1024, 2 * 513], BF16)
    hg1 = nc.dram_tensor("hg1", [4096, 2 * 513], BF16)
    hin2 = nc.dram_tensor("hin2", [1024, 2], BF16)
    hg2 = nc.dram_tensor("hg2", [4096, 2], BF16)

    row = p.sbuf("row", [1, 3 * TCORE], F32)
    x_int = p.sbuf("x_int", [128, 8 * TCORE], BF16)

    with p.phase() as blk:
        c = load_consts(p, blk, io)
        scc = p.sbuf("sccs", [128, 4], F32)
        bdf = p.sbuf("bdf", [128, 2], F32)
        bd2t = p.sbuf("bd2t", [2, 128], F32)
        lnsb = p.sbuf("lnsb", [128, 48], F32)
        b1sb = p.sbuf("b1sb", [128, 88], F32)

        def _(g):
            for dst, src in [(scc, scc_d), (bdf, bd_d), (bd2t, bdt_d),
                             (lnsb, lnc_d), (b1sb, b1c_d)]:
                dma(g, p.da, dst[:, :], src[:, :])
            dma_wait_all(g, p.da)
        blk.gpsimd(_)
        identb = p.sbuf("identb", [128, 128], BF16)
        bd2b = p.sbuf("bd2b", [128, 2], BF16)

        def v0(v):
            v.wait_ge(p.da.h, p.da.n)
            v.tensor_copy(identb[:, :], c["ident"][:, :])
            v.tensor_copy(bd2b[:, :], bdf[:, :])
        blk.vector(v0)

    with p.scope():
        xb16 = p.sbuf("xb16", [128, 8 * TCORE], BF16)
        with p.phase() as blk:
            emit_load_xl(p, blk, io, xb16)
        with p.phase() as blk:
            emit_sumsq_row(p, blk, xb16, TCORE, c["ident"], row, 0)
        with p.phase() as blk:
            emit_row_rstd(p, blk, row, 0, 0, TCORE)
        with p.phase() as blk:
            pb = [p.psum(f"lnb{i}", [128, 512], F32) for i in range(2)]
            pe0, dv0 = p.pe.n, p.dv.n

            def t_(e):
                for si in range(4):
                    if si >= 2:
                        e.wait_ge(p.dv.h, dv0 + si - 1)
                    mm = nc.tensor.matmul(pb[si % 2][:, :], c["ones1"][0:1, :],
                                          row[0:1, si * 512:si * 512 + 512],
                                          start=True, stop=True)
                    mm.then_inc(p.pe.h, 1)
            blk.tensor(t_)
            p.pe.n = pe0 + 4

            def v_(v):
                for si in range(4):
                    v.wait_ge(p.pe.h, pe0 + si + 1)
                    last = None
                    for kc in range(8):
                        sl = slice(kc * TCORE + si * 512,
                                   kc * TCORE + si * 512 + 512)
                        last = v.tensor_mul(x_int[:, sl], xb16[:, sl],
                                            pb[si % 2][:, :])
                    last.then_inc(p.dv.h, 1)
            blk.vector(v_)
            p.dv.n = dv0 + 4

    for l in range(2):
        with p.scope():
            qb = p.sbuf("qb", [128, 4 * 2048], BF16)
            kb = p.sbuf("kb", [128, 4 * 3072], BF16)
            vtm = p.sbuf("vtml", [128, 24 * 512], BF16)
            col0 = p.sbuf("col0", [128, 16], BF16)
            hh = [p.sbuf(f"hh{s}", [128, 8 * 512], BF16) for s in range(2)]

            with p.phase() as blk:
                def g_(ge):
                    for s in range(2):
                        ge.dma_start(
                            out=hin1.ap()[:, s * 513:(s + 1) * 513]
                            .rearrange("(dc p) t -> p dc t", p=128),
                            in_=bass.AP(x_int, s * 1024 + 511,
                                        [[8 * TCORE, 128], [TCORE, 8], [1, 513]])
                        ).then_inc(p.da.h, 16)
                    p.da.n += 32
                    ge.wait_ge(p.da.h, p.da.n)
                    ge.collective_compute(
                        "AllGather", ALU.bypass, replica_groups=GROUPS,
                        ins=[hin1.ap().opt()], outs=[hg1.ap().opt()]
                    ).then_inc(p.cc.h, 1)
                    p.cc.n += 1
                    ge.wait_ge(p.cc.h, p.cc.n)
                blk.gpsimd(g_)

            with p.scope():
                hx = p.sbuf("hx", [128, 8 * 1026], BF16)
                with p.phase() as blk:
                    da0 = p.da.n

                    def g_(ge):
                        pid = ge.partition_id()
                        for g in ge.Switch(pid % 4, 4):
                            srcs = lf_halo_src(g)
                            for s in range(2):
                                r, sp = srcs[s] or (0, 0)
                                ge.dma_start(
                                    out=bass.AP(hx, s * 513,
                                                [[8 * 1026, 128], [1026, 8],
                                                 [1, 513]]),
                                    in_=hg1.ap()[r * 1024:(r + 1) * 1024,
                                                 sp * 513:(sp + 1) * 513]
                                    .rearrange("(dc p) t -> p dc t", p=128)
                                ).then_inc(p.da.h, 16)
                        ge.wait_ge(p.da.h, da0 + 32)
                    blk.gpsimd(g_)
                    p.da.n = da0 + 32
                with p.phase() as blk:
                    def v_(v):
                        pid = v.partition_id()
                        for g in v.Switch(pid % 4, 4):
                            if g == 0:
                                for dc in range(7):
                                    v.memset(hx[:, dc * 1026:dc * 1026 + 513],
                                             0.0)
                                v.memset(hx[:, 7 * 1026:7 * 1026 + 513],
                                         0.0).then_inc(p.dv.h, 1)
                            else:
                                v.nop().then_inc(p.dv.h, 1)
                        p.dv.n += 1
                        v.wait_ge(p.dv.h, p.dv.n)
                        for dc in range(4, 8):
                            for s in range(2):
                                v.tensor_copy(
                                    col0[:, dc * 2 + s:dc * 2 + s + 1],
                                    hx[:, dc * 1026 + s * 513 + 512:
                                       dc * 1026 + s * 513 + 513])
                        last = None
                        for s in range(2):
                            for dc in range(8):
                                off = 1 if dc < 4 else 0
                                o0 = dc * 1026 + s * 513 + off
                                last = v.tensor_copy(
                                    hh[s][:, dc * 512:(dc + 1) * 512],
                                    hx[:, o0:o0 + 512])
                        last.then_inc(p.dv.h, 1)
                        p.dv.n += 1
                    blk.vector(v_)

            with p.scope():
                with p.phase() as blk:
                    wqk = load_weight(p, blk, wd[f"wqkT{l}"], D, 1024, "wqk")
                    wv = load_weight(p, blk, wd[f"wvT{l}"], D, 512, "wvq")
                hsl = p.sbuf("hsl", [128, 8 * 1024], BF16)
                for s in range(2):
                    with p.phase() as blk:
                        def v_(v, s=s):
                            last = None
                            for dc in range(8):
                                xb = dc * TCORE + s * 1024
                                hb_ = dc * 1024
                                if dc < 4:
                                    last = v.tensor_copy(
                                        hsl[:, hb_:hb_ + 1024],
                                        x_int[:, xb:xb + 1024])
                                else:
                                    last = v.tensor_copy(
                                        hsl[:, hb_ + 1:hb_ + 1024],
                                        x_int[:, xb:xb + 1023])
                                    last = v.tensor_copy(
                                        hsl[:, hb_:hb_ + 1],
                                        col0[:, dc * 2 + s:dc * 2 + s + 1])
                            last.then_inc(p.dv.h, 1)
                            p.dv.n += 1
                        blk.vector(v_)

                    def epi_qk(s):
                        def epi(a_, psum, fc, t0):
                            if fc < 4:
                                d0 = fc * 2048 + s * 1024 + t0
                                dst = qb[:, d0:d0 + 512]
                            else:
                                d0 = (fc - 4) * 3072 + s * 1536 + 512 + t0
                                dst = kb[:, d0:d0 + 512]
                            return a_.activation(dst, psum[:, :], AF.Copy)
                        return epi

                    def epi_kh(s):
                        def epi(a_, psum, fc, t0):
                            d0 = (fc - 4) * 3072 + s * 1536 + t0
                            return a_.activation(kb[:, d0:d0 + 512],
                                                 psum[:, :], AF.Copy)
                        return epi

                    with p.phase() as blk:
                        emit_proj(p, blk, wqk, 1024, hsl, 1024, epi_qk(s),
                                  nk=8, tokens=1024)
                    with p.phase() as blk:
                        emit_proj(p, blk, wqk, 1024, hh[s], 512, epi_kh(s),
                                  nk=8, tokens=512, fclist=[4, 5, 6, 7])
                    with p.phase() as blk:
                        emit_proj_tm(p, blk, wv, 512, hsl, 1024, vtm,
                                     out_col0=(s * 12 + 4) * 512)
                    with p.phase() as blk:
                        emit_proj_tm(p, blk, wv, 512, hh[s], 512, vtm,
                                     out_col0=(s * 12) * 512)

            with p.scope():
                with p.phase() as blk:
                    emit_l2norm_scale(
                        p, blk, qb,
                        [fc * 2048 + t * 512 for fc in range(4)
                         for t in range(4)],
                        scc[:, 2 * l:2 * l + 1], bd2b, bd2t)
                    emit_l2norm_scale(
                        p, blk, kb,
                        [fc * 3072 + t * 512 for fc in range(4)
                         for t in range(6)],
                        scc[:, 2 * l + 1:2 * l + 2], bd2b, bd2t)

            with p.scope():
                csq = p.sbuf("csq", [32, TCORE], BF16)
                ssq = p.sbuf("ssq", [32, TCORE], BF16)
                csk = p.sbuf("csk", [32, 3072], BF16)
                ssk = p.sbuf("ssk", [32, 3072], BF16)
                with p.phase() as blk:
                    def _(g):
                        dma(g, p.da, csq[:, :], qtab_d[:, 0:TCORE])
                        dma(g, p.da, ssq[:, :], qtab_d[:, TCORE:2 * TCORE])
                        dma(g, p.da, csk[:, :], ktab_d[:, 0:3072])
                        dma(g, p.da, ssk[:, :], ktab_d[:, 3072:2 * 3072])
                        dma_wait_all(g, p.da)
                    blk.gpsimd(_)
                stg = p.sbuf("rstg", [32, 4 * 512], BF16)
                stgg = p.sbuf("rstgg", [32, 4 * 512], BF16)
                rtiles = []
                for buf, nt, stride, ct, st in [(qb, 4, 2048, csq, ssq),
                                                (kb, 6, 3072, csk, ssk)]:
                    for fc in range(4):
                        for tt in range(nt):
                            rtiles.append((buf, fc * stride + tt * 512,
                                           ct, st, tt * 512))
                vt = [t for i, t in enumerate(rtiles) if i % 5 != 4]
                gt = [t for i, t in enumerate(rtiles) if i % 5 == 4]
                with p.phase() as blk:
                    def mkloop(tiles, mystg, sem, tmp_name):
                        def run(eng):
                            for buf, col, ct, st, t0 in tiles:
                                cp = None
                                for k4 in range(4):
                                    cp = eng.tensor_copy(
                                        mystg[0:32, k4 * 512:(k4 + 1) * 512],
                                        buf[k4 * 32:(k4 + 1) * 32,
                                            col:col + 512])
                                cp.then_inc(sem.h, 1)
                                sem.n += 1
                                eng.wait_ge(sem.h, sem.n)
                                emit_rope_pair(p, eng, mystg, buf, col,
                                               ct, st, t0, 512, sem=sem,
                                               tmp_name=tmp_name)
                        return run
                    blk.vector(mkloop(vt, stg, p.dv, "ropetmp"))
                    blk.gpsimd(mkloop(gt, stgg, p.po, "ropetmpg"))

            with p.scope():
                ctx = p.sbuf("ctxl", [128, 4 * 2048], BF16)
                with p.scope():
                    atl = p.sbuf("atl", [128, 9 * 512], BF16)
                    pm0 = p.sbuf("pm0", [128, 4 * 512], BF16)
                    nmr = p.sbuf("nmr", [1, 8 * 2048], BF16)
                    mcols = p.sbuf("mcols", [128, 32], BF16)
                    mcolsT = p.sbuf("mcolsT", [128, 32], BF16)
                    with p.phase() as blk:
                        def _(g):
                            dma(g, p.da, atl[:, :], atl_d[:, :])
                            dma(g, p.da, pm0[:, :], pm0_d[:, :])
                            dma_wait_all(g, p.da)
                        blk.gpsimd(_)
                    with p.scope():
                        with p.phase() as blk:
                            for h in range(8):
                                emit_l1f_stats(p, blk, qb, kb, h, mcols)
                                emit_cols_to_row(p, blk, mcols, mcolsT, nmr,
                                                 h * 2048, 16, negate=True)
                    with p.phase() as blk:
                        for h in range(8):
                            emit_l1f_flash_h(p, blk, qb, kb, vtm, ctx, nmr,
                                             atl, pm0, identb, c, h,
                                             nmr_off=h * 2048)

                with p.scope():
                    with p.phase() as blk:
                        wo = load_weight(p, blk, wd[f"woT{l}"], 512, 1024,
                                         "wol")
                    ytm = p.sbuf("ytm", [128, 8 * TCORE], BF16)
                    with p.phase() as blk:
                        emit_proj(p, blk, wo, 1024, ctx, TCORE,
                                  act_epi(ytm, TCORE), nk=4)
                    with p.phase() as blk:
                        emit_mul_pass(p, blk, x_int, x_int, ytm, 32, 512,
                                      op=ALU.add)

        with p.scope():
            with p.phase() as blk:
                def g_(ge):
                    for s in range(2):
                        ge.dma_start(
                            out=hin2.ap()[:, s:s + 1]
                            .rearrange("(dc p) o -> p dc o", p=128),
                            in_=bass.AP(x_int, s * 1024 + 1023,
                                        [[8 * TCORE, 128], [TCORE, 8], [1, 1]]),
                            allow_slow_non_contiguous=True
                        ).then_inc(p.da.h, 16)
                    p.da.n += 32
                    ge.wait_ge(p.da.h, p.da.n)
                    ge.collective_compute(
                        "AllGather", ALU.bypass, replica_groups=GROUPS,
                        ins=[hin2.ap().opt()], outs=[hg2.ap().opt()]
                    ).then_inc(p.cc.h, 1)
                    p.cc.n += 1
                    ge.wait_ge(p.cc.h, p.cc.n)
                blk.gpsimd(g_)
            hx2 = p.sbuf("hx2", [128, 16], BF16)
            with p.phase() as blk:
                da0 = p.da.n

                def g_(ge):
                    pid = ge.partition_id()
                    for g in ge.Switch(pid % 4, 4):
                        srcs = lf_halo_src(g)
                        for s in range(2):
                            r, sp = srcs[s] or (0, 0)
                            ge.dma_start(
                                out=bass.AP(hx2, s, [[16, 128], [2, 8], [1, 1]]),
                                in_=hg2.ap()[r * 1024:(r + 1) * 1024, sp:sp + 1]
                                .rearrange("(dc p) o -> p dc o", p=128),
                                allow_slow_non_contiguous=True
                            ).then_inc(p.da.h, 16)
                    ge.wait_ge(p.da.h, da0 + 32)
                blk.gpsimd(g_)
                p.da.n = da0 + 32
            h2 = p.sbuf("h2", [128, 8 * TCORE], BF16)
            with p.phase() as blk:
                def v_(v):
                    pid = v.partition_id()
                    for g in v.Switch(pid % 4, 4):
                        if g == 0:
                            for dc in range(7):
                                v.memset(hx2[:, dc * 2:dc * 2 + 1], 0.0)
                            v.memset(hx2[:, 14:15], 0.0).then_inc(p.dv.h, 1)
                        else:
                            v.nop().then_inc(p.dv.h, 1)
                    p.dv.n += 1
                    v.wait_ge(p.dv.h, p.dv.n)
                    last = None
                    for s in range(2):
                        for dc in range(8):
                            base = dc * TCORE + s * 1024
                            if dc < 4:
                                last = v.tensor_copy(h2[:, base:base + 1024],
                                                     x_int[:, base:base + 1024])
                            else:
                                last = v.tensor_copy(
                                    h2[:, base + 1:base + 1024],
                                    x_int[:, base:base + 1023])
                                last = v.tensor_copy(
                                    h2[:, base:base + 1],
                                    hx2[:, dc * 2 + s:dc * 2 + s + 1])
                    last.then_inc(p.dv.h, 1)
                    p.dv.n += 1
                blk.vector(v_)

            o_ssq, o_mu, o_rstd = 0, TCORE, 2 * TCORE
            with p.phase() as blk:
                emit_sumsq_row(p, blk, h2, TCORE, c["ident"], row, o_ssq)
            with p.phase() as blk:
                pbm = [p.psum(f"mubh{i}", [1, 512], F32) for i in range(2)]
                pe0, dv0 = p.pe.n, p.dv.n

                def t_(e):
                    for si in range(4):
                        if si >= 2:
                            e.wait_ge(p.dv.h, dv0 + si - 1)
                        for kc in range(8):
                            mm = nc.tensor.matmul(
                                pbm[si % 2][:, :], c["onescb"][:, 0:1],
                                h2[:, kc * TCORE + si * 512:
                                   kc * TCORE + si * 512 + 512],
                                start=(kc == 0), stop=(kc == 7))
                        mm.then_inc(p.pe.h, 1)
                blk.tensor(t_)
                p.pe.n = pe0 + 4

                def v_(v):
                    for si in range(4):
                        v.wait_ge(p.pe.h, pe0 + si + 1)
                        v.tensor_scalar_mul(
                            row[0:1, o_mu + si * 512:o_mu + si * 512 + 512],
                            pbm[si % 2][0:1, :], 1.0 / D).then_inc(p.dv.h, 1)
                blk.vector(v_)
                p.dv.n = dv0 + 4
            with p.phase() as blk:
                def s_(a):
                    a.activation(row[0:1, o_rstd:o_rstd + TCORE],
                                 row[0:1, o_mu:o_mu + TCORE],
                                 AF.Square).then_inc(p.ac.h, 1)
                    p.ac.n += 1
                blk.scalar(s_)

                def v_(v):
                    v.wait_ge(p.ac.h, p.ac.n)
                    v.tensor_scalar(row[0:1, o_ssq:o_ssq + TCORE],
                                    row[0:1, o_ssq:o_ssq + TCORE],
                                    1.0 / D, None,
                                    op0=ALU.mult).then_inc(p.dv.h, 1)
                    p.dv.n += 1
                    v.wait_ge(p.dv.h, p.dv.n)
                    v.tensor_sub(row[0:1, o_ssq:o_ssq + TCORE],
                                 row[0:1, o_ssq:o_ssq + TCORE],
                                 row[0:1, o_rstd:o_rstd + TCORE]
                                 ).then_inc(p.dv.h, 1)
                    p.dv.n += 1
                blk.vector(v_)

                def s2_(a):
                    a.wait_ge(p.dv.h, p.dv.n)
                    a.activation(row[0:1, o_rstd:o_rstd + TCORE],
                                 row[0:1, o_ssq:o_ssq + TCORE],
                                 AF.Ln, bias=EPS).then_inc(p.ac.h, 1)
                    p.ac.n += 1
                    a.wait_ge(p.ac.h, p.ac.n)
                    a.activation(row[0:1, o_rstd:o_rstd + TCORE],
                                 row[0:1, o_rstd:o_rstd + TCORE],
                                 AF.Exp, scale=-0.5).then_inc(p.ac.h, 1)
                    p.ac.n += 1
                blk.scalar(s2_)

                def v2_(v):
                    v.wait_ge(p.ac.h, p.ac.n)
                    v.tensor_mul(row[0:1, o_mu:o_mu + TCORE],
                                 row[0:1, o_mu:o_mu + TCORE],
                                 row[0:1, o_rstd:o_rstd + TCORE]
                                 ).then_inc(p.dv.h, 1)
                    p.dv.n += 1
                blk.vector(v2_)
            with p.phase() as blk:
                ps_r = p.psum("psr2", [128, 512], F32)
                ps_m = p.psum("psm2", [128, 512], F32)
                pe0, dv0 = p.pe.n, p.dv.n

                def t_(e):
                    for si in range(4):
                        if si:
                            e.wait_ge(p.dv.h, dv0 + 25 * si)
                        mm = nc.tensor.matmul(
                            ps_r[:, :], c["ones1"][0:1, :],
                            row[0:1, o_rstd + si * 512:o_rstd + si * 512 + 512],
                            start=True, stop=True)
                        mm = nc.tensor.matmul(
                            ps_m[:, :], c["ones1"][0:1, :],
                            row[0:1, o_mu + si * 512:o_mu + si * 512 + 512],
                            start=True, stop=True)
                        mm.then_inc(p.pe.h, 1)
                blk.tensor(t_)
                p.pe.n = pe0 + 4

                def v_(v):
                    for si in range(4):
                        v.wait_ge(p.pe.h, pe0 + si + 1)
                        for kc in range(8):
                            sl = slice(kc * TCORE + si * 512,
                                       kc * TCORE + si * 512 + 512)
                            v.tensor_mul(h2[:, sl], h2[:, sl],
                                         ps_r[:, :]).then_inc(p.dv.h, 1)
                            p.dv.n += 1
                            v.wait_ge(p.dv.h, p.dv.n)
                            v.tensor_sub(h2[:, sl], h2[:, sl],
                                         ps_m[:, :]).then_inc(p.dv.h, 1)
                            p.dv.n += 1
                            v.wait_ge(p.dv.h, p.dv.n)
                            v.tensor_scalar(
                                h2[:, sl], h2[:, sl],
                                lnsb[:, l * 24 + kc:l * 24 + kc + 1],
                                lnsb[:, l * 24 + 8 + kc:l * 24 + 8 + kc + 1],
                                op0=ALU.mult, op1=ALU.add).then_inc(p.dv.h, 1)
                            p.dv.n += 1
                            v.wait_ge(p.dv.h, p.dv.n)
                        v.nop().then_inc(p.dv.h, 1)
                        p.dv.n += 1
                blk.vector(v_)

            for th in range(2):
                with p.scope():
                    hq = p.sbuf("hq", [128, LF_PAIRS * 1024], BF16)
                    ytm2 = p.sbuf("ytm2", [128, 8 * 1024], BF16)
                    asm = [p.sbuf(f"asm{i}", [128, 512], BF16)
                           for i in range(2)]
                    gsm = [p.sbuf(f"gsm{i}", [128, 512], BF16)
                           for i in range(2)]
                    with p.phase() as blk:
                        acW0, dvW0 = p.ac.n, p.dv.n

                        def epi_geglu(a_, psum, gi, ti):
                            m = (gi // 2) * 2 + ti
                            par = m % 2
                            bias = b1sb[:, l * 44 + gi:l * 44 + gi + 1]
                            if gi % 2 == 0:
                                if m >= 2:
                                    a_.wait_ge(p.dv.h, dvW0 + m - 1)
                                return a_.activation(asm[par][:, :],
                                                     psum[:, :], AF.Identity,
                                                     bias=bias)
                            return a_.activation(gsm[par][:, :], psum[:, :],
                                                 AF.Gelu, bias=bias)

                        emit_proj_stream2(p, blk, wd[f"w1t{l}"], D, LF_W1N,
                                          h2, TCORE,
                                          [th * 1024, th * 1024 + 512],
                                          epi_geglu, G=2)

                        def vsec(v):
                            for m in range(2 * LF_PAIRS):
                                i, ti = m // 2, m % 2
                                v.wait_ge(p.ac.h, acW0 + 2 * m + 2)
                                v.tensor_mul(
                                    hq[:, i * 1024 + ti * 512:
                                       i * 1024 + ti * 512 + 512],
                                    asm[m % 2][:, :],
                                    gsm[m % 2][:, :]).then_inc(p.dv.h, 1)
                        blk.vector(vsec)
                        p.dv.n = dvW0 + 2 * LF_PAIRS
                    with p.phase() as blk:
                        def epi_w2(a_, psum, fc, ti):
                            d0 = fc * 1024 + ti * 512
                            return a_.activation(
                                ytm2[:, d0:d0 + 512], psum[:, :], AF.Identity,
                                bias=lnsb[:, l * 24 + 16 + fc:
                                          l * 24 + 16 + fc + 1])
                        emit_proj_stream2(p, blk, wd[f"w2t{l}"], LF_W2K, 1024,
                                          hq, 1024, [0, 512], epi_w2, G=2)
                    with p.phase() as blk:
                        def v_(v, th=th):
                            last = None
                            for dc in range(8):
                                for ti in range(2):
                                    xsl = slice(
                                        dc * TCORE + th * 1024 + ti * 512,
                                        dc * TCORE + th * 1024 + ti * 512 + 512)
                                    ysl = slice(dc * 1024 + ti * 512,
                                                dc * 1024 + ti * 512 + 512)
                                    last = v.tensor_tensor(
                                        x_int[:, xsl], x_int[:, xsl],
                                        ytm2[:, ysl], op=ALU.add)
                            last.then_inc(p.dv.h, 1)
                            p.dv.n += 1
                        blk.vector(v_)

    with p.scope():
        emit_postln_residual_streamed(p, io, x_int, c, row)
    return p.finish()


def _xpos_tabs(pos):
    hi = np.arange(0, 64, 2, dtype=np.float64)
    invf = 1.0 / 10000.0 ** (hi / 64.0)
    fr = pos[None, :].astype(np.float64) * invf[:, None]
    base = (hi + 0.4 * 64.0) / (1.4 * 64.0)
    power = (pos.astype(np.float64) - 4096.0) / 256.0
    scale = base[:, None] ** power[None, :]
    return np.cos(fr), np.sin(fr), scale


def host_prep_l1ffn(H, inputs):
    f32 = np.float32
    wqkv = np.asarray(inputs["L1_ffn_wqkv"], f32)
    wo = np.asarray(inputs["L1_ffn_wo"], f32)
    w1 = np.asarray(inputs["L1_ffn_w1"], f32)
    b1 = np.asarray(inputs["L1_ffn_b1"], f32)
    w2 = np.asarray(inputs["L1_ffn_w2"], f32)
    b2 = np.asarray(inputs["L1_ffn_b2"], f32)
    lng = np.asarray(inputs["L1_ffn_ln_g"], f32)
    lnb = np.asarray(inputs["L1_ffn_ln_b"], f32)
    qs = np.asarray(inputs["L1_ffn_q_scale"], f32)
    ks = np.asarray(inputs["L1_ffn_k_scale"], f32)
    HID = w2.shape[1]
    NPAD = LF_PAIRS * 128
    H["lf_b1c"] = np.zeros((128, 88), f32)
    H["lf_lnc"] = np.zeros((128, 48), f32)
    for l in range(2):
        H[f"lf_wqkT{l}"] = np.ascontiguousarray(wqkv[l][:, 0:1024]).astype(BF)
        H[f"lf_wvT{l}"] = np.ascontiguousarray(wqkv[l][:, 1024:1536]).astype(BF)
        H[f"lf_woT{l}"] = np.ascontiguousarray(wo[l]).astype(BF)
        w1p = np.zeros((1024, 2, NPAD), f32)
        w1p[:, 0, 0:HID] = w1[l][:, 0:HID]
        w1p[:, 1, 0:HID] = w1[l][:, HID:]
        w1i = w1p.reshape(1024, 2, LF_PAIRS, 128).transpose(0, 2, 1, 3) \
                 .reshape(1024, LF_W1N)
        H[f"lf_w1t{l}"] = host_tile_weight(
            np.ascontiguousarray(w1i)).astype(BF)
        w2p = np.zeros((LF_W2K, 1024), f32)
        w2p[0:HID, :] = w2[l]
        H[f"lf_w2t{l}"] = host_tile_weight(w2p).astype(BF)
        b1p = np.zeros((2, NPAD), f32)
        b1p[0, 0:HID] = b1[l][0:HID]
        b1p[1, 0:HID] = b1[l][HID:]
        bcol = b1p.reshape(2, LF_PAIRS, 128).transpose(1, 0, 2) \
                  .reshape(2 * LF_PAIRS, 128).T
        H["lf_b1c"][:, l * 44:(l + 1) * 44] = bcol
        H["lf_lnc"][:, l * 24:l * 24 + 8] = lng[l].reshape(8, 128).T
        H["lf_lnc"][:, l * 24 + 8:l * 24 + 16] = lnb[l].reshape(8, 128).T
        H["lf_lnc"][:, l * 24 + 16:l * 24 + 24] = b2[l].reshape(8, 128).T
    p64 = np.arange(128) % 64
    scc = np.zeros((128, 4), f32)
    for l in range(2):
        scc[:, 2 * l] = qs[l][p64]
        scc[:, 2 * l + 1] = ks[l][p64]
    H["lf_scc"] = scc
    bd = np.zeros((128, 2), f32)
    bd[0:64, 0] = 1.0
    bd[64:128, 1] = 1.0
    H["lf_bd"] = bd
    H["lf_bdt"] = np.ascontiguousarray(bd.T)
    atl = np.full((128, 9, 512), -30000.0, f32)
    pcol = np.arange(128)[:, None]
    jrow = np.arange(512)[None, :]
    for o in range(4):
        atl[:, o][jrow >= 128 * o + pcol] = 0.0
        atl[:, 4 + o][jrow <= 128 * o + pcol] = 0.0
    H["lf_atl"] = np.ascontiguousarray(atl.reshape(128, 9 * 512))
    H["lf_pm0"] = []
    H["lf_qtab"] = []
    H["lf_ktab"] = []
    for cidx in range(NCORES):
        g = cidx % 4
        if g == 0:
            pm = np.full((128, 4 * 512), -30000.0, f32)
        else:
            pm = H["lf_atl"][:, 4 * 512:8 * 512].copy()
        H["lf_pm0"].append(np.ascontiguousarray(pm))
        chs = chunks_of_core(cidx)
        qpos = np.concatenate([cs * 1024 + np.arange(1024) for cs in chs])
        kparts = []
        for cs in chs:
            kparts.append(cs * 1024 - 512 + np.arange(512))
            kparts.append(cs * 1024 + np.arange(1024))
        kpos = np.concatenate(kparts)
        cq, sq_, scq = _xpos_tabs(qpos)
        ck, sk_, sck = _xpos_tabs(kpos)
        H["lf_qtab"].append(np.ascontiguousarray(
            np.concatenate([cq * scq * 8.0, sq_ * scq * 8.0],
                           axis=1)).astype(f32))
        H["lf_ktab"].append(np.ascontiguousarray(
            np.concatenate([ck / sck, sk_ / sck], axis=1)).astype(f32))
    return H


def ins_l1_ffn(cx, H, inputs, state):
    m = {"xin": state["stream"][cx], "ident_f32": H["ident_f32"],
         "ones1_f32": H["ones1_f32"], "constv": H["constv"],
         "atl": H["lf_atl"], "pm0": H["lf_pm0"][cx],
         "qtab": H["lf_qtab"][cx], "ktab": H["lf_ktab"][cx],
         "scc": H["lf_scc"], "bd": H["lf_bd"], "bdt": H["lf_bdt"],
         "lnc": H["lf_lnc"], "b1c": H["lf_b1c"]}
    for l in range(2):
        for nm in ["wqkT", "wvT", "woT", "w1t", "w2t"]:
            m[f"{nm}{l}"] = H[f"lf_{nm}{l}"]
    return m


BLOCK_LAUNCHES = []
HOST_PREPS = []
BLOCK_LAUNCHES.insert(0, ("l0_attn", build_l0_attn, ins_l0_attn))
HOST_PREPS.append(host_prep_l0attn)
BLOCK_LAUNCHES.insert(1, ("l0_ffn", build_l0_ffn, ins_l0_ffn))
HOST_PREPS.append(host_prep_l0ffn)
BLOCK_LAUNCHES.insert(2, ("l1_attn", build_l1_attn, ins_l1_attn))
HOST_PREPS.append(host_prep_l1attn)
BLOCK_LAUNCHES.insert(3, ("l1_ffn", build_l1_ffn, ins_l1_ffn))
HOST_PREPS.append(host_prep_l1ffn)
BLOCK_LAUNCHES.append(("l2_attn", build_l2_attn, ins_l2_attn))
HOST_PREPS.append(host_prep_l2attn)
BLOCK_LAUNCHES.append(("l2_ffn", build_l2_ffn, ins_l2_ffn))
HOST_PREPS.append(host_prep_l2ffn)
BLOCK_LAUNCHES.append(("l3_attn", build_l3_attn, ins_l3_attn))
HOST_PREPS.append(host_prep_l3attn)
BLOCK_LAUNCHES.append(("l3_ffn", build_l3_ffn, ins_l3_ffn))
HOST_PREPS.append(host_prep_l3ffn)

def get_launches():
    return [("prep", build_prep, ins_prep)] + BLOCK_LAUNCHES + \
        [("final", build_final, ins_final)]



_BLOCK_PREFS = ["l0a_", "l0f_", "l1a_", "l1f_", "l2a_", "l2f_", "l3a_", "l3f_"]


def fused_stages():
    return [("p_", build_prep, ins_prep)] + \
        [(pref, b, i) for pref, (nm, b, i) in zip(_BLOCK_PREFS,
                                                  BLOCK_LAUNCHES)] + \
        [("f_", build_final, ins_final)]


def build_fused():
    FUSE.clear()
    FUSE.update({"active": True, "pref": ""})
    Prog._shared = None
    p = Prog()
    nc = p.nc
    ident = nc.dram_tensor("ident_f32", [128, 128], F32, kind="ExternalInput")
    ones1 = nc.dram_tensor("ones1_f32", [1, 128], F32, kind="ExternalInput")
    constv = nc.dram_tensor("constv", [128, len(CONST_VALS)], F32,
                            kind="ExternalInput")
    xout_ext = nc.dram_tensor("xout", [D, TCORE], F32, kind="ExternalOutput")
    pp = [nc.dram_tensor(f"xstream{i}", [D, TCORE], F32) for i in range(2)]
    mu_scr = nc.dram_tensor("mu0scr", [1, TCORE], F32)
    shared_io = {"ident_f32": ident, "ones1_f32": ones1, "constv": constv}
    FUSE["io"] = dict(shared_io)
    with p.phase() as blk:
        load_consts(p, blk, FUSE["io"], want_f16=True)
    ios = [{"xout": pp[0], "mu0": mu_scr}]
    for i in range(8):
        ios.append({"xin": pp[i % 2], "xout": pp[(i + 1) % 2]})
    ios.append({"xin": pp[0], "mu0in": mu_scr, "xout": xout_ext})
    for (pref, bfn, _), iod in zip(fused_stages(), ios):
        FUSE["pref"] = pref
        FUSE["io"] = dict(shared_io)
        FUSE["io"].update(iod)
        with p.scope():
            bfn()
    p.stack.__exit__(None, None, None)
    Prog._shared = None
    FUSE.clear()
    FUSE["active"] = False
    return nc


_FUSED_SHARED = {"ident_f32", "ones1_f32", "constv"}
_FUSED_DROP = {"xin", "mu0in"}


def ins_fused(cx, H, inputs):
    dummy = {"stream": [None] * NCORES, "mu0": [None] * NCORES}
    m = {}
    for pref, bfn, ifn in fused_stages():
        d = ifn(cx, H, inputs, dummy)
        for k, v in d.items():
            if k in _FUSED_DROP:
                continue
            if k in _FUSED_SHARED:
                m[k] = v
            else:
                m[pref + k] = v
    return m


def run_launches(launches, H, inputs):
    state = {"stream": [None] * NCORES, "mu0": [None] * NCORES}
    built = {}
    for name, build_fn, ins_fn in launches:
        if name not in built:
            built[name] = build_fn()
        nc = built[name]
        in_maps = [ins_fn(c, H, inputs, state) for c in range(NCORES)]
        res = run_bass_kernel_spmd(nc, in_maps, core_ids=list(range(NCORES)))
        for c in range(NCORES):
            for k, v in res.results[c].items():
                if k == "xout":
                    state["stream"][c] = v
                elif k == "mu0":
                    state["mu0"][c] = v
                else:
                    state.setdefault(k, [None] * NCORES)[c] = v
    return state


def assemble(state):
    out = np.zeros((B, S, D), np.float32)
    for c in range(NCORES):
        b = c // 4
        for si, cg in enumerate(chunks_of_core(c)):
            out[b, cg * CHT:(cg + 1) * CHT, :] = \
                state["stream"][c][:, si * CHT:(si + 1) * CHT].T
    return out


def kernel(**inputs):
    H = host_prep(inputs)
    for f in HOST_PREPS:
        f(H, inputs)
    nc = build_fused()
    in_maps = [ins_fused(c, H, inputs) for c in range(NCORES)]
    res = run_bass_kernel_spmd(nc, in_maps, core_ids=list(range(NCORES)))
    state = {"stream": [res.results[c]["xout"] for c in range(NCORES)]}
    return assemble(state)


def host_prep(inputs):
    H = {}
    H["ident_f32"] = np.eye(128, dtype=np.float32)
    H["ones1_f32"] = np.ones((1, 128), dtype=np.float32)
    H["constv"] = np.broadcast_to(np.array(CONST_VALS, np.float32),
                                  (128, len(CONST_VALS))).copy()

    x = np.asarray(inputs["L0_attn_x"])
    H["x_fm"] = []
    for c in range(NCORES):
        b = c // 4
        sl = [x[b, cg * CHT:(cg + 1) * CHT, :] for cg in chunks_of_core(c)]
        xt = np.concatenate(sl, axis=0)
        H["x_fm"].append(np.ascontiguousarray(xt.T))
    return H
